# version 10
# baseline (speedup 1.0000x reference)
import math
import os as _os
import contextlib
L1F_PART = _os.environ.get('L1F_PART', 'full')
L1F_PREP = _os.environ.get('L1F_PREP', 'all')
L1F_STAGES = ['qkv', 'sq', 'ev', 'ssp', 'sqrt', 'max', 'rcp', 'cpy', 'f2', 'f', 'all']
def _prep_lt(s):
    return L1F_STAGES.index(L1F_PREP) < L1F_STAGES.index(s)
import numpy as np
import concourse.bass as bass
import concourse.mybir as mybir
from concourse.bass_utils import run_bass_kernel_spmd
F32 = mybir.dt.float32
BF16 = mybir.dt.bfloat16
AF = mybir.ActivationFunctionType
ALU = mybir.AluOpType
AX = mybir.AxisListType
T = 2048
D = 1024
NT = 16
NC = 8
WZ = 2049
NEG = -3e+38
DI = 2730
DIP = 2816
NK2 = 22
EPS0 = 1e-05
_UC = [0]

def _U(name):
    _UC[0] += 1
    return f'{name}_{_UC[0]}'

class Stream:

    def __init__(self, key):
        self.key = key
        self.ops = []
        self.count = 0
        self.dcount = 0

    def tok(self):
        last = self.ops[-1]
        if last[2] is None:
            if last[3] and self.key != 'y':
                self.dcount += 16
                last[2] = ('d', self.dcount)
            else:
                self.count += 16 if last[3] else 1
                last[2] = ('c', self.count)
        kd, val = last[2]
        return (self.key + 'd' if kd == 'd' else self.key, val)

class Prog:

    def __init__(self):
        self.streams = {k: Stream(k) for k in ('v', 's', 't', 'g', 'y')}
        self.pending = {}

    def do(self, key, fn, waits=(), dma=False):
        st = self.streams[key]
        wd = dict(self.pending.pop(key, ()))
        for w in waits:
            if w is None:
                continue
            k, val = w
            if k == key:
                continue
            if wd.get(k, 0) < val:
                wd[k] = val
        st.ops.append([fn, wd, None, dma])
        if dma:
            st.tok()
        return st

    def tok(self, key):
        return self.streams[key].tok()

    def barrier(self, keys=('v', 's', 't', 'g', 'y')):
        toks = {}
        for k in keys:
            st = self.streams[k]
            if st.ops:
                toks[k] = st.tok()
        for k in keys:
            pend = self.pending.setdefault(k, {})
            for kk, t in toks.items():
                if kk == k:
                    continue
                rk, val = t
                if pend.get(rk, 0) < val:
                    pend[rk] = val
        return toks

    def emit(self, nc, block, sems):

        def make(stream):

            def run(eng):
                hw = {}
                for fn, waits, inc, is_dma in stream.ops:
                    for pk, val in waits.items():
                        if hw.get(pk, 0) < val:
                            eng.wait_ge(sems[pk], val)
                            hw[pk] = val
                    ins = fn(eng)
                    if inc is not None:
                        sk = stream.key + 'd' if inc[0] == 'd' else stream.key
                        ins.then_inc(sems[sk], 16 if is_dma else 1)
            return run
        block.vector(make(self.streams['v']))
        block.scalar(make(self.streams['s']))
        block.tensor(make(self.streams['t']))
        block.gpsimd(make(self.streams['g']))
        block.sync(make(self.streams['y']))

def alibi_slopes(n=16):
    start = 2.0 ** (-2.0 ** (-(math.log2(n) - 3)))
    return [start * start ** i for i in range(n)]

def host_constants():
    import ml_dtypes
    bf = lambda a: np.ascontiguousarray(a).astype(ml_dtypes.bfloat16)
    c = {}
    c['ident'] = bf(np.eye(128))
    c['identf'] = np.eye(128, dtype=np.float32)
    sw = np.zeros((128, 128), np.float32)
    for b in range(2):
        for i in range(32):
            sw[b * 64 + i, b * 64 + 32 + i] = 1.0
            sw[b * 64 + 32 + i, b * 64 + i] = 1.0
    c['swapm'] = bf(sw)
    k_ = np.arange(128)[:, None]
    r_ = np.arange(128)[None, :]
    c['tri_lo'] = bf((k_ < r_).astype(np.float32))
    c['tri_hi'] = bf((k_ > r_).astype(np.float32))
    c['negi'] = bf(np.eye(128) * NEG)
    bd = np.zeros((128, 32), np.float32)
    bd[0:64, 0] = 1.0
    bd[64:128, 1] = 1.0
    c['bdones'] = bf(bd)
    f2 = np.zeros((2, 128), np.float32)
    f2[0, 0:64] = 1.0
    f2[1, 64:128] = 1.0
    c['f2sel'] = bf(f2)
    c['onesrow'] = bf(np.ones((1, 128)))
    c['onescol'] = bf(np.ones((128, 1)))
    c['onescolf'] = np.ones((128, 1), np.float32)
    DH = 64
    half = np.arange(0, DH, 2, dtype=np.float64)
    inv_freq = 1.0 / 10000.0 ** (half / DH)
    t_ = np.arange(T, dtype=np.float64)
    freqs = np.concatenate([t_[:, None] * inv_freq[None, :]] * 2, -1)
    base_scale = (half + 0.4 * DH) / (1.4 * DH)
    power = (t_ - T // 2) / 128.0
    scale = np.concatenate([base_scale[None, :] ** power[:, None]] * 2, -1)
    sgn = np.where(np.arange(DH) < 32, -1.0, 1.0)[:, None]
    c['rotc'] = bf(np.vstack([(np.cos(freqs) * scale).T] * 2))
    c['rots'] = bf(np.vstack([(np.sin(freqs) * scale).T * sgn] * 2))
    c['rotci'] = bf(np.vstack([(np.cos(freqs) / scale).T] * 2))
    c['rotsi'] = bf(np.vstack([(np.sin(freqs) / scale).T * sgn] * 2))
    return c

def build(upto='all', taps=(), race_check=True, n_sub=6):
    nc = bass.Bass('TRN2', target_bir_lowering=False, detect_race_conditions=race_check)
    P = Prog()
    consts = host_constants()
    er = contextlib.ExitStack()

    def din(name, shape, dt=F32):
        return nc.dram_tensor(name, list(shape), dt, kind='ExternalInput').ap()
    X_in = din('x', (T, D))
    OUT = nc.dram_tensor('out', [T, D], F32, kind='ExternalOutput').ap()
    WATT = din('w_attnT', (D, 3 * D))
    WPRJ = din('w_projT', (D, D))
    WFC = din('w_fcT', (D, 4096))
    WPR2 = din('w_prT', (4096, D))
    WQKV1 = din('wqkv1T', (D, 3 * D))
    WO1 = din('wo1T', (D, D))
    BQKV1 = din('bqkv1', (3, 128, 8))
    BO1 = din('bo1', (128, 8))
    NQW = din('nqw', (128, 8))
    NKW = din('nkw', (128, 8))
    WQKV6 = din('wqkv6', (6, D, 3 * D))
    WO6 = din('wo6', (6, D, D))
    QS6 = din('qs6', (6, 128, 1))
    KS6 = din('ks6', (6, 128, 1))
    LNG6 = din('lng6', (6, 128, 8))
    LNB6 = din('lnb6', (6, 128, 8))
    W1A6 = din('w1a6', (6, D, DIP))
    W1G6 = din('w1g6', (6, D, DIP))
    B1A6 = din('b1a6', (6, 128, NK2))
    B1G6 = din('b1g6', (6, 128, NK2))
    W26 = din('w26', (6, DIP, D))
    B26 = din('b26', (6, 128, 8))
    CON = {k: din('c_' + k, v.shape, F32 if v.dtype == np.float32 else BF16) for k, v in consts.items() if k not in ('identf', 'onescolf')}
    XSP = nc.dram_tensor('xspill', [128, NC * T], F32, kind='Internal').ap()
    WFCBF = nc.dram_tensor('wfcbf', [D, 4096], BF16, kind='Internal').ap()
    WPRBF = nc.dram_tensor('wprbf', [4096, D], BF16, kind='Internal').ap()
    WQ6BF = nc.dram_tensor('wq6bf', [6, D, 3 * D], BF16, kind='Internal').ap()
    WO6BF = nc.dram_tensor('wo6bf', [6, D, D], BF16, kind='Internal').ap()
    W1A6BF = nc.dram_tensor('w1a6bf', [6, D, DIP], BF16, kind='Internal').ap()
    W1G6BF = nc.dram_tensor('w1g6bf', [6, D, DIP], BF16, kind='Internal').ap()
    W26BF = nc.dram_tensor('w26bf', [6, DIP, D], BF16, kind='Internal').ap()
    tap_toks = []

    def tap(name, ap, waits):
        if name not in taps:
            return
        tp = nc.dram_tensor('tap_' + name, list(ap.shape), ap.dtype, kind='ExternalOutput').ap()
        P.do('y', lambda e, o=tp, i=ap: e.dma_start(out=o, in_=i), waits, dma=True)
        tap_toks.append(P.tok('y'))
    sbp = lambda n, sh, dt: er.enter_context(nc.sbuf_tensor(n, list(sh), dt))
    xn = sbp('xn', (128, NC * WZ), BF16)
    gb = sbp('gb', (128, NC * T), BF16)
    CB = {k: sbp('cb_' + k, v.shape, F32 if v.dtype == np.float32 else BF16) for k, v in consts.items() if not k.startswith('rot') and k not in ('identf', 'onescolf')}
    ST = [sbp(f'st{i}', (128, 16), F32) for i in range(8)]
    sqw = sbp('sqw', (128, 1024), BF16)
    cpool = sbp('cpool', (128, 8), F32)
    CB['cpool'] = cpool
    for _ci, _cv in ((0, 1e-05), (1, 1e-06), (2, 0.2), (3, -1.0), (4, 1.0 / 1024)):
        P.do('g', lambda e, i=_ci, v=_cv, cpool=cpool: e.memset(cpool[:, i:i + 1], v))
    tok_cpool = P.tok('g')
    ctoks = []
    for k in CON:
        if k.startswith('rot'):
            continue
        P.do('y', lambda e, d=CB[k], s=CON[k]: e.dma_start(out=d[:], in_=s), dma=True)
        ctoks.append(P.tok('y'))
    tok_con = ctoks[-1]
    ident = CB['ident'][:]

    def zc(base, c, t0, t1, shift=False):
        off = 0 if shift else 1
        return base[:, c * WZ + off + t0:c * WZ + off + t1]

    class WBuf:
        def __init__(self, es, name, cols, n_stg=2, n_bf=2, engs=('g', 'v')):
            self.stg = [es.enter_context(nc.sbuf_tensor(_U(name + 'S'), [128, cols], F32))
                        for _ in range(n_stg)]
            self.bf = [es.enter_context(nc.sbuf_tensor(_U(name + 'B'), [128, cols], BF16))
                       for _ in range(n_bf)]
            self.stg_free = [None] * n_stg
            self.cast_tok = [None] * n_bf
            self.busy = [[] for _ in range(n_bf)]
            self.i = 0
            self.j = 0
            self.engs = engs
            self.cols = cols

        def load(self, src2d, extra=(), w=None):
            si = self.j % len(self.stg)
            i = self.i
            self.i = (self.i + 1) % len(self.bf)
            self.j += 1
            w = w if w is not None else self.cols
            tw = dma_ch(self.stg[si][:, 0:w], src2d,
                        [self.stg_free[si]] + list(extra))
            eng = self.engs[i % len(self.engs)]
            P.do(eng, lambda e, d=self.bf[i], s=self.stg[si], w=w:
                 e.tensor_copy(d[:, 0:w], s[:, 0:w]), [tw] + self.busy[i])
            self.cast_tok[i] = P.tok(eng)
            self.stg_free[si] = self.cast_tok[i]
            self.busy[i] = []
            self._last = i
            return self.bf[i], self.cast_tok[i]

        def load_bf(self, src2d_bf, extra=(), w=None):
            i = self.i
            self.i = (self.i + 1) % len(self.bf)
            w = w if w is not None else self.cols
            cc = src2d_bf.shape[0] // 128
            srcv = src2d_bf.rearrange('(c p) n -> p c n', p=128)
            dv = self.bf[i][:, 0:w].rearrange('p (c n) -> p c n', c=cc)
            P.do('y', lambda e, d=dv, s=srcv: e.dma_start(out=d, in_=s),
                 self.busy[i] + list(extra), dma=True)
            self.cast_tok[i] = P.tok('y')
            self.busy[i] = []
            self._last = i
            return self.bf[i], self.cast_tok[i]

        def writeback(self, dst2d_bf, after, w=None):
            i = self._last
            w = w if w is not None else self.cols
            cc = dst2d_bf.shape[0] // 128
            dstv = dst2d_bf.rearrange('(c p) n -> p c n', p=128)
            sv = self.bf[i][:, 0:w].rearrange('p (c n) -> p c n', c=cc)
            P.do('y', lambda e, d=dstv, s=sv: e.dma_start(out=d, in_=s),
                 list(after), dma=True)
            tok = P.tok('y')
            self.busy[i].append(tok)
            return tok

        def consumed(self, tok):
            self.busy[self._last].append(tok)

    def dma_ch(dst, src2d, waits=()):
        cc = src2d.shape[0] // 128
        src = src2d.rearrange('(c p) n -> p c n', p=128)
        dv = dst.rearrange('p (c n) -> p c n', c=cc)
        P.do('y', lambda e, d=dv, s=src: e.dma_start(out=d, in_=s), waits, dma=True)
        return P.tok('y')

    def _ln_combine(ntiles, eps, tv, ts_):
        su, ssq, meanb, varb, rstd, m2s, vr2, mr = [s[:] for s in ST]
        sl = slice(0, ntiles)
        P.do('v', lambda e, varb=varb, sl=sl, ssq=ssq: e.tensor_scalar(varb[:, sl], ssq[:, sl], 1.0 / 1024, None, ALU.mult), [ts_])
        P.do('s', lambda e, meanb=meanb, sl=sl, su=su, cpool=cpool: e.activation(meanb[:, sl], su[:, sl], AF.Identity, scale=cpool[:, 4:5]), [tv, tok_cpool])
        c2 = P.tok('s')
        P.do('v', lambda e, m2s=m2s, sl=sl, meanb=meanb: e.tensor_mul(m2s[:, sl], meanb[:, sl], meanb[:, sl]), [c2])
        c3 = P.tok('v')
        P.do('g', lambda e, vr2=vr2, sl=sl, varb=varb, m2s=m2s: e.tensor_sub(vr2[:, sl], varb[:, sl], m2s[:, sl]), [c3])
        c4 = P.tok('g')
        epcol = 0 if abs(eps - 1e-05) < 1e-12 else 1
        P.do('s', lambda e, ec=epcol, vr2=vr2, sl=sl, cpool=cpool: e.activation(vr2[:, sl], vr2[:, sl], AF.Sqrt, bias=cpool[:, ec:ec + 1]), [c4, tok_cpool])
        c5 = P.tok('s')
        P.do('v', lambda e, rstd=rstd, sl=sl, vr2=vr2: e.reciprocal(rstd[:, sl], vr2[:, sl]), [c5])
        c6 = P.tok('v')
        P.do('g', lambda e, mr=mr, sl=sl, meanb=meanb, rstd=rstd: e.tensor_tensor(mr[:, sl], meanb[:, sl], rstd[:, sl], ALU.mult), [c6, c2])
        c7 = P.tok('g')
        P.do('v', lambda e, m2s=m2s, sl=sl, mr=mr: e.tensor_scalar(m2s[:, sl], mr[:, sl], -1.0, None, ALU.mult), [c7])
        c8 = P.tok('v')
        return (c8, c6)

    def emit_ln_tm(src_tile, ntiles, dst_fn, tag, eps, waits=()):
        su, ssq, meanb, varb, rstd, m2s, vr2, mr = [s[:] for s in ST]
        out_toks = []
        with contextlib.ExitStack() as es:
            tmpb = [es.enter_context(nc.sbuf_tensor(_U(f'{tag}tm{i}'), [128, 1024], BF16)) for i in range(2)]
            bkp = [es.enter_context(nc.psum_tensor(_U(f'{tag}bk{i}'), [128, 1024], BF16)) for i in range(2)]
            tv = ts_ = None
            for a in range(ntiles):
                P.do('v', lambda e, aa=a, su=su, src_tile=src_tile: e.reduce_sum(su[:, aa:aa + 1], src_tile(aa), axis=AX.X), list(waits))
                tv = P.tok('v')
                P.do('s', lambda e, aa=a, sqw=sqw, src_tile=src_tile, ssq=ssq: e.activation(sqw[:], src_tile(aa), AF.Square, accum_out=ssq[:, aa:aa + 1]), list(waits))
                ts_ = P.tok('s')
            c8, c6 = _ln_combine(ntiles, eps, tv, ts_)
            bz = {}
            for a in range(ntiles):
                b = a % 2
                P.do('s', lambda e, aa=a, bb=b, tmpb=tmpb, src_tile=src_tile, m2s=m2s, rstd=rstd: e.activation(tmpb[bb][:], src_tile(aa), AF.Identity, bias=m2s[:, aa:aa + 1], scale=rstd[:, aa:aa + 1]), [c8, c6, bz.get((b, 'e'))])
                ta = P.tok('s')
                for c in range(NC):
                    P.do('t', lambda e, cc=c, bb=b, bkp=bkp, tmpb=tmpb, ident=ident: e.transpose(bkp[bb][:, cc * 128:(cc + 1) * 128], tmpb[bb][:, cc * 128:(cc + 1) * 128], ident), [ta, tok_con, bz.get((b, 'e'))] if c == 0 else [])
                tb = P.tok('t')
                P.do('v', lambda e, d=dst_fn(a), bb=b, bkp=bkp: e.tensor_copy(d, bkp[bb][:].rearrange('p (c w) -> p c w', c=NC)), [tb])
                te = P.tok('v')
                bz[b, 'e'] = te
                out_toks.append(te)
        return out_toks

    def emit_ln_fm(src_col, ntiles, dst_fn, tag, eps, waits=()):
        su, ssq, meanb, varb, rstd, m2s, vr2, mr = [s[:] for s in ST]
        out_toks = []
        with contextlib.ExitStack() as es:
            cst = [es.enter_context(nc.sbuf_tensor(_U(f'{tag}cs{i}'), [128, 1024], BF16)) for i in range(2)]
            trp = [es.enter_context(nc.psum_tensor(_U(f'{tag}tr{i}'), [128, 1024], BF16)) for i in range(2)]
            tmpb = [es.enter_context(nc.sbuf_tensor(_U(f'{tag}tm{i}'), [128, 1024], BF16)) for i in range(2)]
            bkp = [es.enter_context(nc.psum_tensor(_U(f'{tag}bk{i}'), [128, 1024], BF16)) for i in range(2)]
            bz = {}
            tv = ts_ = None

            def cast_and_transpose(a, b, w0):
                for c in range(NC):
                    P.do('v', lambda e, aa=a, cc=c, bb=b, cst=cst, src_col=src_col: e.tensor_copy(cst[bb][:, cc * 128:(cc + 1) * 128], src_col(aa, cc)), w0 if c == 0 else [])
                tcc = P.tok('v')
                for c in range(NC):
                    P.do('t', lambda e, cc=c, bb=b, trp=trp, cst=cst, ident=ident: e.transpose(trp[bb][:, cc * 128:(cc + 1) * 128], cst[bb][:, cc * 128:(cc + 1) * 128], ident), [tcc, tok_con] + w0 if c == 0 else [])
                return P.tok('t')
            for a in range(ntiles):
                b = a % 2
                w0 = list(waits) + [bz.get((b, 'v')), bz.get((b, 's'))]
                tt = cast_and_transpose(a, b, w0)
                P.do('v', lambda e, aa=a, bb=b, su=su, trp=trp: e.reduce_sum(su[:, aa:aa + 1], trp[bb][:], axis=AX.X), [tt])
                tv = P.tok('v')
                P.do('s', lambda e, bb=b, aa=a, sqw=sqw, trp=trp, ssq=ssq: e.activation(sqw[:], trp[bb][:], AF.Square, accum_out=ssq[:, aa:aa + 1]), [tt, tv])
                ts_ = P.tok('s')
                bz[b, 'v'] = tv
                bz[b, 's'] = ts_
            c8, c6 = _ln_combine(ntiles, eps, tv, ts_)
            bz2 = {}
            for a in range(ntiles):
                b = a % 2
                w2 = [c8, c6, bz.get((b, 'v')), bz.get((b, 's')), bz2.get((b, 'a'))]
                tt = cast_and_transpose(a, b, w2)
                P.do('s', lambda e, aa=a, bb=b, tmpb=tmpb, trp=trp, m2s=m2s, rstd=rstd: e.activation(tmpb[bb][:], trp[bb][:], AF.Identity, bias=m2s[:, aa:aa + 1], scale=rstd[:, aa:aa + 1]), [tt])
                ta = P.tok('s')
                bz[b, 'v'] = None
                bz[b, 's'] = ta
                for c in range(NC):
                    P.do('t', lambda e, cc=c, bb=b, bkp=bkp, tmpb=tmpb, ident=ident: e.transpose(bkp[bb][:, cc * 128:(cc + 1) * 128], tmpb[bb][:, cc * 128:(cc + 1) * 128], ident), [ta, bz2.get((b, 'e'))] if c == 0 else [])
                tb = P.tok('t')
                P.do('v', lambda e, d=dst_fn(a), bb=b, bkp=bkp: e.tensor_copy(d, bkp[bb][:].rearrange('p (c w) -> p c w', c=NC)), [tb])
                te = P.tok('v')
                bz2[b, 'a'] = ta
                bz2[b, 'e'] = te
                out_toks.append(te)
        return out_toks

    def attn_head(tag, hs, q_ap, k_ap, vtm_ap, p_sb, pt_sb, nmx, sums, rcs, ao_write, qt_list, q0_of, keyspec, gates, alibi_ap=None, alibi_base=None, mask_fn=None):
        he = hs + 64
        tp = (0, 0) if hs == 0 else (64, 0)
        max_klen = max(keyspec(q)[1] for q in qt_list)
        CW = 512 if max_klen <= 512 else 1024
        with contextlib.ExitStack() as eA:
            sps = [eA.enter_context(nc.psum_tensor(_U(f'{tag}sp{i}'), [128, CW], F32)) for i in range(2)]
            ptp = [eA.enter_context(nc.psum_tensor(_U(f'{tag}pp{i}'), [128, CW], BF16)) for i in range(2)]
            avp = [eA.enter_context(nc.psum_tensor(_U(f'{tag}av{i}'), [128, 128], F32)) for i in range(2)]
            nm2 = eA.enter_context(nc.sbuf_tensor(_U(tag + 'nm'), [128, 8], F32))
            sm2 = eA.enter_context(nc.sbuf_tensor(_U(tag + 'sm'), [128, 8], F32))
            sps_busy = [gates.get('prev_exp')] * 2
            ptp_busy = [gates.get('prev_ptev')] * 2
            avp_busy = [gates.get('prev_ao')] * 2
            nm_busy = [None] * 2
            tmp_busy = [None] * 2
            sm_busy = [None] * 2
            tpt_last = gates.get('prev_tpt')
            av_last = gates.get('prev_tav')
            first = True
            pend = None

            def stage_b2(stq):
                nonlocal tpt_last, av_last
                qt, kb, klen, chs = stq['qt'], stq['kb'], stq['klen'], stq['chs']
                qp = qt % 2
                ptevs = []
                for ci, ch in enumerate(chs):
                    c0, clen, _, _ = ch
                    pp = (qt + ci) % 2
                    for kt in range(clen // 128):
                        P.do('t', lambda e, kl=kt * 128, cc0=c0, pp=pp, ptp=ptp, p_sb=p_sb, ident=ident: e.transpose(ptp[pp][:, kl:kl + 128], p_sb[:, cc0 + kl:cc0 + kl + 128], ident), [stq['norm'][ci], ptp_busy[pp], tok_con] if kt == 0 else [])
                    tpt_last = P.tok('t')
                    P.do('v', lambda e, cc0=c0, cl=clen, pp=pp, pt_sb=pt_sb, ptp=ptp: e.tensor_copy(pt_sb[:, cc0:cc0 + cl], ptp[pp][:, 0:cl]), [tpt_last, av_last])
                    ptp_busy[pp] = P.tok('v')
                    ptevs.append(ptp_busy[pp])
                gates['prev_ptev'] = ptevs[-1]
                nkt = klen // 128
                for ci, ch in enumerate(chs):
                    c0, clen, _, _ = ch
                    for kl in range(clen // 128):
                        kt = c0 // 128 + kl
                        w_ = ([ptevs[ci], avp_busy[qp]] + list(gates.get('vtm', []))) if kl == 0 else []
                        P.do('t', lambda e, k0=kt * 128, kb0=kb, first_=kt == 0, last=kt == nkt - 1, qp=qp, avp=avp, hs=hs, he=he, vtm_ap=vtm_ap, pt_sb=pt_sb: e.matmul(avp[qp][hs:he, :], vtm_ap[:, kb0 + k0 + hs:kb0 + k0 + hs + 64], pt_sb[:, k0:k0 + 128], start=first_, stop=last, tile_position=(0, hs)), w_)
                av_last = P.tok('t')
                gates['prev_tav'] = av_last
                avp_busy[qp] = ao_write(qt, avp[qp], [av_last])
                gates['prev_ao'] = avp_busy[qp]
            for qt in qt_list:
                kb, klen = keyspec(qt)
                q0 = q0_of(qt)
                nch = (klen + CW - 1) // CW
                p2 = 2 * (qt % 2)
                chs = []
                for c in range(nch):
                    c0 = c * CW
                    clen = min(CW, klen - c0)
                    par = (qt + c) % 2
                    sp_ = sps[par]
                    w0 = [sps_busy[par]]
                    if first:
                        w0 += list(gates.get('qkv', []))
                    for ks in range((clen + 511) // 512):
                        n0 = ks * 512
                        n1 = min(clen, n0 + 512)
                        P.do('t', lambda e, a0=n0, a1=n1, g0=c0, qq=q0, kb0=kb, sp=mask_fn is None, sp_=sp_, q_ap=q_ap, hs=hs, he=he, k_ap=k_ap, tp=tp: e.matmul(sp_[:, a0:a1], q_ap[hs:he, qq:qq + 128], k_ap[hs:he, kb0 + g0 + a0:kb0 + g0 + a1], start=True, stop=sp, tile_position=tp), w0 if ks == 0 else [])
                    if mask_fn is not None:
                        mask_fn(qt, sp_)
                    tsc = P.tok('t')
                    first = False
                    mcol = nm2[:, p2 + c:p2 + c + 1]
                    wmn = [tsc] + list(nm_busy[qt % 2] or [])
                    if alibi_ap is not None:
                        b0 = alibi_base(qt)
                        P.do('v', lambda e, cl=clen, bb=b0 + c0, sp_=sp_, alibi_ap=alibi_ap: e.scalar_tensor_tensor(sp_[:, 0:cl], sp_[:, 0:cl], 1.0, alibi_ap[:, bb:bb + cl], ALU.mult, ALU.add), wmn + list(gates.get('alibi', [])))
                        P.do('v', lambda e, cl=clen, mc=mcol, sp_=sp_: e.tensor_scalar(sp_[:, 0:cl], sp_[:, 0:cl], 1.0, -3e+38, ALU.mult, ALU.max, accum_out=mc))
                    else:
                        P.do('v', lambda e, cl=clen, mc=mcol, sp_=sp_: e.tensor_scalar(sp_[:, 0:cl], sp_[:, 0:cl], 1.0, -3e+38, ALU.mult, ALU.max, accum_out=mc), wmn)
                    chs.append((c0, clen, par, P.tok('v')))
                if nch > 1:
                    tg2 = []
                    for c in range(2):
                        P.do('g', lambda e, pp2=p2, cc=c, nm2=nm2: e.tensor_scalar(nm2[:, 4 + pp2 + cc:5 + pp2 + cc], nm2[:, pp2 + cc:pp2 + cc + 1], -1.0, None, ALU.mult), [chs[c][3], tmp_busy[qt % 2]])
                        tg2.append(P.tok('g'))
                    P.do('v', lambda e, col=qt, pp2=p2, nmx=nmx, nm2=nm2: e.tensor_tensor(nmx[:, col:col + 1], nm2[:, 4 + pp2:5 + pp2], nm2[:, 5 + pp2:6 + pp2], ALU.min), tg2)
                    nm_busy[qt % 2] = tg2
                    tmp_busy[qt % 2] = P.tok('v')
                    tneg = tmp_busy[qt % 2]
                else:
                    P.do('g', lambda e, col=qt, pp2=p2, nmx=nmx, nm2=nm2: e.tensor_scalar(nmx[:, col:col + 1], nm2[:, pp2:pp2 + 1], -1.0, None, ALU.mult), [chs[0][3]])
                    nm_busy[qt % 2] = [P.tok('g')]
                    tneg = nm_busy[qt % 2][0]
                if pend is not None:
                    stage_b2(pend)
                exps = []
                norm = []
                for ci, ch in enumerate(chs):
                    c0, clen, par, mtok = ch
                    scol = sums[:, qt:qt + 1] if nch == 1 else sm2[:, p2 + ci:p2 + ci + 1]
                    we = [mtok, tneg, tpt_last, sm_busy[qt % 2] if nch > 1 else None]
                    P.do('s', lambda e, cc0=c0, cl=clen, col=qt, sc=scol, pp=par, p_sb=p_sb, sps=sps, nmx=nmx: e.activation(p_sb[:, cc0:cc0 + cl], sps[pp][:, 0:cl], AF.Exp, bias=nmx[:, col:col + 1], accum_out=sc), we)
                    sps_busy[par] = P.tok('s')
                    exps.append(sps_busy[par])
                gates['prev_exp'] = exps[-1]
                if nch > 1:
                    P.do('g', lambda e, col=qt, pp2=p2, sums=sums, sm2=sm2: e.tensor_tensor(sums[:, col:col + 1], sm2[:, pp2:pp2 + 1], sm2[:, pp2 + 1:pp2 + 2], ALU.add), exps)
                    sm_busy[qt % 2] = P.tok('g')
                    trc0 = sm_busy[qt % 2]
                else:
                    trc0 = exps[-1]
                P.do('v', lambda e, col=qt, rcs=rcs, sums=sums: e.reciprocal(rcs[:, col:col + 1], sums[:, col:col + 1]), [trc0])
                trc = P.tok('v')
                for ci, ch in enumerate(chs):
                    c0, clen, par, _ = ch
                    P.do('s', lambda e, cc0=c0, cl=clen, col=qt, p_sb=p_sb, rcs=rcs: e.activation(p_sb[:, cc0:cc0 + cl], p_sb[:, cc0:cc0 + cl], AF.Identity, scale=rcs[:, col:col + 1]), [trc])
                    norm.append(P.tok('s'))
                pend = dict(qt=qt, kb=kb, klen=klen, chs=chs, norm=norm)
            stage_b2(pend)
            gates['prev_tpt'] = tpt_last
    fin = []
    with contextlib.ExitStack() as EA:
        x_sb = EA.enter_context(nc.sbuf_tensor(_U('x0'), [128, NT * 1024], F32))
        P.do('y', lambda e, x_sb=x_sb, X_in=X_in: e.dma_start(out=x_sb[:].rearrange('p (a d) -> p a d', a=NT), in_=X_in.rearrange('(a p) d -> p a d', p=128)), dma=True)
        tok_x = P.tok('y')
        P.do('g', lambda e, xn=xn: e.memset(xn[:].rearrange('p (c w) -> p c w', c=NC)[:, :, 0:1], 0.0))
        tok_xnz = P.tok('g')
        xtile = lambda a: x_sb[:, a * 1024:(a + 1) * 1024]
        xn_dst = lambda a: xn[:].rearrange('p (c w) -> p c w', c=NC)[:, :, 1 + a * 128:1 + (a + 1) * 128]
        ln0 = emit_ln_tm(xtile, NT, xn_dst, 'l0', EPS0, waits=[tok_x])
        xn_gate = ln0 + [tok_xnz]
        tap('xn0', xn[:], xn_gate)
        P.barrier()
        tok_g0 = None
        if upto not in ('ln',):
            with contextlib.ExitStack() as ef:
                wb = WBuf(ef, 'fw', 4096, n_stg=1, n_bf=2)
                h_bf = ef.enter_context(nc.sbuf_tensor(_U('fh'), [128, 32 * 512], BF16))
                lanes = []
                for ln_ in range(2):
                    lanes.append(dict(
                        yt=ef.enter_context(nc.sbuf_tensor(_U(f'fy{ln_}'), [128, 256], F32)),
                        sp=ef.enter_context(nc.sbuf_tensor(_U(f'fs{ln_}'), [128, 256], F32)),
                        m0=ef.enter_context(nc.sbuf_tensor(_U(f'fm{ln_}a'), [128, 256], F32)),
                        fa=ef.enter_context(nc.sbuf_tensor(_U(f'fa{ln_}'), [128, 256], F32)),
                        free=None))
                t_wfree = xn_gate[-1]
                t_bffree = None
                t_ycopy = None
                t_spk = None
                for q in range(4):
                    tq0 = q * 512
                    t_hev = {}
                    t_hfree = t_ycopy if t_ycopy is not None else t_bffree
                    eh = contextlib.ExitStack()
                    hps = [eh.enter_context(nc.psum_tensor(_U(f'fhp{q}_{i}'), [128, 512], F32)) for i in range(2)]
                    for fs in range(8):
                        if q == 0:
                            wbf, tc = wb.load(WFC[:, fs * 512:(fs + 1) * 512],
                                              [t_wfree] if fs == 0 else ())
                            wb.writeback(WFCBF[:, fs * 512:(fs + 1) * 512], [tc])
                        else:
                            wbf, tc = wb.load_bf(WFCBF[:, fs * 512:(fs + 1) * 512])
                        t_wfree = None
                        last_mm = None
                        for m in range(4):
                            fc = fs * 4 + m
                            b = fc % 2
                            w0 = [tc, t_hev.get(b), t_hfree] + xn_gate
                            for k in range(NC):
                                P.do('t', lambda e, kk=k, mm2=m, bb=b, t0=tq0, hps=hps, wbf=wbf, zc=zc, xn=xn: e.matmul(hps[bb][:], wbf[:, kk * 512 + mm2 * 128:kk * 512 + (mm2 + 1) * 128], zc(xn, kk, t0, t0 + 512), start=kk == 0, stop=kk == NC - 1), w0 if k == 0 else [])
                            last_mm = P.tok('t')
                            wb.consumed(last_mm)
                            P.do('s', lambda e, f=fc, bb=b, h_bf=h_bf, hps=hps: e.activation(h_bf[:, f * 512:(f + 1) * 512], hps[bb][:], AF.Relu), [last_mm])
                            tr = P.tok('s')
                            P.do('v', lambda e, f=fc, h_bf=h_bf: e.tensor_mul(h_bf[:, f * 512:(f + 1) * 512], h_bf[:, f * 512:(f + 1) * 512], h_bf[:, f * 512:(f + 1) * 512]), [tr])
                            t_hev[b] = P.tok('v')
                        t_bffree = last_mm
                    th_all = [t_hev.get(0), t_hev.get(1)]
                    eh.close()
                    with contextlib.ExitStack() as ey:
                        yps = [ey.enter_context(nc.psum_tensor(_U(f'fyp{q}_{i}'), [128, 512], F32)) for i in range(8)]
                        t_ymm = {}
                        for fs in range(8):
                            if q == 0:
                                wbf, tc = wb.load(WPR2[fs * 512:(fs + 1) * 512, :])
                                wb.writeback(WPRBF[fs * 512:(fs + 1) * 512, :], [tc])
                            else:
                                wbf, tc = wb.load_bf(WPRBF[fs * 512:(fs + 1) * 512, :])
                            last_mm = None
                            for of in range(8):
                                w0 = [tc] + (th_all if fs == 0 else [])
                                for m in range(4):
                                    fc = fs * 4 + m
                                    P.do('t', lambda e, oo=of, mm2=m, f=fc, yps=yps, wbf=wbf, h_bf=h_bf: e.matmul(yps[oo][:], wbf[:, mm2 * 1024 + oo * 128:mm2 * 1024 + (oo + 1) * 128], h_bf[:, f * 512:(f + 1) * 512], start=f == 0, stop=f == 31), w0 if m == 0 else [])
                                last_mm = P.tok('t')
                                t_ymm[of] = last_mm
                            wb.consumed(last_mm)
                            t_bffree = last_mm
                        for of in range(8):
                            for hf in range(2):
                                ln_ = (of * 2 + hf) % 2
                                eng = ('g', 'v')[ln_]
                                L = lanes[ln_]
                                yt, spk, facc = (L['yt'], L['sp'], L['fa'])
                                P.do('s', lambda e, oo=of, h0=hf * 256, y_t=yt, yps=yps: e.copy(y_t[:], yps[oo][:, h0:h0 + 256]), [t_ymm[of], L['free']])
                                ty = P.tok('s')
                                P.do(eng, lambda e, spk=spk, y_t=yt: e.tensor_scalar(spk[:], y_t[:], 1.0, None, ALU.is_gt), [ty])
                                P.do(eng, lambda e, facc=facc, spk=spk: e.tensor_copy(facc[:], spk[:]))
                                for si in range(4):
                                    a_ = yt if si == 0 else L['m0']
                                    b_ = L['m0']
                                    P.do(eng, lambda e, aa=a_, bb=b_: e.tensor_scalar(bb[:], aa[:], 0.9, None, ALU.mult))
                                    P.do(eng, lambda e, bb=b_, y_t=yt: e.tensor_add(bb[:], bb[:], y_t[:]))
                                    P.do(eng, lambda e, bb=b_, spk=spk: e.tensor_sub(bb[:], bb[:], spk[:]))
                                    P.do(eng, lambda e, bb=b_, spk=spk: e.tensor_scalar(spk[:], bb[:], 1.0, None, ALU.is_gt))
                                    P.do(eng, lambda e, facc=facc, spk=spk: e.tensor_add(facc[:], facc[:], spk[:]))
                                tg = P.tok(eng)
                                P.do('s', lambda e, oo=of, t0=tq0 + hf * 256, gb=gb, facc=facc, cpool=cpool: e.activation(gb[:, oo * T + t0:oo * T + t0 + 256], facc[:], AF.Silu, scale=cpool[:, 2:3]), [tg, tok_cpool])
                                L['free'] = P.tok('s')
                                t_spk = L['free']
                                t_ycopy = ty
                tok_g0 = t_spk
            tap('g0', gb[:], [tok_g0])
        P.barrier()
        if upto not in ('ln', 'ffn0'):
            slopes = alibi_slopes(16)
            with contextlib.ExitStack() as ea:
                ao = ea.enter_context(nc.sbuf_tensor(_U('ao'), [128, 4 * T], BF16))
                nmx = ea.enter_context(nc.sbuf_tensor(_U('anm'), [128, 16], F32))
                sums = ea.enter_context(nc.sbuf_tensor(_U('asm'), [128, 16], F32))
                rcs = ea.enter_context(nc.sbuf_tensor(_U('arc'), [128, 16], F32))
                t_xadd = {}
                tok_g_pg = []
                for pg in range(2):
                    P.barrier()
                    tok_ao_pg = []
                    with contextlib.ExitStack() as ep:
                        relm = ep.enter_context(nc.sbuf_tensor(_U('arel'), [128, 2688], BF16))
                        am = ep.enter_context(nc.sbuf_tensor(_U('aam'), [128, 2688], BF16))
                        wba = WBuf(ep, 'aw', 1024)
                        q_sb = ep.enter_context(nc.sbuf_tensor(_U('aq'), [128, T], BF16))
                        k_sb = ep.enter_context(nc.sbuf_tensor(_U('ak'), [128, T], BF16))
                        v_sb = ep.enter_context(nc.sbuf_tensor(_U('av_'), [128, T], BF16))
                        vtm = ep.enter_context(nc.sbuf_tensor(_U('avt'), [128, T], BF16))
                        pt_sb = ep.enter_context(nc.sbuf_tensor(_U('apt'), [128, T], BF16))
                        P.do('g', lambda e, relm=relm: e.iota(relm[:], [[1, 2688]], base=-2176, channel_multiplier=-1, allow_small_or_imprecise_dtypes=True), [tok_g0] if tok_g0 else [])
                        P.do('g', lambda e, am=am, relm=relm: e.tensor_scalar(am[:], relm[:], 0.0, None, ALU.is_gt))
                        P.do('g', lambda e, am=am: e.tensor_scalar(am[:], am[:], NEG, None, ALU.mult))
                        P.do('g', lambda e, relm=relm, am=am: e.tensor_add(relm[:], relm[:], am[:]))
                        tok_rel = P.tok('g')
                        t_wfree = tok_g0
                        t_qkvfree = None
                        gates = {}
                        for pi in range(4):
                            pr = pg * 4 + pi
                            p_sb = v_sb
                            tjs = []
                            last_wmm = t_qkvfree
                            for j, off in enumerate((pr * 128, D + pr * 128, 2 * D + pr * 128)):
                                wbf, tc = wba.load(WATT[:, off:off + 128],
                                                   [t_wfree] if t_wfree else ())
                                t_wfree = None
                                dst, scl = ((q_sb, 0.125), (k_sb, None), (v_sb, None))[j]
                                with contextlib.ExitStack() as eq:
                                    qps = [eq.enter_context(nc.psum_tensor(_U(f'aqp{pr}_{j}_{i}'), [128, 512], F32)) for i in range(2)]
                                    tev = {}
                                    for sp in range(4):
                                        b = sp % 2
                                        w0 = [tc, tev.get(b)] + xn_gate + ([gates.get('prev_exp'), gates.get('prev_tav'), gates.get('prev_ao'), gates.get('prev_ptev')] if j == 0 else [])
                                        for k in range(NC):
                                            P.do('t', lambda e, kk=k, ss=sp, bb=b, qps=qps, wbf=wbf, zc=zc, xn=xn: e.matmul(qps[bb][:], wbf[:, kk * 128:(kk + 1) * 128], zc(xn, kk, ss * 512, (ss + 1) * 512), start=kk == 0, stop=kk == NC - 1), w0 if k == 0 else [])
                                        tm = P.tok('t')
                                        last_wmm = tm
                                        wba.consumed(tm)
                                        if scl is not None:
                                            P.do('v', lambda e, d=dst, ss=sp, bb=b, s_=scl, qps=qps: e.tensor_scalar(d[:, ss * 512:(ss + 1) * 512], qps[bb][:], s_, None, ALU.mult), [tm])
                                            tev[b] = P.tok('v')
                                        else:
                                            P.do('s', lambda e, d=dst, ss=sp, bb=b, qps=qps: e.copy(d[:, ss * 512:(ss + 1) * 512], qps[bb][:]), [tm])
                                            tev[b] = P.tok('s')
                                    tjs += [tev.get(0), tev.get(1)]
                            t_qkvfree = last_wmm
                            with contextlib.ExitStack() as ev:
                                vtp = ev.enter_context(nc.psum_tensor(_U(f'avp{pr}'), [128, 2048], BF16))
                                for a in range(NT):
                                    P.do('t', lambda e, aa=a, vtp=vtp, v_sb=v_sb, ident=ident: e.transpose(vtp[:, aa * 128:(aa + 1) * 128], v_sb[:, aa * 128:(aa + 1) * 128], ident), tjs + [gates.get('prev_tav'), gates.get('prev_ao'), gates.get('prev_ptev')] if a == 0 else [])
                                tvt = P.tok('t')
                                P.do('v', lambda e, vtm=vtm, vtp=vtp: e.tensor_copy(vtm[:], vtp[:]), [tvt])
                                tok_vtm = P.tok('v')
                            if pr == 0:
                                tap('q0', q_sb[:], tjs)
                                tap('k0', k_sb[:], tjs)
                                tap('vtm0', vtm[:], [tok_vtm])
                            for hh in range(2):
                                head = pr * 2 + hh
                                P.do('g', lambda e, sl=slopes[head], am=am, relm=relm: e.tensor_scalar(am[:], relm[:], sl, None, ALU.mult), [tok_rel, gates.get('prev_exp')])
                                tok_am = P.tok('g')
                                gates['qkv'] = tjs + [tok_vtm]
                                gates['vtm'] = [tok_vtm]
                                gates['alibi'] = [tok_am]

                                def ao_w(qt, avp, w, pi=pi, hh=hh):
                                    P.do('v', lambda e, col=qt, h0=hh * 64, ao=ao, pi=pi, avp=avp: e.tensor_copy(ao[h0:h0 + 64, pi * T + col * 128:pi * T + (col + 1) * 128], avp[h0:h0 + 64, :]), w)
                                    tok_ao_pg.append(P.tok('v'))
                                    return tok_ao_pg[-1]
                                attn_head(f'a{pr}h{hh}', hh * 64, q_sb, k_sb, vtm, p_sb, pt_sb, nmx, sums, rcs, ao_w, list(range(NT)), lambda qt: qt * 128, lambda qt: (0, (qt + 1) * 128), gates, alibi_ap=am[:], alibi_base=lambda qt: 2176 - qt * 128)
                    tap(f'ao{pg}', ao[:], tok_ao_pg[-1:])
                    P.barrier()
                    with contextlib.ExitStack() as ew:
                        wstg2 = ew.enter_context(nc.sbuf_tensor(_U('pw'), [128, 4096], F32))
                        wbf2 = ew.enter_context(nc.sbuf_tensor(_U('pwb'), [128, 4096], BF16))
                        tmp = [ew.enter_context(nc.sbuf_tensor(_U(f'pt{i}'), [128, 512], BF16)) for i in range(2)]
                        pps = [ew.enter_context(nc.psum_tensor(_U(f'pp{i}'), [128, 512], F32)) for i in range(2)]
                        ptp = [ew.enter_context(nc.psum_tensor(_U(f'ptr{i}'), [128, 512], BF16)) for i in range(2)]
                        tev = {}
                        ttr = {}
                        cnt = 0
                        tw = dma_ch(wstg2[:], WPRJ[pg * 512:(pg + 1) * 512, :], tok_ao_pg[-1:])
                        P.do('g', lambda e, wbf2=wbf2, wstg2=wstg2: e.tensor_copy(wbf2[:], wstg2[:]), [tw])
                        tc = P.tok('g')
                        for of in range(8):
                            for sp in range(4):
                                b = cnt % 2
                                cnt += 1
                                w0 = [tc, tev.get(b)] + tok_ao_pg[-1:]
                                for k in range(4):
                                    P.do('t', lambda e, kk=k, oo=of, ss=sp, bb=b, pps=pps, wbf2=wbf2, ao=ao: e.matmul(pps[bb][:], wbf2[:, kk * 1024 + oo * 128:kk * 1024 + (oo + 1) * 128], ao[:, kk * T + ss * 512:kk * T + (ss + 1) * 512], start=kk == 0, stop=kk == 3), w0 if k == 0 else [])
                                tm = P.tok('t')
                                P.do('v', lambda e, bb=b, oo=of, ss=sp, tmp=tmp, pps=pps, gb=gb: e.tensor_tensor(tmp[bb][:], pps[bb][:], gb[:, oo * T + ss * 512:oo * T + (ss + 1) * 512], ALU.mult), [tm, tok_g0, ttr.get(bb if False else b)])
                                tv = P.tok('v')
                                tev[b] = tv
                                for j in range(4):
                                    P.do('t', lambda e, bb=b, jj=j, ptp=ptp, tmp=tmp, ident=ident: e.transpose(ptp[bb][:, jj * 128:(jj + 1) * 128], tmp[bb][:, jj * 128:(jj + 1) * 128], ident), [tv, t_xadd.get(b)] if j == 0 else [])
                                tt2 = P.tok('t')
                                ttr[b] = tt2
                                dstx = x_sb[:].rearrange('p (a d) -> p a d', a=NT)[:, sp * 4:sp * 4 + 4, of * 128:(of + 1) * 128]
                                P.do('v', lambda e, d=dstx, bb=b, ptp=ptp: e.tensor_tensor(d, d, ptp[bb][:].rearrange('p (j w) -> p j w', j=4), ALU.add), [tt2, tok_x])
                                t_xadd[b] = P.tok('v')
                    tok_g_pg = [t_xadd.get(0), t_xadd.get(1)]
            fin = [t for t in tok_g_pg if t]
            tap('xb0', x_sb[:], fin)
        if upto in ('ln', 'ffn0', 'b0'):
            if not fin:
                fin = xn_gate + ([tok_g0] if tok_g0 else [])
            P.do('y', lambda e, OUT=OUT, x_sb=x_sb: e.dma_start(out=OUT.rearrange('(a p) d -> p a d', p=128), in_=x_sb[:].rearrange('p (a d) -> p a d', a=NT)), fin + [tok_x], dma=True)
            tout = P.tok('y')
            P.do('g', lambda e: e.nop(), [tout] + tap_toks)
            P.do('y', lambda e: e.nop(), [tout] + tap_toks)
            return finalize(nc, P, er)
        P.barrier()
        ln1 = emit_ln_tm(xtile, NT, xn_dst, 'l1', EPS0, waits=fin)
        xn_gate = ln1
        tap('xn1', xn[:], ln1)
        P.do('y', lambda e, XSP=XSP, x_sb=x_sb: e.dma_start(out=XSP, in_=x_sb[:]), fin + ln1, dma=True)
        tok_spill = P.tok('y')
        fin = ln1 + [tok_spill]
    fin = emit_l1_ffn(nc, P, xn, gb, CB, ST, None, CON, tok_con, WQKV6, WO6, QS6, KS6, LNG6, LNB6, W1A6, W1G6, B1A6, B1G6, W26, B26, fin, tap, attn_head, emit_ln_fm, zc, dma_ch, upto, n_sub, WBuf, (WQ6BF, WO6BF, W1A6BF, W1G6BF, W26BF))
    if upto == 'ffn1':
        P.do('y', lambda e, OUT=OUT, XSP=XSP: e.dma_start(out=OUT.rearrange('t d -> (t d)'), in_=XSP.rearrange('p n -> (p n)')), fin, dma=True)
        fin = [P.tok('y')]
        P.do('g', lambda e: e.nop(), fin + tap_toks)
        P.do('y', lambda e: e.nop(), fin + tap_toks)
        return finalize(nc, P, er)
    emit_l1_attn(nc, P, xn, gb, CB, ST, None, tok_con, WQKV1, WO1, BQKV1, BO1, NQW, NKW, XSP, OUT, fin, tap, tap_toks, attn_head, zc, dma_ch, WBuf)
    return finalize(nc, P, er)

def finalize(nc, P, er=None):
    sems = {}
    with contextlib.ExitStack() as esem:
        for k in ('v', 's', 't', 'g', 'y', 'vd', 'sd', 'td', 'gd'):
            sems[k] = esem.enter_context(nc.semaphore('sem_' + k))
        with nc.Block() as block:
            P.emit(nc, block, sems)
    return nc

def emit_l1_ffn(nc, P, xn, gb, CB, ST, sqs, CON, tok_con, WQKV6, WO6, QS6, KS6, LNG6, LNB6, W1A6, W1G6, B1A6, B1G6, W26, B26, fin, tap, attn_head, emit_ln, zc, dma_ch, upto, n_sub, WBuf=None, WCACHE=None):
    ident = CB['ident'][:]
    with contextlib.ExitStack() as EZ:
        z_sb = EZ.enter_context(nc.sbuf_tensor(_U('z'), [128, NC * WZ], F32))
        nmx = EZ.enter_context(nc.sbuf_tensor(_U('znm'), [128, 16], F32))
        sums = EZ.enter_context(nc.sbuf_tensor(_U('zsm'), [128, 16], F32))
        rcs = EZ.enter_context(nc.sbuf_tensor(_U('zrc'), [128, 16], F32))
        lnbb = EZ.enter_context(nc.sbuf_tensor(_U('lnbb'), [128, 8], BF16))
        beffa = EZ.enter_context(nc.sbuf_tensor(_U('beffa'), [128, NK2], F32))
        beffg = EZ.enter_context(nc.sbuf_tensor(_U('beffg'), [128, NK2], F32))
        P.do('g', lambda e, z_sb=z_sb: e.memset(z_sb[:].rearrange('p (c w) -> p c w', c=NC)[:, :, 0:1], 0.0), fin)
        P.do('v', lambda e, z_sb=z_sb, xn=xn: e.tensor_copy(z_sb[:].rearrange('p (c w) -> p c w', c=NC)[:, :, 1:WZ], xn[:].rearrange('p (c w) -> p c w', c=NC)[:, :, 1:WZ]), fin)
        tz = P.tok('v')
        tzg = P.tok('g')
        z_gate = [tz, tzg]
        WQ6BF, WO6BF, W1A6BF, W1G6BF, W26BF = WCACHE
        pstg = [EZ.enter_context(nc.sbuf_tensor(_U('pws0'), [128, 1024], F32))] * 2
        pbf = [EZ.enter_context(nc.sbuf_tensor(_U(f'pwb{i}'), [128, 1024], BF16)) for i in range(2)]
        plg = EZ.enter_context(nc.sbuf_tensor(_U('plg'), [128, 8], F32))
        pst = {'f': [None, None], 'b': [None, None], 'lg': None, 'i': 0}

        def prep_l1f(li):
            def slab(src2d, dst2d, scaled, extra=()):
                i = pst['i'] % 2
                pst['i'] += 1
                cc = src2d.shape[0] // 128
                n = src2d.shape[1]
                P.do('g', lambda e, d=pstg[i][:, 0:cc * n].rearrange('p (c n) -> p c n', c=cc), s=src2d.rearrange('(c p) n -> p c n', p=128): e.dma_start(out=d, in_=s), [pst['f'][0], pst['f'][1]] + list(extra), dma=True)
                tin = P.tok('g')
                if scaled:
                    for k in range(cc):
                        P.do('g', lambda e, ii=i, kk=k, n=n, pbf=pbf, pstg=pstg, plg=plg: e.tensor_scalar(pbf[ii][:, kk * n:(kk + 1) * n], pstg[ii][:, kk * n:(kk + 1) * n], 1.0, plg[:, kk:kk + 1], ALU.mult, ALU.mult), [tin, pst['b'][i]] if k == 0 else [])
                else:
                    P.do('g', lambda e, ii=i, w=cc * n, pbf=pbf, pstg=pstg: e.tensor_copy(pbf[ii][:, 0:w], pstg[ii][:, 0:w]), [tin, pst['b'][i]])
                tcst = P.tok('g')
                pst['f'][i] = tcst
                P.do('g', lambda e, d=dst2d.rearrange('(c p) n -> p c n', p=128), s=pbf[i][:, 0:cc * n].rearrange('p (c n) -> p c n', c=cc): e.dma_start(out=d, in_=s), [tcst], dma=True)
                pst['b'][i] = P.tok('g')
                return tcst
            out = {}
            for off in range(0, 3 * D, 128):
                slab(WQKV6[li][:, off:off + 128], WQ6BF[li][:, off:off + 128], False)
            out['wq'] = pst['b'][(pst['i'] + 1) % 2]
            for r in range(0, D, 128):
                slab(WO6[li][r:r + 128, :], WO6BF[li][r:r + 128, :], False)
            out['wo'] = pst['b'][(pst['i'] + 1) % 2]
            P.do('g', lambda e, d=plg, s=LNG6[li]: e.dma_start(out=d[:], in_=s), [pst['lg']], dma=True)
            tlg = P.tok('g')
            last_sc = None
            for off in range(0, DIP, 128):
                last_sc = slab(W1A6[li][:, off:off + 128], W1A6BF[li][:, off:off + 128], True, [tlg] if off == 0 else ())
            for off in range(0, DIP, 128):
                last_sc = slab(W1G6[li][:, off:off + 128], W1G6BF[li][:, off:off + 128], True)
            pst['lg'] = last_sc
            out['w1'] = pst['b'][(pst['i'] + 1) % 2]
            for r in range(0, DIP, 128):
                slab(W26[li][r:r + 128, :], W26BF[li][r:r + 128, :], False)
            out['w2'] = pst['b'][(pst['i'] + 1) % 2]
            return out
        preps = [prep_l1f(0)]
        nsteps = int(round(n_sub * 2))
        for li in range((nsteps + 1) // 2):
            P.do('v', lambda e, gb=gb, z_sb=z_sb: e.tensor_copy(gb[:].rearrange('p (c w) -> p c w', c=NC), z_sb[:].rearrange('p (c w) -> p c w', c=NC)[:, :, 1:WZ]), z_gate)
            t_zbf = P.tok('v')

            def zb_rhs(k, t0, t1):
                if k < 4:
                    return gb[:, k * T + t0:k * T + t1]
                if t0 == 0:
                    return gb[:, k * T + 0:k * T + t1 - 1]
                return gb[:, k * T + t0 - 1:k * T + t1 - 1]
            tok_z1 = emit_l1f_attn(nc, P, z_sb, gb, zb_rhs, CB, CON, tok_con, nmx, sums, rcs, sqs, WQ6BF, WO6BF, QS6, KS6, li, [t_zbf] + z_gate, attn_head, dma_ch, preps[li])
            z_gate = tok_z1
            tap(f'z_a{li}', z_sb[:], tok_z1)
            if 2 * li + 1 >= nsteps:
                break
            P.barrier(('v', 's', 't', 'y'))
            if li + 1 < (nsteps + 1) // 2:
                preps.append(prep_l1f(li + 1))
            tok_z2 = emit_l1f_ffn(nc, P, z_sb, xn, CB, ST, sqs, tok_con, LNG6, LNB6, W1A6, W1G6, B1A6, B1G6, W26, B26, lnbb, beffa, beffg, li, z_gate, emit_ln, zc, dma_ch, (W1A6BF, W1G6BF, W26BF, preps[li]))
            z_gate = tok_z2
            tap(f'z_f{li}', z_sb[:], tok_z2)
        toks = []
        for c in range(NC):
            P.do('s', lambda e, cc=c, gb=gb, zc=zc, z_sb=z_sb: e.activation(gb[:, cc * T:(cc + 1) * T], zc(z_sb, cc, 0, T), AF.Silu), z_gate)
            toks.append(P.tok('s'))
        tap('g2', gb[:], toks)
    return toks

def emit_l1f_attn(nc, P, z_sb, gb, zb_rhs, CB, CON, tok_con, nmx, sums, rcs, sqs, WQ6BF, WO6BF, QS6, KS6, li, gate0, attn_head, dma_ch, prep=None):
    ident = CB['ident'][:]
    t_zadd = {}
    with contextlib.ExitStack() as EO:
        qsc = EO.enter_context(nc.sbuf_tensor(_U('qsc'), [128, 1], F32))
        ksc = EO.enter_context(nc.sbuf_tensor(_U('ksc'), [128, 1], F32))
        P.do('y', lambda e, qsc=qsc, QS6=QS6, li=li: e.dma_start(out=qsc[:], in_=QS6[li]), list(gate0), dma=True)
        P.do('y', lambda e, ksc=ksc, KS6=KS6, li=li: e.dma_start(out=ksc[:], in_=KS6[li]), dma=True)
        tok_qs = P.tok('y')
        for pg in range(2):
            P.barrier()
            tok_ao = []
            eg_outer = contextlib.ExitStack()
            ao = eg_outer.enter_context(nc.sbuf_tensor(_U('zao'), [128, 4 * T], BF16))
            with contextlib.ExitStack() as ep:
                rotb = {k: ep.enter_context(nc.sbuf_tensor(_U('rb_') + k, [128, T], BF16)) for k in ('rotc', 'rots', 'rotci', 'rotsi')}
                rtk = []
                for k in ('rotc', 'rots', 'rotci', 'rotsi'):
                    P.do('y', lambda e, d=rotb[k], s=CON[k]: e.dma_start(out=d[:], in_=s), list(gate0), dma=True)
                    rtk.append(P.tok('y'))
                tok_rot = rtk[-1]
                wbf = [ep.enter_context(nc.sbuf_tensor(_U(f'zwb{i}'), [128, 1024], BF16)) for i in range(2)]
                q_sb = ep.enter_context(nc.sbuf_tensor(_U('zq'), [128, T], BF16))
                k_sb = ep.enter_context(nc.sbuf_tensor(_U('zk'), [128, T], BF16))
                v_sb = ep.enter_context(nc.sbuf_tensor(_U('zv'), [128, T], BF16))
                vtm = ep.enter_context(nc.sbuf_tensor(_U('zvt'), [128, T], BF16))
                pt_sb = ep.enter_context(nc.sbuf_tensor(_U('zpt'), [128, 512], BF16))
                p_sb = ep.enter_context(nc.sbuf_tensor(_U('zp'), [128, 512], BF16))
                fs_f = [ep.enter_context(nc.sbuf_tensor(_U(f'zf{i}'), [2, 512], F32)) for i in range(2)]
                fs_b = [ep.enter_context(nc.sbuf_tensor(_U(f'zfb{i}'), [2, 512], BF16)) for i in range(2)]
                t1b = [ep.enter_context(nc.sbuf_tensor(_U(f'zt1{i}'), [128, 512], BF16)) for i in range(2)]
                t2b = [ep.enter_context(nc.sbuf_tensor(_U(f'zt2{i}'), [128, 512], BF16)) for i in range(2)]
                sq2 = [ep.enter_context(nc.sbuf_tensor(_U(f'zsq{i}'), [128, 512], BF16)) for i in range(2)]
                pw = prep.get('wq') if prep is not None else None
                wfree = {0: pw, 1: pw}
                gates = {}
                prevs = {}
                for pi in range(4):
                    pr = pg * 4 + pi
                    tjs = []
                    for j, off in enumerate((pr * 128, D + pr * 128, 2 * D + pr * 128)):
                        wi = (pr * 3 + j) % 2
                        tc = dma_ch(wbf[wi][:], WQ6BF[li][:, off:off + 128], [wfree[wi]])
                        wb_ = wbf[wi]
                        dst = (q_sb, k_sb, v_sb)[j]
                        with contextlib.ExitStack() as eq:
                            qps = [eq.enter_context(nc.psum_tensor(_U(f'zqp{i}'), [128, 512], F32)) for i in range(2)]
                            sqp = [eq.enter_context(nc.psum_tensor(_U(f'zsqp{i}'), [32, 512], F32)) for i in range(2)]
                            f2p = [eq.enter_context(nc.psum_tensor(_U(f'zf2p{i}'), [128, 512], F32)) for i in range(2)]
                            swp = [eq.enter_context(nc.psum_tensor(_U(f'zswp{i}'), [128, 512], F32)) for i in range(2)]
                            tev = prevs.setdefault('tev', {})
                            tsqd = prevs.setdefault('tsqd', {})
                            for sp in range(4):
                                b = sp % 2
                                t0 = sp * 512
                                w0 = [tc, tev.get(b), tsqd.get(b)] + gate0 + ([gates.get('prev_exp'), gates.get('prev_tav'), gates.get('prev_ao'), gates.get('prev_ptev')] if j == 0 else [])
                                for k in range(NC):
                                    sh = k >= 4
                                    rhs = zb_rhs(k, t0, t0 + 512)
                                    o0 = 1 if sh and t0 == 0 else 0
                                    P.do('t', lambda e, kk=k, bb=b, r=rhs, oo=o0, qps=qps, wb_=wb_: e.matmul(qps[bb][:, oo:512], wb_[:, kk * 128:(kk + 1) * 128], r, start=kk == 0, stop=kk == NC - 1), w0 if k == 0 else [])
                                tm = P.tok('t')
                                wfree[wi] = tm
                                if j == 2:
                                    P.do('s', lambda e, ss=sp, bb=b, v_sb=v_sb, qps=qps: e.copy(v_sb[:, ss * 512:(ss + 1) * 512], qps[bb][:]), [tm])
                                    tev[b] = P.tok('s')
                                    tjs.append(tev[b])
                                    continue
                                P.do('s', lambda e, bb=b, sq2=sq2, qps=qps: e.activation(sq2[bb][:], qps[bb][:], AF.Square), [tm, prevs.get(('tssp', b))])
                                tsq = P.tok('s')
                                tsqd[b] = tsq
                                scl = 8.0 if j == 0 else 1.0
                                scol = qsc if j == 0 else ksc
                                P.do('v', lambda e, d=dst, ss=sp, bb=b, s_=scl, sc=scol, qps=qps: e.tensor_scalar(d[:, ss * 512:(ss + 1) * 512], qps[bb][:], s_, sc[:], ALU.mult, ALU.mult), [tm, tok_qs, tsq])
                                tev[b] = P.tok('v')
                                P.do('t', lambda e, bb=b, sqp=sqp, CB=CB, sq2=sq2: e.matmul(sqp[bb][:], CB['bdones'][:], sq2[bb][:], start=True, stop=True), [tsq, tok_con, prevs.get(('tf1', b))])
                                tssp = P.tok('t')
                                prevs[('tssp', b)] = tssp
                                P.do('s', lambda e, bb=b, fs_f=fs_f, sqp=sqp: e.activation(fs_f[bb][:], sqp[bb][0:2, :], AF.Sqrt), [tssp, prevs.get(('tf3', b))])
                                tf1 = P.tok('s')
                                prevs[('tf1', b)] = tf1
                                P.do('v', lambda e, bb=b, fs_f=fs_f: e.tensor_scalar(fs_f[bb][:], fs_f[bb][:], 1e-12, None, ALU.max), [tf1])
                                tf2 = P.tok('v')
                                P.do('v', lambda e, bb=b, fs_f=fs_f: e.reciprocal(fs_f[bb][:], fs_f[bb][:]), [tf2])
                                tf3 = P.tok('v')
                                prevs[('tf3', b)] = tf3
                                P.do('s', lambda e, bb=b, fs_b=fs_b, fs_f=fs_f: e.copy(fs_b[bb][:], fs_f[bb][:]), [tf3, prevs.get(('tf5', b))])
                                tf4 = P.tok('s')
                                P.do('t', lambda e, bb=b, f2p=f2p, CB=CB, fs_b=fs_b: e.matmul(f2p[bb][:], CB['f2sel'][:], fs_b[bb][:], start=True, stop=True), [tf4, prevs.get(('tqn', b))])
                                tf5 = P.tok('t')
                                prevs[('tf5', b)] = tf5
                                P.do('v', lambda e, d=dst, ss=sp, bb=b, f2p=f2p: e.tensor_mul(d[:, ss * 512:(ss + 1) * 512], d[:, ss * 512:(ss + 1) * 512], f2p[bb][:]), [tf5])
                                tqn = P.tok('v')
                                prevs[('tqn', b)] = tqn
                                P.do('t', lambda e, d=dst, ss=sp, bb=b, swp=swp, CB=CB: e.matmul(swp[bb][:], CB['swapm'][:], d[:, ss * 512:(ss + 1) * 512], start=True, stop=True), [tqn, prevs.get(('tr2', b))])
                                tsw = P.tok('t')
                                rc_, rs_ = ('rotc', 'rots') if j == 0 else ('rotci', 'rotsi')
                                P.do('v', lambda e, d=dst, ss=sp, bb=b, rr=rc_, t1b=t1b, rotb=rotb: e.tensor_mul(t1b[bb][:], d[:, ss * 512:(ss + 1) * 512], rotb[rr][:, ss * 512:(ss + 1) * 512]), [tqn, tok_rot, prevs.get(('tadd', b))])
                                P.do('v', lambda e, ss=sp, bb=b, rr=rs_, t2b=t2b, swp=swp, rotb=rotb: e.tensor_mul(t2b[bb][:], swp[bb][:], rotb[rr][:, ss * 512:(ss + 1) * 512]), [tsw, prevs.get(('tadd', b))])
                                tr2 = P.tok('v')
                                prevs[('tr2', b)] = tr2
                                P.do('g', lambda e, d=dst, ss=sp, bb=b, t1b=t1b, t2b=t2b: e.tensor_add(d[:, ss * 512:(ss + 1) * 512], t1b[bb][:], t2b[bb][:]), [tr2])
                                tadd = P.tok('g')
                                prevs[('tadd', b)] = tadd
                                tjs.append(tadd)
                    with contextlib.ExitStack() as ev:
                        vtp = ev.enter_context(nc.psum_tensor(_U('zvp'), [128, 2048], BF16))
                        for a in range(NT):
                            P.do('t', lambda e, aa=a, vtp=vtp, v_sb=v_sb, ident=ident: e.transpose(vtp[:, aa * 128:(aa + 1) * 128], v_sb[:, aa * 128:(aa + 1) * 128], ident), tjs + [gates.get('prev_tav'), gates.get('prev_ao'), gates.get('prev_ptev')] if a == 0 else [])
                        tvt = P.tok('t')
                        P.do('v', lambda e, vtm=vtm, vtp=vtp: e.tensor_copy(vtm[:], vtp[:]), [tvt])
                        tok_vtm = P.tok('v')
                    gates['qkv'] = tjs + [tok_vtm]
                    gates['vtm'] = [tok_vtm]

                    def mask_fn(qt, sps, _dummy=None):
                        klen = (qt - max(0, qt - 2) + 1) * 128
                        ops = []
                        if qt >= 2:
                            ops.append((0, 'tri_lo'))
                        ops.append((klen - 128, 'tri_hi'))
                        for i, (c0, nm) in enumerate(ops):
                            P.do('t', lambda e, cc=c0, n=nm, last=i == len(ops) - 1, sps=sps, CB=CB: e.matmul(sps[:, cc:cc + 128], CB[n][:], CB['negi'][:], start=False, stop=last))
                    for hh in range(2 if L1F_PART in ('attn', 'full') else 0):

                        def ao_w(qt, avp, w, pi=pi, hh=hh):
                            P.do('v', lambda e, col=qt, h0=hh * 64, ao=ao, pi=pi, avp=avp: e.tensor_copy(ao[h0:h0 + 64, pi * T + col * 128:pi * T + (col + 1) * 128], avp[h0:h0 + 64, :]), w)
                            tok_ao.append(P.tok('v'))
                            return tok_ao[-1]
                        attn_head(f'z{pr}h{hh}', hh * 64, q_sb, k_sb, vtm, p_sb, pt_sb, nmx, sums, rcs, ao_w, list(range(NT)), lambda qt: qt * 128, lambda qt: (max(0, qt - 2) * 128, (qt - max(0, qt - 2) + 1) * 128), gates, mask_fn=mask_fn)
            P.barrier()
            if L1F_PART != 'full':
                eg_outer.close()
                continue
            with contextlib.ExitStack() as ew:
                wbf2 = ew.enter_context(nc.sbuf_tensor(_U('zwb2'), [128, 4096], BF16))
                pps = [ew.enter_context(nc.psum_tensor(_U(f'zpp{i}'), [128, 512], F32)) for i in range(2)]
                tc = dma_ch(wbf2[:], WO6BF[li][pg * 512:(pg + 1) * 512, :], tok_ao[-1:] + ([prep.get('wo')] if prep is not None else []))
                tev = {}
                cnt = 0
                for of in range(8):
                    for sp in range(4):
                        b = cnt % 2
                        cnt += 1
                        w0 = [tc, tev.get(b)] + tok_ao[-1:]
                        for k in range(4):
                            P.do('t', lambda e, kk=k, oo=of, ss=sp, bb=b, pps=pps, wbf2=wbf2, ao=ao: e.matmul(pps[bb][:], wbf2[:, kk * 1024 + oo * 128:kk * 1024 + (oo + 1) * 128], ao[:, kk * T + ss * 512:kk * T + (ss + 1) * 512], start=kk == 0, stop=kk == 3), w0 if k == 0 else [])
                        tm = P.tok('t')
                        P.do('v', lambda e, bb=b, oo=of, ss=sp, z_sb=z_sb, pps=pps: e.tensor_tensor(z_sb[:, oo * WZ + 1 + ss * 512:oo * WZ + 1 + (ss + 1) * 512], pps[bb][:], z_sb[:, oo * WZ + 1 + ss * 512:oo * WZ + 1 + (ss + 1) * 512], ALU.add), [tm, t_zadd.get((b, of, sp))])
                        tev[b] = P.tok('v')
                        t_zadd[b, of, sp] = tev[b]
            eg_outer.close()
    return [t for t in t_zadd.values()]

def emit_l1f_ffn(nc, P, z_sb, xn, CB, ST, sqs, tok_con, LNG6, LNB6, W1A6, W1G6, B1A6, B1G6, W26, B26, lnbb, beffa, beffg, li, gate0, emit_ln, zc, dma_ch, WCACHE=None):
    EPS1 = 1e-05
    W1ABF, W1GBF, W2BF, prep = WCACHE
    t_zw = {}
    with contextlib.ExitStack() as EF:
        xn1 = EF.enter_context(nc.sbuf_tensor(_U('xn1'), [128, NC * 512], BF16))
        h_bf = EF.enter_context(nc.sbuf_tensor(_U('zh'), [128, NK2 * 512], BF16))
        wba = EF.enter_context(nc.sbuf_tensor(_U('zfwa'), [128, 4096], BF16))
        wbg = EF.enter_context(nc.sbuf_tensor(_U('zfwg'), [128, 4096], BF16))
        gel = [EF.enter_context(nc.sbuf_tensor(_U(f'zgel{i}'), [128, 512], BF16)) for i in range(2)]
        lg = EF.enter_context(nc.sbuf_tensor(_U('zlg'), [128, 8], F32))
        lb = EF.enter_context(nc.sbuf_tensor(_U('zlb'), [128, 8], F32))
        lgr = EF.enter_context(nc.sbuf_tensor(_U('zlgr'), [128, 8], F32))
        ba = EF.enter_context(nc.sbuf_tensor(_U('zba'), [128, NK2], F32))
        bg = EF.enter_context(nc.sbuf_tensor(_U('zbg'), [128, NK2], F32))
        b2 = EF.enter_context(nc.sbuf_tensor(_U('zb2'), [128, 8], F32))
        tks = []
        for dst, srcw in ((lg, LNG6[li]), (lb, LNB6[li]), (ba, B1A6[li]), (bg, B1G6[li]), (b2, B26[li])):
            P.do('y', lambda e, d=dst, s=srcw: e.dma_start(out=d[:], in_=s), list(gate0), dma=True)
            tks.append(P.tok('y'))
        tok_cst = tks[-1]
        P.do('v', lambda e, lgr=lgr, lg=lg: e.reciprocal(lgr[:], lg[:]), [tok_cst])
        trl = P.tok('v')
        for k in range(NC):
            P.do('s', lambda e, kk=k, lnbb=lnbb, lgr=lgr, lb=lb: e.activation(lnbb[:, kk:kk + 1], lgr[:, kk:kk + 1], AF.Identity, scale=lb[:, kk:kk + 1]), [trl] if k == 0 else [])
        tok_lnbb = P.tok('s')
        for q in range(4):
            P.barrier(('v', 's', 't', 'y'))
            tq0 = q * 512

            def src_col(a, c, t0=tq0):
                return zc(z_sb, c, t0 + a * 128, t0 + (a + 1) * 128, shift=c >= 4)

            def dst_fn(a):
                return xn1[:].rearrange('p (c w) -> p c w', c=NC)[:, :, a * 128:(a + 1) * 128]
            lt = emit_ln(src_col, 4, dst_fn, f'zl{li}_{q}', EPS1, waits=gate0 + [t_zw.get('last')])
            t_hev = None
            with contextlib.ExitStack() as e1:
                psa = [e1.enter_context(nc.psum_tensor(_U(f'zpsa{i}'), [128, 512], F32)) for i in range(2)]
                psg = [e1.enter_context(nc.psum_tensor(_U(f'zpsg{i}'), [128, 512], F32)) for i in range(2)]
                psb = e1.enter_context(nc.psum_tensor(_U('zpsb'), [128, 16], F32))
                t_stt = {}
                t_gel = {}
                ocg = 0
                for os_ in range(DIP // 512):
                    pass
                nos = DIP // 512 + (1 if DIP % 512 else 0)
                for os_ in range(nos):
                    o0 = os_ * 512
                    ow = min(512, DIP - o0)
                    nch = ow // 128
                    pw1 = [prep.get('w1')] if prep is not None and q == 0 and os_ == 0 else []
                    tca = dma_ch(wba[:, 0:8 * ow], W1ABF[li][:, o0:o0 + ow],
                                 [t_zw.get('wafree')] + pw1)
                    tcg = dma_ch(wbg[:, 0:8 * ow], W1GBF[li][:, o0:o0 + ow],
                                 [t_zw.get('wgfree')])
                    if q == 0:
                        prev_b1 = None
                        for oc in range(nch):
                            gco = o0 // 128 + oc
                            for k in range(NC):
                                P.do('t', lambda e, kk=k, occ=oc, w=ow, psb=psb, wba=wba, lnbb=lnbb: e.matmul(psb[:, 0:1], wba[:, kk * w + occ * 128:kk * w + occ * 128 + 128], lnbb[:, kk:kk + 1], start=kk == 0, stop=kk == NC - 1), [tca, tcg, tok_lnbb, prev_b1] if k == 0 else [])
                            tb1 = P.tok('t')
                            P.do('v', lambda e, gc=gco, beffa=beffa, ba=ba, psb=psb: e.tensor_tensor(beffa[:, gc:gc + 1], ba[:, gc:gc + 1], psb[:, 0:1], ALU.add), [tb1, tok_cst])
                            tba = P.tok('v')
                            for k in range(NC):
                                P.do('t', lambda e, kk=k, occ=oc, w=ow, psb=psb, wbg=wbg, lnbb=lnbb: e.matmul(psb[:, 1:2], wbg[:, kk * w + occ * 128:kk * w + occ * 128 + 128], lnbb[:, kk:kk + 1], start=kk == 0, stop=kk == NC - 1), [tba] if k == 0 else [])
                            tb2 = P.tok('t')
                            P.do('v', lambda e, gc=gco, beffg=beffg, bg=bg, psb=psb: e.tensor_tensor(beffg[:, gc:gc + 1], bg[:, gc:gc + 1], psb[:, 1:2], ALU.add), [tb2])
                            prev_b1 = P.tok('v')
                        tbef = P.tok('v')
                    else:
                        tbef = None
                    tsc = tcg
                    for oc in range(nch):
                        gco = o0 // 128 + oc
                        b = ocg % 2
                        ocg += 1
                        w0 = [tsc, tca, t_stt.get(b)] + lt
                        for k in range(NC):
                            P.do('t', lambda e, kk=k, occ=oc, w=ow, bb=b, psa=psa, wba=wba, xn1=xn1: e.matmul(psa[bb][:], wba[:, kk * w + occ * 128:kk * w + occ * 128 + 128], xn1[:, kk * 512:(kk + 1) * 512], start=kk == 0, stop=kk == NC - 1), w0 if k == 0 else [])
                        for k in range(NC):
                            P.do('t', lambda e, kk=k, occ=oc, w=ow, bb=b, psg=psg, wbg=wbg, xn1=xn1: e.matmul(psg[bb][:], wbg[:, kk * w + occ * 128:kk * w + occ * 128 + 128], xn1[:, kk * 512:(kk + 1) * 512], start=kk == 0, stop=kk == NC - 1), [t_gel.get(b)] if k == 0 else [])
                        thg = P.tok('t')
                        P.do('s', lambda e, gc=gco, bb=b, gel=gel, psg=psg, beffg=beffg: e.activation(gel[bb][:], psg[bb][:], AF.Gelu, bias=beffg[:, gc:gc + 1]), [thg])
                        tge = P.tok('s')
                        t_gel[b] = tge
                        P.do('v', lambda e, gc=gco, bb=b, h_bf=h_bf, psa=psa, beffa=beffa, gel=gel: e.scalar_tensor_tensor(h_bf[:, gc * 512:(gc + 1) * 512], psa[bb][:], beffa[:, gc:gc + 1], gel[bb][:], ALU.add, ALU.mult), [tge])
                        t_hev = P.tok('v')
                        t_stt[b] = t_hev
                    t_zw['wafree'] = thg
                    t_zw['wgfree'] = thg
            P.barrier(('v', 's', 't', 'y'))
            with contextlib.ExitStack() as e2:
                zps = [e2.enter_context(nc.psum_tensor(_U(f'zzp{i}'), [128, 512], F32)) for i in range(8)]
                t_mm = {}
                for fs in range(6):
                    r0 = fs * 512
                    rw = min(512, DIP - r0)
                    nch = rw // 128
                    pw2 = [prep.get('w2')] if prep is not None and q == 0 and fs == 0 else []
                    tc2 = dma_ch(wba[:, 0:nch * 1024], W2BF[li][r0:r0 + rw, :],
                                 [t_zw.get('w2free')] + ([t_hev] if fs == 0 else []) + pw2)
                    last = None
                    for of in range(8):
                        for m in range(nch):
                            kc = fs * 4 + m
                            P.do('t', lambda e, oo=of, mm2=m, k=kc, zps=zps, wba=wba, h_bf=h_bf: e.matmul(zps[oo][:], wba[:, mm2 * 1024 + oo * 128:mm2 * 1024 + (oo + 1) * 128], h_bf[:, k * 512:(k + 1) * 512], start=k == 0, stop=k == NK2 - 1), [tc2, t_hev] if of == 0 and m == 0 else [])
                        last = P.tok('t')
                        t_mm[of] = last
                    t_zw['w2free'] = last
                tlast = []
                for of in range(8):
                    P.do('v', lambda e, oo=of, t0=tq0, zc=zc, z_sb=z_sb, zps=zps, b2=b2: e.scalar_tensor_tensor(zc(z_sb, oo, t0, t0 + 512), zps[oo][:], b2[:, oo:oo + 1], zc(z_sb, oo, t0, t0 + 512), ALU.add, ALU.add), [t_mm[of], tok_cst])
                    tlast.append(P.tok('v'))
                t_zw['last'] = tlast[-1]
        return tlast

def emit_l1_attn(nc, P, xn, gb, CB, ST, sqs, tok_con, WQKV1, WO1, BQKV1, BO1, NQW, NKW, XSP, OUT, fin, tap, tap_toks, attn_head, zc, dma_ch, WBuf=None):
    ident = CB['ident'][:]
    P.barrier()
    with contextlib.ExitStack() as EL:
        sqs = EL.enter_context(nc.sbuf_tensor(_U('sqs'), [128, 512], BF16))
        k_all = EL.enter_context(nc.sbuf_tensor(_U('lk'), [128, NC * T], BF16))
        nmx = EL.enter_context(nc.sbuf_tensor(_U('lnm'), [128, 16], F32))
        sums = EL.enter_context(nc.sbuf_tensor(_U('lsm'), [128, 16], F32))
        rcs = EL.enter_context(nc.sbuf_tensor(_U('lrc'), [128, 16], F32))
        bqkv = [EL.enter_context(nc.sbuf_tensor(_U(f'lbq{j}'), [128, 8], F32)) for j in range(3)]
        bo = EL.enter_context(nc.sbuf_tensor(_U('lbo'), [128, 8], F32))
        nw8 = EL.enter_context(nc.sbuf_tensor(_U('lnw'), [128, 8], F32))
        nkw_ = EL.enter_context(nc.sbuf_tensor(_U('lkw'), [128, 8], F32))
        rr_f = EL.enter_context(nc.sbuf_tensor(_U('lrr'), [1, 512], F32))
        rr_b = EL.enter_context(nc.sbuf_tensor(_U('lrb'), [1, 512], BF16))
        for j in range(3):
            P.do('y', lambda e, d=bqkv[j], BQKV1=BQKV1, j=j: e.dma_start(out=d[:], in_=BQKV1[j]), list(fin), dma=True)
        P.do('y', lambda e, bo=bo, BO1=BO1: e.dma_start(out=bo[:], in_=BO1), dma=True)
        P.do('y', lambda e, nw8=nw8, NQW=NQW: e.dma_start(out=nw8[:], in_=NQW), dma=True)
        P.do('y', lambda e, nkw_=nkw_, NKW=NKW: e.dma_start(out=nkw_[:], in_=NKW), dma=True)
        tok_b = P.tok('y')
        P.do('v', lambda e, nw8=nw8: e.tensor_scalar(nw8[:], nw8[:], 0.125, None, ALU.mult), [tok_b])
        tok_nw = P.tok('v')

        def qk_build(WCOL, bias_j, wcol_scale, dst_sb, dst_w, spans, tag, gate):
            toks = []
            with contextlib.ExitStack() as eb:
                wbq = WBuf(eb, tag + 'w', 1024)
                ssp = eb.enter_context(nc.psum_tensor(_U(tag + 'ss'), [1, 512], F32))
                f2p = eb.enter_context(nc.psum_tensor(_U(tag + 'f2'), [128, 512], F32))
                qps = [eb.enter_context(nc.psum_tensor(_U(tag + f'qp{i}'), [128, 512], F32)) for i in range(2)]
                t_wfree = gate[-1] if gate else None
                tevs = {}
                last_wmm = None
                prevs_qk = {}
                dst_off = min(spans) * 512
                for sp in spans:
                    t0 = sp * 512
                    lt0 = t0 - dst_off
                    tsqs = []
                    for oc in range(NC):
                        wbf, tc = wbq.load(WCOL(oc), [t_wfree] if t_wfree else ())
                        t_wfree = None
                        b = oc % 2
                        w0 = [tc, tevs.get(b)] + gate
                        for k in range(NC):
                            P.do('t', lambda e, kk=k, tt0=t0, bb=b, qps=qps, wbf=wbf, zc=zc, xn=xn: e.matmul(qps[bb][:], wbf[:, kk * 128:(kk + 1) * 128], zc(xn, kk, tt0, tt0 + 512), start=kk == 0, stop=kk == NC - 1), w0 if k == 0 else [])
                        tm = P.tok('t')
                        tevs['w', oc] = tm
                        last_wmm = tm
                        wbq.consumed(tm)
                        P.do('s', lambda e, bb=b, occ=oc, sqs=sqs, qps=qps, bqkv=bqkv, bias_j=bias_j: e.activation(sqs[:], qps[bb][:], AF.Square, bias=bqkv[bias_j][:, occ:occ + 1]), [tm, tok_b])
                        tsq = P.tok('s')
                        P.do('t', lambda e, occ=oc, ssp=ssp, CB=CB, sqs=sqs: e.matmul(ssp[:], CB['onescol'][:], sqs[:], start=occ == 0, stop=occ == NC - 1), [tsq, tok_con])
                        tsqs.append(P.tok('t'))
                        P.do('s', lambda e, bb=b, occ=oc, d=dst_sb, tt0=lt0, dst_w=dst_w, qps=qps, bqkv=bqkv, bias_j=bias_j: e.activation(d[:, occ * dst_w + tt0:occ * dst_w + tt0 + 512], qps[bb][:], AF.Identity, bias=bqkv[bias_j][:, occ:occ + 1]), [tm])
                        tevs[b] = P.tok('s')
                    P.do('s', lambda e, rr_f=rr_f, ssp=ssp, CB=CB: e.activation(rr_f[:, 0:512], ssp[:], AF.Sqrt, scale=CB['cpool'][0:1, 4:5], bias=CB['cpool'][0:1, 1:2]), [tsqs[-1], prevs_qk.get('tr4')])
                    tr1 = P.tok('s')
                    P.do('v', lambda e, rr_f=rr_f: e.reciprocal(rr_f[:, 0:512], rr_f[:, 0:512]), [tr1])
                    tr2 = P.tok('v')
                    P.do('s', lambda e, rr_b=rr_b, rr_f=rr_f: e.copy(rr_b[:, 0:512], rr_f[:, 0:512]), [tr2])
                    tr3 = P.tok('s')
                    P.do('t', lambda e, f2p=f2p, CB=CB, rr_b=rr_b, t0=t0: e.matmul(f2p[:], CB['onesrow'][:], rr_b[:, 0:512], start=True, stop=True), [tr3, tok_con, prevs_qk.get('apply')])
                    tr4 = P.tok('t')
                    prevs_qk['tr4'] = tr4
                    wcol = nw8 if wcol_scale else nkw_
                    for oc in range(NC):
                        P.do('v', lambda e, occ=oc, d=dst_sb, tt0=lt0, wc=wcol, dst_w=dst_w, f2p=f2p: e.scalar_tensor_tensor(d[:, occ * dst_w + tt0:occ * dst_w + tt0 + 512], d[:, occ * dst_w + tt0:occ * dst_w + tt0 + 512], wc[:, occ:occ + 1], f2p[:], ALU.mult, ALU.mult), [tr4, tok_nw] if oc == 0 else [tr4])
                        toks.append(P.tok('v'))
                        prevs_qk['apply'] = toks[-1]
            return toks
        tk = qk_build(lambda oc: WQKV1[:, D + oc * 128:D + (oc + 1) * 128], 1, False, k_all[:], T, range(4), 'lkb', fin)
        P.barrier()
        tap('l1k', k_all[:], tk)
        for th in range(2):
            P.barrier()
            h0 = th * 1024
            with contextlib.ExitStack() as EH:
                q_h = EH.enter_context(nc.sbuf_tensor(_U('lq'), [128, NC * 1024], BF16))
                ao = EH.enter_context(nc.sbuf_tensor(_U('lao'), [128, NC * 1024], BF16))
                tq = qk_build(lambda oc: WQKV1[:, oc * 128:(oc + 1) * 128], 0, True, q_h[:], 1024, [2 * th, 2 * th + 1], 'lqb', fin)
                P.barrier()
                tq = [tq[-1]]
                tok_ao = []
                with contextlib.ExitStack() as EP:
                    wstg = EP.enter_context(nc.sbuf_tensor(_U('lvw'), [128, 1024], F32))
                    wbf = EP.enter_context(nc.sbuf_tensor(_U('lvwb'), [128, 1024], BF16))
                    v_sb = EP.enter_context(nc.sbuf_tensor(_U('lv'), [128, T], BF16))
                    vtm = EP.enter_context(nc.sbuf_tensor(_U('lvt'), [128, T], BF16))
                    p_sb = EP.enter_context(nc.sbuf_tensor(_U('lp'), [128, T], BF16))
                    pt_sb = EP.enter_context(nc.sbuf_tensor(_U('lpt'), [128, T], BF16))
                    t_wfree = fin[-1] if fin else None
                    gates = {}
                    for pr in range(8):
                        tw = dma_ch(wstg[:], WQKV1[:, 2 * D + pr * 128:2 * D + (pr + 1) * 128], [t_wfree])
                        P.do('g', lambda e, wbf=wbf, wstg=wstg: e.tensor_copy(wbf[:], wstg[:]), [tw, gates.get('vfree')])
                        tc = P.tok('g')
                        t_wfree = tc
                        with contextlib.ExitStack() as eq:
                            qps = [eq.enter_context(nc.psum_tensor(_U(f'lvp{i}'), [128, 512], F32)) for i in range(2)]
                            tev = {}
                            for sp in range(4):
                                b = sp % 2
                                w0 = [tc, tev.get(b)] + fin + ([gates.get('prev_exp'), gates.get('prev_tav'), gates.get('prev_ao'), gates.get('prev_ptev')] if sp == 0 else [])
                                for k in range(NC):
                                    P.do('t', lambda e, kk=k, ss=sp, bb=b, qps=qps, wbf=wbf, zc=zc, xn=xn: e.matmul(qps[bb][:], wbf[:, kk * 128:(kk + 1) * 128], zc(xn, kk, ss * 512, (ss + 1) * 512), start=kk == 0, stop=kk == NC - 1), w0 if k == 0 else [])
                                tm = P.tok('t')
                                P.do('s', lambda e, ss=sp, bb=b, v_sb=v_sb, qps=qps, bqkv=bqkv, pr=pr: e.activation(v_sb[:, ss * 512:(ss + 1) * 512], qps[bb][:], AF.Identity, bias=bqkv[2][:, pr:pr + 1]), [tm, tok_b])
                                tev[b] = P.tok('s')
                            tvv = [tev.get(0), tev.get(1)]
                        with contextlib.ExitStack() as ev:
                            vtp = ev.enter_context(nc.psum_tensor(_U('lvtp'), [128, 2048], BF16))
                            for a in range(NT):
                                P.do('t', lambda e, aa=a, vtp=vtp, v_sb=v_sb, ident=ident: e.transpose(vtp[:, aa * 128:(aa + 1) * 128], v_sb[:, aa * 128:(aa + 1) * 128], ident), tvv + [gates.get('prev_tav'), gates.get('prev_ao'), gates.get('prev_ptev')] if a == 0 else [])
                            tvt = P.tok('t')
                            P.do('v', lambda e, vtm=vtm, vtp=vtp: e.tensor_copy(vtm[:], vtp[:]), [tvt])
                            tok_vtm = P.tok('v')
                        gates['vfree'] = tvt
                        gates['qkv'] = tq + tk + [tok_vtm]
                        gates['vtm'] = [tok_vtm]
                        qv = q_h[:, pr * 1024:(pr + 1) * 1024]
                        kv = k_all[:, pr * T:(pr + 1) * T]
                        for hh in range(2):

                            def ao_w(qt, avp, w, pr=pr, hh=hh):
                                P.do('v', lambda e, col=qt, hq=hh * 64, ao=ao, pr=pr, avp=avp: e.tensor_copy(ao[hq:hq + 64, pr * 1024 + col * 128:pr * 1024 + (col + 1) * 128], avp[hq:hq + 64, :]), w)
                                tok_ao.append(P.tok('v'))
                                return tok_ao[-1]
                            attn_head(f'l{th}_{pr}h{hh}', hh * 64, qv, kv, vtm, p_sb, pt_sb, nmx, sums, rcs, ao_w, list(range(8)), lambda qt: qt * 128, lambda qt: (0, T), gates)
                P.barrier()
                with contextlib.ExitStack() as EW:
                    x2 = EW.enter_context(nc.sbuf_tensor(_U('x2'), [128, 8 * 1024], F32))
                    wstg2 = EW.enter_context(nc.sbuf_tensor(_U('lw2'), [128, 4096], F32))
                    wbf2 = EW.enter_context(nc.sbuf_tensor(_U('lwb2'), [128, 4096], BF16))
                    tmp = [EW.enter_context(nc.sbuf_tensor(_U(f'ltm{i}'), [128, 512], BF16)) for i in range(2)]
                    pps = [EW.enter_context(nc.psum_tensor(_U(f'lpp{i}'), [128, 512], F32)) for i in range(2)]
                    ptp = [EW.enter_context(nc.psum_tensor(_U(f'lptr{i}'), [128, 512], BF16)) for i in range(2)]
                    P.do('y', lambda e, hh0=h0, x2=x2, XSP=XSP: e.dma_start(out=x2[:], in_=XSP[:, hh0 * 8:hh0 * 8 + 8192]), fin, dma=True)
                    tx2 = P.tok('y')
                    tev = {}
                    ttr = {}
                    tadd = {}
                    t_w2f = None
                    for wh in range(2):
                        tw = dma_ch(wstg2[:], WO1[wh * 512:(wh + 1) * 512, :], [t_w2f] + tok_ao[-1:])
                        P.do('g', lambda e, wbf2=wbf2, wstg2=wstg2: e.tensor_copy(wbf2[:], wstg2[:]), [tw, tadd.get('wfree')])
                        tc = P.tok('g')
                        t_w2f = tc
                        cnt = 0
                        for of in range(8):
                            for sp in range(2):
                                b = cnt % 2
                                cnt += 1
                                w0 = [tc, tev.get(b)] + tok_ao[-1:]
                                for k in range(4):
                                    kk2 = wh * 4 + k
                                    P.do('t', lambda e, kk=k, oo=of, ss=sp, bb=b, kg=kk2, pps=pps, wbf2=wbf2, ao=ao: e.matmul(pps[bb][:], wbf2[:, kk * 1024 + oo * 128:kk * 1024 + (oo + 1) * 128], ao[:, kg * 1024 + ss * 512:kg * 1024 + (ss + 1) * 512], start=kk == 0, stop=kk == 3), w0 if k == 0 else [])
                                tm = P.tok('t')
                                P.do('v', lambda e, bb=b, oo=of, ss=sp, hh0=h0, first=wh == 0, tmp=tmp, pps=pps, bo=bo, gb=gb: e.scalar_tensor_tensor(tmp[bb][:], pps[bb][:], bo[:, oo:oo + 1] if not first else 0.0, gb[:, oo * T + hh0 + ss * 512:oo * T + hh0 + (ss + 1) * 512], ALU.add, ALU.mult), [tm, tok_b, ttr.get(b)])
                                tv = P.tok('v')
                                tev[b] = tv
                                for j in range(4):
                                    P.do('t', lambda e, bb=b, jj=j, ptp=ptp, tmp=tmp, ident=ident: e.transpose(ptp[bb][:, jj * 128:(jj + 1) * 128], tmp[bb][:, jj * 128:(jj + 1) * 128], ident), [tv, tadd.get(b)] if j == 0 else [])
                                tt2 = P.tok('t')
                                ttr[b] = tt2
                                dstx = x2[:].rearrange('p (a d) -> p a d', a=8)[:, sp * 4:sp * 4 + 4, of * 128:(of + 1) * 128]
                                P.do('v', lambda e, d=dstx, bb=b, ptp=ptp: e.tensor_tensor(d, d, ptp[bb][:].rearrange('p (j w) -> p j w', j=4), ALU.add), [tt2, tx2])
                                tadd[b] = P.tok('v')
                        tadd['wfree'] = tm
                    P.do('y', lambda e, hh0=h0, OUT=OUT, x2=x2: e.dma_start(out=OUT[hh0:hh0 + 1024, :].rearrange('(a p) d -> p a d', p=128), in_=x2[:].rearrange('p (a d) -> p a d', a=8)), [tadd.get(0), tadd.get(1)], dma=True)
                    tout = P.tok('y')
                    if th == 1:
                        P.do('g', lambda e: e.nop(), [tout] + tap_toks)
                        P.do('y', lambda e: e.nop(), [tout] + tap_toks)
                    else:
                        out_tok_half = tout

def _chunkmaj(v):
    return np.ascontiguousarray(v.reshape(8, 128).T)

def _prep_inputs(I):
    A = np.ascontiguousarray
    W1 = I['L1_ffn_W1']
    W1a = np.zeros((6, D, DIP), np.float32)
    W1g = np.zeros((6, D, DIP), np.float32)
    W1a[:, :, :DI] = W1[:, :, :DI]
    W1g[:, :, :DI] = W1[:, :, DI:]
    b1 = I['L1_ffn_b1']
    b1a = np.zeros((6, 128, NK2), np.float32)
    b1g = np.zeros((6, 128, NK2), np.float32)
    for i in range(6):
        b1a[i].T.flat[:DI] = b1[i, :DI]
        b1g[i].T.flat[:DI] = b1[i, DI:]
    b1a = np.stack([np.pad(b1[i, :DI], (0, DIP - DI)).reshape(NK2, 128).T for i in range(6)])
    b1g = np.stack([np.pad(b1[i, DI:], (0, DIP - DI)).reshape(NK2, 128).T for i in range(6)])
    W2p = np.zeros((6, DIP, D), np.float32)
    W2p[:, :DI, :] = I['L1_ffn_W2']
    base = {'w_attnT': A(I['L0_attn_w_attn'].T), 'w_projT': A(I['L0_attn_w_proj'].T), 'w_fcT': A(I['L0_ffn_W_fc'].T), 'w_prT': A(I['L0_ffn_W_proj'].T), 'wqkv1T': A(np.concatenate([I['L1_attn_Wq'].T, I['L1_attn_Wk'].T, I['L1_attn_Wv'].T], 1)), 'wo1T': A(I['L1_attn_Wo'].T), 'bqkv1': np.stack([_chunkmaj(I['L1_attn_bq']), _chunkmaj(I['L1_attn_bk']), _chunkmaj(I['L1_attn_bv'])]), 'bo1': _chunkmaj(I['L1_attn_bo']), 'nqw': _chunkmaj(I['L1_attn_norm_q_w']), 'nkw': _chunkmaj(I['L1_attn_norm_k_w']), 'wqkv6': A(I['L1_ffn_Wqkv']), 'wo6': A(I['L1_ffn_Wo']), 'qs6': A(np.tile(I['L1_ffn_q_scale'], (1, 2))[:, :, None]), 'ks6': A(np.tile(I['L1_ffn_k_scale'], (1, 2))[:, :, None]), 'lng6': np.stack([_chunkmaj(I['L1_ffn_ln_g'][i]) for i in range(6)]), 'lnb6': np.stack([_chunkmaj(I['L1_ffn_ln_b'][i]) for i in range(6)]), 'w1a6': W1a, 'w1g6': W1g, 'b1a6': b1a, 'b1g6': b1g, 'w26': W2p, 'b26': np.stack([_chunkmaj(I['L1_ffn_b2'][i]) for i in range(6)])}
    for k, v in host_constants().items():
        if k not in ('identf', 'onescolf'):
            base['c_' + k] = v
    return base

def kernel(**inputs):
    base = _prep_inputs(inputs)
    x = np.ascontiguousarray(inputs['L0_attn_x']).astype(np.float32)
    nc = build('all')
    in_maps = [dict(base, x=np.ascontiguousarray(x[i])) for i in range(8)]
    res = run_bass_kernel_spmd(nc, in_maps, core_ids=list(range(8)))
    return np.stack([np.asarray(r['out']) for r in res.results])

# version 11
# speedup vs baseline: 1.0457x; 1.0457x over previous
import math
import os as _os
import contextlib
L1F_PART = _os.environ.get('L1F_PART', 'full')
L1F_PREP = _os.environ.get('L1F_PREP', 'all')
L1F_STAGES = ['qkv', 'sq', 'ev', 'ssp', 'sqrt', 'max', 'rcp', 'cpy', 'f2', 'f', 'all']
def _prep_lt(s):
    return L1F_STAGES.index(L1F_PREP) < L1F_STAGES.index(s)
import numpy as np
import concourse.bass as bass
import concourse.mybir as mybir
from concourse.bass_utils import run_bass_kernel_spmd
F32 = mybir.dt.float32
BF16 = mybir.dt.bfloat16
AF = mybir.ActivationFunctionType
ALU = mybir.AluOpType
AX = mybir.AxisListType
T = 2048
D = 1024
NT = 16
NC = 8
WZ = 2049
NEG = -3e+38
DI = 2730
DIP = 2816
NK2 = 22
EPS0 = 1e-05
_UC = [0]

def _U(name):
    _UC[0] += 1
    return f'{name}_{_UC[0]}'

class Stream:

    def __init__(self, key):
        self.key = key
        self.ops = []
        self.count = 0
        self.dcount = 0

    def tok(self):
        last = self.ops[-1]
        if last[2] is None:
            if last[3] and self.key != 'y':
                self.dcount += 16
                last[2] = ('d', self.dcount)
            else:
                self.count += 16 if last[3] else 1
                last[2] = ('c', self.count)
        kd, val = last[2]
        return (self.key + 'd' if kd == 'd' else self.key, val)

class Prog:

    def __init__(self):
        self.streams = {k: Stream(k) for k in ('v', 's', 't', 'g', 'y')}
        self.pending = {}

    def do(self, key, fn, waits=(), dma=False):
        st = self.streams[key]
        wd = dict(self.pending.pop(key, ()))
        for w in waits:
            if w is None:
                continue
            k, val = w
            if k == key:
                continue
            if wd.get(k, 0) < val:
                wd[k] = val
        st.ops.append([fn, wd, None, dma])
        if dma:
            st.tok()
        return st

    def tok(self, key):
        return self.streams[key].tok()

    def barrier(self, keys=('v', 's', 't', 'g', 'y')):
        toks = {}
        for k in keys:
            st = self.streams[k]
            if st.ops:
                toks[k] = st.tok()
        for k in keys:
            pend = self.pending.setdefault(k, {})
            for kk, t in toks.items():
                if kk == k:
                    continue
                rk, val = t
                if pend.get(rk, 0) < val:
                    pend[rk] = val
        return toks

    def emit(self, nc, block, sems):

        def make(stream):

            def run(eng):
                hw = {}
                for fn, waits, inc, is_dma in stream.ops:
                    for pk, val in waits.items():
                        if hw.get(pk, 0) < val:
                            eng.wait_ge(sems[pk], val)
                            hw[pk] = val
                    ins = fn(eng)
                    if inc is not None:
                        sk = stream.key + 'd' if inc[0] == 'd' else stream.key
                        ins.then_inc(sems[sk], 16 if is_dma else 1)
            return run
        block.vector(make(self.streams['v']))
        block.scalar(make(self.streams['s']))
        block.tensor(make(self.streams['t']))
        block.gpsimd(make(self.streams['g']))
        block.sync(make(self.streams['y']))

def alibi_slopes(n=16):
    start = 2.0 ** (-2.0 ** (-(math.log2(n) - 3)))
    return [start * start ** i for i in range(n)]

def host_constants():
    import ml_dtypes
    bf = lambda a: np.ascontiguousarray(a).astype(ml_dtypes.bfloat16)
    c = {}
    c['ident'] = bf(np.eye(128))
    c['identf'] = np.eye(128, dtype=np.float32)
    sw = np.zeros((128, 128), np.float32)
    for b in range(2):
        for i in range(32):
            sw[b * 64 + i, b * 64 + 32 + i] = 1.0
            sw[b * 64 + 32 + i, b * 64 + i] = 1.0
    c['swapm'] = bf(sw)
    k_ = np.arange(128)[:, None]
    r_ = np.arange(128)[None, :]
    c['tri_lo'] = bf((k_ < r_).astype(np.float32))
    c['tri_hi'] = bf((k_ > r_).astype(np.float32))
    c['negi'] = bf(np.eye(128) * NEG)
    bd = np.zeros((128, 32), np.float32)
    bd[0:64, 0] = 1.0
    bd[64:128, 1] = 1.0
    c['bdones'] = bf(bd)
    f2 = np.zeros((2, 128), np.float32)
    f2[0, 0:64] = 1.0
    f2[1, 64:128] = 1.0
    c['f2sel'] = bf(f2)
    c['onesrow'] = bf(np.ones((1, 128)))
    c['onescol'] = bf(np.ones((128, 1)))
    c['onescolf'] = np.ones((128, 1), np.float32)
    DH = 64
    half = np.arange(0, DH, 2, dtype=np.float64)
    inv_freq = 1.0 / 10000.0 ** (half / DH)
    t_ = np.arange(T, dtype=np.float64)
    freqs = np.concatenate([t_[:, None] * inv_freq[None, :]] * 2, -1)
    base_scale = (half + 0.4 * DH) / (1.4 * DH)
    power = (t_ - T // 2) / 128.0
    scale = np.concatenate([base_scale[None, :] ** power[:, None]] * 2, -1)
    sgn = np.where(np.arange(DH) < 32, -1.0, 1.0)[:, None]
    c['rotc'] = bf(np.vstack([(np.cos(freqs) * scale).T] * 2))
    c['rots'] = bf(np.vstack([(np.sin(freqs) * scale).T * sgn] * 2))
    c['rotci'] = bf(np.vstack([(np.cos(freqs) / scale).T] * 2))
    c['rotsi'] = bf(np.vstack([(np.sin(freqs) / scale).T * sgn] * 2))
    return c

def build(upto='all', taps=(), race_check=True, n_sub=6):
    nc = bass.Bass('TRN2', target_bir_lowering=False, detect_race_conditions=race_check)
    P = Prog()
    consts = host_constants()
    er = contextlib.ExitStack()

    def din(name, shape, dt=F32):
        return nc.dram_tensor(name, list(shape), dt, kind='ExternalInput').ap()
    X_in = din('x', (T, D))
    OUT = nc.dram_tensor('out', [T, D], F32, kind='ExternalOutput').ap()
    WATT = din('w_attnT', (D, 3 * D))
    WPRJ = din('w_projT', (D, D))
    WFC = din('w_fcT', (D, 4096))
    WPR2 = din('w_prT', (4096, D))
    WQKV1 = din('wqkv1T', (D, 3 * D))
    WO1 = din('wo1T', (D, D))
    BQKV1 = din('bqkv1', (3, 128, 8))
    BO1 = din('bo1', (128, 8))
    NQW = din('nqw', (128, 8))
    NKW = din('nkw', (128, 8))
    WQKV6 = din('wqkv6', (6, D, 3 * D))
    WO6 = din('wo6', (6, D, D))
    QS6 = din('qs6', (6, 128, 1))
    KS6 = din('ks6', (6, 128, 1))
    LNG6 = din('lng6', (6, 128, 8))
    LNB6 = din('lnb6', (6, 128, 8))
    W1A6 = din('w1a6', (6, D, DIP))
    W1G6 = din('w1g6', (6, D, DIP))
    B1A6 = din('b1a6', (6, 128, NK2))
    B1G6 = din('b1g6', (6, 128, NK2))
    W26 = din('w26', (6, DIP, D))
    B26 = din('b26', (6, 128, 8))
    CON = {k: din('c_' + k, v.shape, F32 if v.dtype == np.float32 else BF16) for k, v in consts.items() if k not in ('identf', 'onescolf')}
    XSP = nc.dram_tensor('xspill', [128, NC * T], F32, kind='Internal').ap()
    WFCBF = nc.dram_tensor('wfcbf', [D, 4096], BF16, kind='Internal').ap()
    WPRBF = nc.dram_tensor('wprbf', [4096, D], BF16, kind='Internal').ap()
    WQ6BF = nc.dram_tensor('wq6bf', [6, D, 3 * D], BF16, kind='Internal').ap()
    WO6BF = nc.dram_tensor('wo6bf', [6, D, D], BF16, kind='Internal').ap()
    W1A6BF = nc.dram_tensor('w1a6bf', [6, D, DIP], BF16, kind='Internal').ap()
    W1G6BF = nc.dram_tensor('w1g6bf', [6, D, DIP], BF16, kind='Internal').ap()
    W26BF = nc.dram_tensor('w26bf', [6, DIP, D], BF16, kind='Internal').ap()
    tap_toks = []

    def tap(name, ap, waits):
        if name not in taps:
            return
        tp = nc.dram_tensor('tap_' + name, list(ap.shape), ap.dtype, kind='ExternalOutput').ap()
        P.do('y', lambda e, o=tp, i=ap: e.dma_start(out=o, in_=i), waits, dma=True)
        tap_toks.append(P.tok('y'))
    sbp = lambda n, sh, dt: er.enter_context(nc.sbuf_tensor(n, list(sh), dt))
    xn = sbp('xn', (128, NC * WZ), BF16)
    gb = sbp('gb', (128, NC * T), BF16)
    CB = {k: sbp('cb_' + k, v.shape, F32 if v.dtype == np.float32 else BF16) for k, v in consts.items() if not k.startswith('rot') and k not in ('identf', 'onescolf')}
    ST = [sbp(f'st{i}', (128, 16), F32) for i in range(8)]
    sqw = sbp('sqw', (128, 1024), BF16)
    cpool = sbp('cpool', (128, 8), F32)
    CB['cpool'] = cpool
    for _ci, _cv in ((0, 1e-05), (1, 1e-06), (2, 0.2), (3, -1.0), (4, 1.0 / 1024)):
        P.do('g', lambda e, i=_ci, v=_cv, cpool=cpool: e.memset(cpool[:, i:i + 1], v))
    tok_cpool = P.tok('g')
    ctoks = []
    for k in CON:
        if k.startswith('rot'):
            continue
        P.do('y', lambda e, d=CB[k], s=CON[k]: e.dma_start(out=d[:], in_=s), dma=True)
        ctoks.append(P.tok('y'))
    tok_con = ctoks[-1]
    ident = CB['ident'][:]

    def zc(base, c, t0, t1, shift=False):
        off = 0 if shift else 1
        return base[:, c * WZ + off + t0:c * WZ + off + t1]

    class WBuf:
        def __init__(self, es, name, cols, n_stg=2, n_bf=2, engs=('g', 'v')):
            self.stg = [es.enter_context(nc.sbuf_tensor(_U(name + 'S'), [128, cols], F32))
                        for _ in range(n_stg)]
            self.bf = [es.enter_context(nc.sbuf_tensor(_U(name + 'B'), [128, cols], BF16))
                       for _ in range(n_bf)]
            self.stg_free = [None] * n_stg
            self.cast_tok = [None] * n_bf
            self.busy = [[] for _ in range(n_bf)]
            self.i = 0
            self.j = 0
            self.engs = engs
            self.cols = cols

        def load(self, src2d, extra=(), w=None):
            si = self.j % len(self.stg)
            i = self.i
            self.i = (self.i + 1) % len(self.bf)
            self.j += 1
            w = w if w is not None else self.cols
            tw = dma_ch(self.stg[si][:, 0:w], src2d,
                        [self.stg_free[si]] + list(extra))
            eng = self.engs[i % len(self.engs)]
            P.do(eng, lambda e, d=self.bf[i], s=self.stg[si], w=w:
                 e.tensor_copy(d[:, 0:w], s[:, 0:w]), [tw] + self.busy[i])
            self.cast_tok[i] = P.tok(eng)
            self.stg_free[si] = self.cast_tok[i]
            self.busy[i] = []
            self._last = i
            return self.bf[i], self.cast_tok[i]

        def load_bf(self, src2d_bf, extra=(), w=None):
            i = self.i
            self.i = (self.i + 1) % len(self.bf)
            w = w if w is not None else self.cols
            cc = src2d_bf.shape[0] // 128
            srcv = src2d_bf.rearrange('(c p) n -> p c n', p=128)
            dv = self.bf[i][:, 0:w].rearrange('p (c n) -> p c n', c=cc)
            P.do('y', lambda e, d=dv, s=srcv: e.dma_start(out=d, in_=s),
                 self.busy[i] + list(extra), dma=True)
            self.cast_tok[i] = P.tok('y')
            self.busy[i] = []
            self._last = i
            return self.bf[i], self.cast_tok[i]

        def writeback(self, dst2d_bf, after, w=None):
            i = self._last
            w = w if w is not None else self.cols
            cc = dst2d_bf.shape[0] // 128
            dstv = dst2d_bf.rearrange('(c p) n -> p c n', p=128)
            sv = self.bf[i][:, 0:w].rearrange('p (c n) -> p c n', c=cc)
            P.do('y', lambda e, d=dstv, s=sv: e.dma_start(out=d, in_=s),
                 list(after), dma=True)
            tok = P.tok('y')
            self.busy[i].append(tok)
            return tok

        def consumed(self, tok):
            self.busy[self._last].append(tok)

    def dma_ch(dst, src2d, waits=()):
        cc = src2d.shape[0] // 128
        src = src2d.rearrange('(c p) n -> p c n', p=128)
        dv = dst.rearrange('p (c n) -> p c n', c=cc)
        P.do('y', lambda e, d=dv, s=src: e.dma_start(out=d, in_=s), waits, dma=True)
        return P.tok('y')

    def _ln_combine(ntiles, eps, tv, ts_):
        su, ssq, meanb, varb, rstd, m2s, vr2, mr = [s[:] for s in ST]
        sl = slice(0, ntiles)
        P.do('v', lambda e, varb=varb, sl=sl, ssq=ssq: e.tensor_scalar(varb[:, sl], ssq[:, sl], 1.0 / 1024, None, ALU.mult), [ts_])
        P.do('s', lambda e, meanb=meanb, sl=sl, su=su, cpool=cpool: e.activation(meanb[:, sl], su[:, sl], AF.Identity, scale=cpool[:, 4:5]), [tv, tok_cpool])
        c2 = P.tok('s')
        P.do('v', lambda e, m2s=m2s, sl=sl, meanb=meanb: e.tensor_mul(m2s[:, sl], meanb[:, sl], meanb[:, sl]), [c2])
        c3 = P.tok('v')
        P.do('g', lambda e, vr2=vr2, sl=sl, varb=varb, m2s=m2s: e.tensor_sub(vr2[:, sl], varb[:, sl], m2s[:, sl]), [c3])
        c4 = P.tok('g')
        epcol = 0 if abs(eps - 1e-05) < 1e-12 else 1
        P.do('s', lambda e, ec=epcol, vr2=vr2, sl=sl, cpool=cpool: e.activation(vr2[:, sl], vr2[:, sl], AF.Sqrt, bias=cpool[:, ec:ec + 1]), [c4, tok_cpool])
        c5 = P.tok('s')
        P.do('v', lambda e, rstd=rstd, sl=sl, vr2=vr2: e.reciprocal(rstd[:, sl], vr2[:, sl]), [c5])
        c6 = P.tok('v')
        P.do('g', lambda e, mr=mr, sl=sl, meanb=meanb, rstd=rstd: e.tensor_tensor(mr[:, sl], meanb[:, sl], rstd[:, sl], ALU.mult), [c6, c2])
        c7 = P.tok('g')
        P.do('v', lambda e, m2s=m2s, sl=sl, mr=mr: e.tensor_scalar(m2s[:, sl], mr[:, sl], -1.0, None, ALU.mult), [c7])
        c8 = P.tok('v')
        return (c8, c6)

    def emit_ln_tm(src_tile, ntiles, dst_fn, tag, eps, waits=()):
        su, ssq, meanb, varb, rstd, m2s, vr2, mr = [s[:] for s in ST]
        out_toks = []
        with contextlib.ExitStack() as es:
            tmpb = [es.enter_context(nc.sbuf_tensor(_U(f'{tag}tm{i}'), [128, 1024], BF16)) for i in range(2)]
            bkp = [es.enter_context(nc.psum_tensor(_U(f'{tag}bk{i}'), [128, 1024], BF16)) for i in range(2)]
            tv = ts_ = None
            for a in range(ntiles):
                P.do('v', lambda e, aa=a, su=su, src_tile=src_tile: e.reduce_sum(su[:, aa:aa + 1], src_tile(aa), axis=AX.X), list(waits))
                tv = P.tok('v')
                P.do('s', lambda e, aa=a, sqw=sqw, src_tile=src_tile, ssq=ssq: e.activation(sqw[:], src_tile(aa), AF.Square, accum_out=ssq[:, aa:aa + 1]), list(waits))
                ts_ = P.tok('s')
            c8, c6 = _ln_combine(ntiles, eps, tv, ts_)
            bz = {}
            for a in range(ntiles):
                b = a % 2
                P.do('s', lambda e, aa=a, bb=b, tmpb=tmpb, src_tile=src_tile, m2s=m2s, rstd=rstd: e.activation(tmpb[bb][:], src_tile(aa), AF.Identity, bias=m2s[:, aa:aa + 1], scale=rstd[:, aa:aa + 1]), [c8, c6, bz.get((b, 'e'))])
                ta = P.tok('s')
                for c in range(NC):
                    P.do('t', lambda e, cc=c, bb=b, bkp=bkp, tmpb=tmpb, ident=ident: e.transpose(bkp[bb][:, cc * 128:(cc + 1) * 128], tmpb[bb][:, cc * 128:(cc + 1) * 128], ident), [ta, tok_con, bz.get((b, 'e'))] if c == 0 else [])
                tb = P.tok('t')
                P.do('v', lambda e, d=dst_fn(a), bb=b, bkp=bkp: e.tensor_copy(d, bkp[bb][:].rearrange('p (c w) -> p c w', c=NC)), [tb])
                te = P.tok('v')
                bz[b, 'e'] = te
                out_toks.append(te)
        return out_toks

    def emit_ln_fm(src_col, ntiles, dst_fn, tag, eps, waits=()):
        su, ssq, meanb, varb, rstd, m2s, vr2, mr = [s[:] for s in ST]
        out_toks = []
        with contextlib.ExitStack() as es:
            cst = [es.enter_context(nc.sbuf_tensor(_U(f'{tag}cs{i}'), [128, 1024], BF16)) for i in range(2)]
            trp = [es.enter_context(nc.psum_tensor(_U(f'{tag}tr{i}'), [128, 1024], BF16)) for i in range(2)]
            tmpb = [es.enter_context(nc.sbuf_tensor(_U(f'{tag}tm{i}'), [128, 1024], BF16)) for i in range(2)]
            bkp = [es.enter_context(nc.psum_tensor(_U(f'{tag}bk{i}'), [128, 1024], BF16)) for i in range(2)]
            bz = {}
            tv = ts_ = None

            def cast_and_transpose(a, b, w0):
                for c in range(NC):
                    P.do('v', lambda e, aa=a, cc=c, bb=b, cst=cst, src_col=src_col: e.tensor_copy(cst[bb][:, cc * 128:(cc + 1) * 128], src_col(aa, cc)), w0 if c == 0 else [])
                tcc = P.tok('v')
                for c in range(NC):
                    P.do('t', lambda e, cc=c, bb=b, trp=trp, cst=cst, ident=ident: e.transpose(trp[bb][:, cc * 128:(cc + 1) * 128], cst[bb][:, cc * 128:(cc + 1) * 128], ident), [tcc, tok_con] + w0 if c == 0 else [])
                return P.tok('t')
            for a in range(ntiles):
                b = a % 2
                w0 = list(waits) + [bz.get((b, 'v')), bz.get((b, 's'))]
                tt = cast_and_transpose(a, b, w0)
                P.do('v', lambda e, aa=a, bb=b, su=su, trp=trp: e.reduce_sum(su[:, aa:aa + 1], trp[bb][:], axis=AX.X), [tt])
                tv = P.tok('v')
                P.do('s', lambda e, bb=b, aa=a, sqw=sqw, trp=trp, ssq=ssq: e.activation(sqw[:], trp[bb][:], AF.Square, accum_out=ssq[:, aa:aa + 1]), [tt, tv])
                ts_ = P.tok('s')
                bz[b, 'v'] = tv
                bz[b, 's'] = ts_
            c8, c6 = _ln_combine(ntiles, eps, tv, ts_)
            bz2 = {}
            for a in range(ntiles):
                b = a % 2
                w2 = [c8, c6, bz.get((b, 'v')), bz.get((b, 's')), bz2.get((b, 'a'))]
                tt = cast_and_transpose(a, b, w2)
                P.do('s', lambda e, aa=a, bb=b, tmpb=tmpb, trp=trp, m2s=m2s, rstd=rstd: e.activation(tmpb[bb][:], trp[bb][:], AF.Identity, bias=m2s[:, aa:aa + 1], scale=rstd[:, aa:aa + 1]), [tt])
                ta = P.tok('s')
                bz[b, 'v'] = None
                bz[b, 's'] = ta
                for c in range(NC):
                    P.do('t', lambda e, cc=c, bb=b, bkp=bkp, tmpb=tmpb, ident=ident: e.transpose(bkp[bb][:, cc * 128:(cc + 1) * 128], tmpb[bb][:, cc * 128:(cc + 1) * 128], ident), [ta, bz2.get((b, 'e'))] if c == 0 else [])
                tb = P.tok('t')
                P.do('v', lambda e, d=dst_fn(a), bb=b, bkp=bkp: e.tensor_copy(d, bkp[bb][:].rearrange('p (c w) -> p c w', c=NC)), [tb])
                te = P.tok('v')
                bz2[b, 'a'] = ta
                bz2[b, 'e'] = te
                out_toks.append(te)
        return out_toks

    def attn_head(tag, hs, q_ap, k_ap, vtm_ap, p_sb, pt_sb, nmx, sums, rcs, ao_write, qt_list, q0_of, keyspec, gates, alibi_ap=None, alibi_base=None, mask_fn=None):
        he = hs + 64
        tp = (0, 0) if hs == 0 else (64, 0)
        max_klen = max(keyspec(q)[1] for q in qt_list)
        CW = 512 if max_klen <= 512 else 1024
        with contextlib.ExitStack() as eA:
            sps = [eA.enter_context(nc.psum_tensor(_U(f'{tag}sp{i}'), [128, CW], F32)) for i in range(2)]
            ptp = [eA.enter_context(nc.psum_tensor(_U(f'{tag}pp{i}'), [128, CW], BF16)) for i in range(2)]
            avp = [eA.enter_context(nc.psum_tensor(_U(f'{tag}av{i}'), [128, 128], F32)) for i in range(2)]
            nm2 = eA.enter_context(nc.sbuf_tensor(_U(tag + 'nm'), [128, 8], F32))
            sm2 = eA.enter_context(nc.sbuf_tensor(_U(tag + 'sm'), [128, 8], F32))
            sps_busy = [gates.get('prev_exp')] * 2
            ptp_busy = [gates.get('prev_ptev')] * 2
            avp_busy = [gates.get('prev_ao')] * 2
            nm_busy = [None] * 2
            tmp_busy = [None] * 2
            sm_busy = [None] * 2
            tpt_last = gates.get('prev_tpt')
            av_last = gates.get('prev_tav')
            first = True
            pend = None

            def stage_b2(stq):
                nonlocal tpt_last, av_last
                qt, kb, klen, chs = stq['qt'], stq['kb'], stq['klen'], stq['chs']
                qp = qt % 2
                ptevs = []
                for ci, ch in enumerate(chs):
                    c0, clen, _, _ = ch
                    pp = (qt + ci) % 2
                    for kt in range(clen // 128):
                        P.do('t', lambda e, kl=kt * 128, cc0=c0, pp=pp, ptp=ptp, p_sb=p_sb, ident=ident: e.transpose(ptp[pp][:, kl:kl + 128], p_sb[:, cc0 + kl:cc0 + kl + 128], ident), [stq['norm'][ci], ptp_busy[pp], tok_con] if kt == 0 else [])
                    tpt_last = P.tok('t')
                    P.do('v', lambda e, cc0=c0, cl=clen, pp=pp, pt_sb=pt_sb, ptp=ptp: e.tensor_copy(pt_sb[:, cc0:cc0 + cl], ptp[pp][:, 0:cl]), [tpt_last, av_last])
                    ptp_busy[pp] = P.tok('v')
                    ptevs.append(ptp_busy[pp])
                gates['prev_ptev'] = ptevs[-1]
                nkt = klen // 128
                for ci, ch in enumerate(chs):
                    c0, clen, _, _ = ch
                    for kl in range(clen // 128):
                        kt = c0 // 128 + kl
                        w_ = ([ptevs[ci], avp_busy[qp]] + list(gates.get('vtm', []))) if kl == 0 else []
                        P.do('t', lambda e, k0=kt * 128, kb0=kb, first_=kt == 0, last=kt == nkt - 1, qp=qp, avp=avp, hs=hs, he=he, vtm_ap=vtm_ap, pt_sb=pt_sb: e.matmul(avp[qp][hs:he, :], vtm_ap[:, kb0 + k0 + hs:kb0 + k0 + hs + 64], pt_sb[:, k0:k0 + 128], start=first_, stop=last, tile_position=(0, hs)), w_)
                av_last = P.tok('t')
                gates['prev_tav'] = av_last
                avp_busy[qp] = ao_write(qt, avp[qp], [av_last])
                gates['prev_ao'] = avp_busy[qp]
            for qt in qt_list:
                kb, klen = keyspec(qt)
                q0 = q0_of(qt)
                nch = (klen + CW - 1) // CW
                p2 = 2 * (qt % 2)
                chs = []
                for c in range(nch):
                    c0 = c * CW
                    clen = min(CW, klen - c0)
                    par = (qt + c) % 2
                    sp_ = sps[par]
                    w0 = [sps_busy[par]]
                    if first:
                        w0 += list(gates.get('qkv', []))
                    for ks in range((clen + 511) // 512):
                        n0 = ks * 512
                        n1 = min(clen, n0 + 512)
                        P.do('t', lambda e, a0=n0, a1=n1, g0=c0, qq=q0, kb0=kb, sp=mask_fn is None, sp_=sp_, q_ap=q_ap, hs=hs, he=he, k_ap=k_ap, tp=tp: e.matmul(sp_[:, a0:a1], q_ap[hs:he, qq:qq + 128], k_ap[hs:he, kb0 + g0 + a0:kb0 + g0 + a1], start=True, stop=sp, tile_position=tp), w0 if ks == 0 else [])
                    if mask_fn is not None:
                        mask_fn(qt, sp_)
                    tsc = P.tok('t')
                    first = False
                    mcol = nm2[:, p2 + c:p2 + c + 1]
                    wmn = [tsc] + list(nm_busy[qt % 2] or [])
                    if alibi_ap is not None:
                        b0 = alibi_base(qt)
                        P.do('v', lambda e, cl=clen, bb=b0 + c0, sp_=sp_, alibi_ap=alibi_ap: e.scalar_tensor_tensor(sp_[:, 0:cl], sp_[:, 0:cl], 1.0, alibi_ap[:, bb:bb + cl], ALU.mult, ALU.add), wmn + list(gates.get('alibi', [])))
                        P.do('v', lambda e, cl=clen, mc=mcol, sp_=sp_: e.tensor_scalar(sp_[:, 0:cl], sp_[:, 0:cl], 1.0, -3e+38, ALU.mult, ALU.max, accum_out=mc))
                    else:
                        P.do('v', lambda e, cl=clen, mc=mcol, sp_=sp_: e.tensor_scalar(sp_[:, 0:cl], sp_[:, 0:cl], 1.0, -3e+38, ALU.mult, ALU.max, accum_out=mc), wmn)
                    chs.append((c0, clen, par, P.tok('v')))
                if nch > 1:
                    tg2 = []
                    for c in range(2):
                        P.do('g', lambda e, pp2=p2, cc=c, nm2=nm2: e.tensor_scalar(nm2[:, 4 + pp2 + cc:5 + pp2 + cc], nm2[:, pp2 + cc:pp2 + cc + 1], -1.0, None, ALU.mult), [chs[c][3], tmp_busy[qt % 2]])
                        tg2.append(P.tok('g'))
                    P.do('v', lambda e, col=qt, pp2=p2, nmx=nmx, nm2=nm2: e.tensor_tensor(nmx[:, col:col + 1], nm2[:, 4 + pp2:5 + pp2], nm2[:, 5 + pp2:6 + pp2], ALU.min), tg2)
                    nm_busy[qt % 2] = tg2
                    tmp_busy[qt % 2] = P.tok('v')
                    tneg = tmp_busy[qt % 2]
                else:
                    P.do('g', lambda e, col=qt, pp2=p2, nmx=nmx, nm2=nm2: e.tensor_scalar(nmx[:, col:col + 1], nm2[:, pp2:pp2 + 1], -1.0, None, ALU.mult), [chs[0][3]])
                    nm_busy[qt % 2] = [P.tok('g')]
                    tneg = nm_busy[qt % 2][0]
                if pend is not None:
                    stage_b2(pend)
                exps = []
                norm = []
                for ci, ch in enumerate(chs):
                    c0, clen, par, mtok = ch
                    scol = sums[:, qt:qt + 1] if nch == 1 else sm2[:, p2 + ci:p2 + ci + 1]
                    we = [mtok, tneg, tpt_last, sm_busy[qt % 2] if nch > 1 else None]
                    P.do('s', lambda e, cc0=c0, cl=clen, col=qt, sc=scol, pp=par, p_sb=p_sb, sps=sps, nmx=nmx: e.activation(p_sb[:, cc0:cc0 + cl], sps[pp][:, 0:cl], AF.Exp, bias=nmx[:, col:col + 1], accum_out=sc), we)
                    sps_busy[par] = P.tok('s')
                    exps.append(sps_busy[par])
                gates['prev_exp'] = exps[-1]
                if nch > 1:
                    P.do('g', lambda e, col=qt, pp2=p2, sums=sums, sm2=sm2: e.tensor_tensor(sums[:, col:col + 1], sm2[:, pp2:pp2 + 1], sm2[:, pp2 + 1:pp2 + 2], ALU.add), exps)
                    sm_busy[qt % 2] = P.tok('g')
                    trc0 = sm_busy[qt % 2]
                else:
                    trc0 = exps[-1]
                P.do('v', lambda e, col=qt, rcs=rcs, sums=sums: e.reciprocal(rcs[:, col:col + 1], sums[:, col:col + 1]), [trc0])
                trc = P.tok('v')
                for ci, ch in enumerate(chs):
                    c0, clen, par, _ = ch
                    P.do('s', lambda e, cc0=c0, cl=clen, col=qt, p_sb=p_sb, rcs=rcs: e.activation(p_sb[:, cc0:cc0 + cl], p_sb[:, cc0:cc0 + cl], AF.Identity, scale=rcs[:, col:col + 1]), [trc])
                    norm.append(P.tok('s'))
                pend = dict(qt=qt, kb=kb, klen=klen, chs=chs, norm=norm)
            stage_b2(pend)
            gates['prev_tpt'] = tpt_last
    fin = []
    with contextlib.ExitStack() as EA:
        x_sb = EA.enter_context(nc.sbuf_tensor(_U('x0'), [128, NT * 1024], F32))
        P.do('y', lambda e, x_sb=x_sb, X_in=X_in: e.dma_start(out=x_sb[:].rearrange('p (a d) -> p a d', a=NT), in_=X_in.rearrange('(a p) d -> p a d', p=128)), dma=True)
        tok_x = P.tok('y')
        P.do('g', lambda e, xn=xn: e.memset(xn[:].rearrange('p (c w) -> p c w', c=NC)[:, :, 0:1], 0.0))
        tok_xnz = P.tok('g')
        xtile = lambda a: x_sb[:, a * 1024:(a + 1) * 1024]
        xn_dst = lambda a: xn[:].rearrange('p (c w) -> p c w', c=NC)[:, :, 1 + a * 128:1 + (a + 1) * 128]
        ln0 = emit_ln_tm(xtile, NT, xn_dst, 'l0', EPS0, waits=[tok_x])
        xn_gate = ln0 + [tok_xnz]
        tap('xn0', xn[:], xn_gate)
        P.barrier()
        tok_g0 = None
        if upto not in ('ln',):
            with contextlib.ExitStack() as ef:
                wb = WBuf(ef, 'fw', 4096, n_stg=1, n_bf=2)
                h_bf = ef.enter_context(nc.sbuf_tensor(_U('fh'), [128, 32 * 512], BF16))
                lanes = []
                for ln_ in range(2):
                    lanes.append(dict(
                        yt=ef.enter_context(nc.sbuf_tensor(_U(f'fy{ln_}'), [128, 256], F32)),
                        sp=ef.enter_context(nc.sbuf_tensor(_U(f'fs{ln_}'), [128, 256], F32)),
                        m0=ef.enter_context(nc.sbuf_tensor(_U(f'fm{ln_}a'), [128, 256], F32)),
                        fa=ef.enter_context(nc.sbuf_tensor(_U(f'fa{ln_}'), [128, 256], F32)),
                        free=None))
                t_wfree = xn_gate[-1]
                t_bffree = None
                t_ycopy = None
                t_spk = None
                for q in range(4):
                    tq0 = q * 512
                    t_hev = {}
                    t_hfree = t_ycopy if t_ycopy is not None else t_bffree
                    eh = contextlib.ExitStack()
                    hps = [eh.enter_context(nc.psum_tensor(_U(f'fhp{q}_{i}'), [128, 512], F32)) for i in range(2)]
                    for fs in range(8):
                        if q == 0:
                            wbf, tc = wb.load(WFC[:, fs * 512:(fs + 1) * 512],
                                              [t_wfree] if fs == 0 else ())
                            wb.writeback(WFCBF[:, fs * 512:(fs + 1) * 512], [tc])
                        else:
                            wbf, tc = wb.load_bf(WFCBF[:, fs * 512:(fs + 1) * 512])
                        t_wfree = None
                        last_mm = None
                        for m in range(4):
                            fc = fs * 4 + m
                            b = fc % 2
                            w0 = [tc, t_hev.get(b), t_hfree] + xn_gate
                            for k in range(NC):
                                P.do('t', lambda e, kk=k, mm2=m, bb=b, t0=tq0, hps=hps, wbf=wbf, zc=zc, xn=xn: e.matmul(hps[bb][:], wbf[:, kk * 512 + mm2 * 128:kk * 512 + (mm2 + 1) * 128], zc(xn, kk, t0, t0 + 512), start=kk == 0, stop=kk == NC - 1), w0 if k == 0 else [])
                            last_mm = P.tok('t')
                            wb.consumed(last_mm)
                            P.do('s', lambda e, f=fc, bb=b, h_bf=h_bf, hps=hps: e.activation(h_bf[:, f * 512:(f + 1) * 512], hps[bb][:], AF.Relu), [last_mm])
                            tr = P.tok('s')
                            P.do('v', lambda e, f=fc, h_bf=h_bf: e.tensor_mul(h_bf[:, f * 512:(f + 1) * 512], h_bf[:, f * 512:(f + 1) * 512], h_bf[:, f * 512:(f + 1) * 512]), [tr])
                            t_hev[b] = P.tok('v')
                        t_bffree = last_mm
                    th_all = [t_hev.get(0), t_hev.get(1)]
                    eh.close()
                    with contextlib.ExitStack() as ey:
                        yps = [ey.enter_context(nc.psum_tensor(_U(f'fyp{q}_{i}'), [128, 512], F32)) for i in range(8)]
                        t_ymm = {}
                        for fs in range(8):
                            if q == 0:
                                wbf, tc = wb.load(WPR2[fs * 512:(fs + 1) * 512, :])
                                wb.writeback(WPRBF[fs * 512:(fs + 1) * 512, :], [tc])
                            else:
                                wbf, tc = wb.load_bf(WPRBF[fs * 512:(fs + 1) * 512, :])
                            last_mm = None
                            for of in range(8):
                                w0 = [tc] + (th_all if fs == 0 else [])
                                for m in range(4):
                                    fc = fs * 4 + m
                                    P.do('t', lambda e, oo=of, mm2=m, f=fc, yps=yps, wbf=wbf, h_bf=h_bf: e.matmul(yps[oo][:], wbf[:, mm2 * 1024 + oo * 128:mm2 * 1024 + (oo + 1) * 128], h_bf[:, f * 512:(f + 1) * 512], start=f == 0, stop=f == 31), w0 if m == 0 else [])
                                last_mm = P.tok('t')
                                t_ymm[of] = last_mm
                            wb.consumed(last_mm)
                            t_bffree = last_mm
                        for of in range(8):
                            for hf in range(2):
                                ln_ = (of * 2 + hf) % 2
                                eng = ('g', 'v')[ln_]
                                L = lanes[ln_]
                                yt, spk, facc = (L['yt'], L['sp'], L['fa'])
                                P.do('s', lambda e, oo=of, h0=hf * 256, y_t=yt, yps=yps: e.copy(y_t[:], yps[oo][:, h0:h0 + 256]), [t_ymm[of], L['free']])
                                ty = P.tok('s')
                                P.do(eng, lambda e, spk=spk, y_t=yt: e.tensor_scalar(spk[:], y_t[:], 1.0, None, ALU.is_gt), [ty])
                                P.do(eng, lambda e, facc=facc, spk=spk: e.tensor_copy(facc[:], spk[:]))
                                for si in range(4):
                                    a_ = yt if si == 0 else L['m0']
                                    b_ = L['m0']
                                    P.do(eng, lambda e, aa=a_, bb=b_: e.tensor_scalar(bb[:], aa[:], 0.9, None, ALU.mult))
                                    P.do(eng, lambda e, bb=b_, y_t=yt: e.tensor_add(bb[:], bb[:], y_t[:]))
                                    P.do(eng, lambda e, bb=b_, spk=spk: e.tensor_sub(bb[:], bb[:], spk[:]))
                                    P.do(eng, lambda e, bb=b_, spk=spk: e.tensor_scalar(spk[:], bb[:], 1.0, None, ALU.is_gt))
                                    P.do(eng, lambda e, facc=facc, spk=spk: e.tensor_add(facc[:], facc[:], spk[:]))
                                tg = P.tok(eng)
                                P.do('s', lambda e, oo=of, t0=tq0 + hf * 256, gb=gb, facc=facc, cpool=cpool: e.activation(gb[:, oo * T + t0:oo * T + t0 + 256], facc[:], AF.Silu, scale=cpool[:, 2:3]), [tg, tok_cpool])
                                L['free'] = P.tok('s')
                                t_spk = L['free']
                                t_ycopy = ty
                tok_g0 = t_spk
            tap('g0', gb[:], [tok_g0])
        P.barrier()
        if upto not in ('ln', 'ffn0'):
            slopes = alibi_slopes(16)
            with contextlib.ExitStack() as ea:
                ao = ea.enter_context(nc.sbuf_tensor(_U('ao'), [128, 4 * T], BF16))
                nmx = ea.enter_context(nc.sbuf_tensor(_U('anm'), [128, 16], F32))
                sums = ea.enter_context(nc.sbuf_tensor(_U('asm'), [128, 16], F32))
                rcs = ea.enter_context(nc.sbuf_tensor(_U('arc'), [128, 16], F32))
                t_xadd = {}
                tok_g_pg = []
                for pg in range(2):
                    P.barrier()
                    tok_ao_pg = []
                    with contextlib.ExitStack() as ep:
                        relm = ep.enter_context(nc.sbuf_tensor(_U('arel'), [128, 2688], BF16))
                        am = ep.enter_context(nc.sbuf_tensor(_U('aam'), [128, 2688], BF16))
                        wba = WBuf(ep, 'aw', 1024)
                        q_sb = ep.enter_context(nc.sbuf_tensor(_U('aq'), [128, T], BF16))
                        k_sb = ep.enter_context(nc.sbuf_tensor(_U('ak'), [128, T], BF16))
                        v_sb = ep.enter_context(nc.sbuf_tensor(_U('av_'), [128, T], BF16))
                        vtm = ep.enter_context(nc.sbuf_tensor(_U('avt'), [128, T], BF16))
                        pt_sb = ep.enter_context(nc.sbuf_tensor(_U('apt'), [128, T], BF16))
                        P.do('g', lambda e, relm=relm: e.iota(relm[:], [[1, 2688]], base=-2176, channel_multiplier=-1, allow_small_or_imprecise_dtypes=True), [tok_g0] if tok_g0 else [])
                        P.do('g', lambda e, am=am, relm=relm: e.tensor_scalar(am[:], relm[:], 0.0, None, ALU.is_gt))
                        P.do('g', lambda e, am=am: e.tensor_scalar(am[:], am[:], NEG, None, ALU.mult))
                        P.do('g', lambda e, relm=relm, am=am: e.tensor_add(relm[:], relm[:], am[:]))
                        tok_rel = P.tok('g')
                        t_wfree = tok_g0
                        t_qkvfree = None
                        gates = {}
                        for pi in range(4):
                            pr = pg * 4 + pi
                            p_sb = v_sb
                            tjs = []
                            last_wmm = t_qkvfree
                            for j, off in enumerate((pr * 128, D + pr * 128, 2 * D + pr * 128)):
                                wbf, tc = wba.load(WATT[:, off:off + 128],
                                                   [t_wfree] if t_wfree else ())
                                t_wfree = None
                                dst, scl = ((q_sb, 0.125), (k_sb, None), (v_sb, None))[j]
                                with contextlib.ExitStack() as eq:
                                    qps = [eq.enter_context(nc.psum_tensor(_U(f'aqp{pr}_{j}_{i}'), [128, 512], F32)) for i in range(2)]
                                    tev = {}
                                    for sp in range(4):
                                        b = sp % 2
                                        w0 = [tc, tev.get(b)] + xn_gate + ([gates.get('prev_exp'), gates.get('prev_tav'), gates.get('prev_ao'), gates.get('prev_ptev')] if j == 0 else [])
                                        for k in range(NC):
                                            P.do('t', lambda e, kk=k, ss=sp, bb=b, qps=qps, wbf=wbf, zc=zc, xn=xn: e.matmul(qps[bb][:], wbf[:, kk * 128:(kk + 1) * 128], zc(xn, kk, ss * 512, (ss + 1) * 512), start=kk == 0, stop=kk == NC - 1), w0 if k == 0 else [])
                                        tm = P.tok('t')
                                        last_wmm = tm
                                        wba.consumed(tm)
                                        if scl is not None:
                                            P.do('v', lambda e, d=dst, ss=sp, bb=b, s_=scl, qps=qps: e.tensor_scalar(d[:, ss * 512:(ss + 1) * 512], qps[bb][:], s_, None, ALU.mult), [tm])
                                            tev[b] = P.tok('v')
                                        else:
                                            P.do('s', lambda e, d=dst, ss=sp, bb=b, qps=qps: e.copy(d[:, ss * 512:(ss + 1) * 512], qps[bb][:]), [tm])
                                            tev[b] = P.tok('s')
                                    tjs += [tev.get(0), tev.get(1)]
                            t_qkvfree = last_wmm
                            with contextlib.ExitStack() as ev:
                                vtp = ev.enter_context(nc.psum_tensor(_U(f'avp{pr}'), [128, 2048], BF16))
                                for a in range(NT):
                                    P.do('t', lambda e, aa=a, vtp=vtp, v_sb=v_sb, ident=ident: e.transpose(vtp[:, aa * 128:(aa + 1) * 128], v_sb[:, aa * 128:(aa + 1) * 128], ident), tjs + [gates.get('prev_tav'), gates.get('prev_ao'), gates.get('prev_ptev')] if a == 0 else [])
                                tvt = P.tok('t')
                                P.do('v', lambda e, vtm=vtm, vtp=vtp: e.tensor_copy(vtm[:], vtp[:]), [tvt])
                                tok_vtm = P.tok('v')
                            if pr == 0:
                                tap('q0', q_sb[:], tjs)
                                tap('k0', k_sb[:], tjs)
                                tap('vtm0', vtm[:], [tok_vtm])
                            for hh in range(2):
                                head = pr * 2 + hh
                                P.do('g', lambda e, sl=slopes[head], am=am, relm=relm: e.tensor_scalar(am[:], relm[:], sl, None, ALU.mult), [tok_rel, gates.get('prev_exp')])
                                tok_am = P.tok('g')
                                gates['qkv'] = tjs + [tok_vtm]
                                gates['vtm'] = [tok_vtm]
                                gates['alibi'] = [tok_am]

                                def ao_w(qt, avp, w, pi=pi, hh=hh):
                                    P.do('v', lambda e, col=qt, h0=hh * 64, ao=ao, pi=pi, avp=avp: e.tensor_copy(ao[h0:h0 + 64, pi * T + col * 128:pi * T + (col + 1) * 128], avp[h0:h0 + 64, :]), w)
                                    tok_ao_pg.append(P.tok('v'))
                                    return tok_ao_pg[-1]
                                attn_head(f'a{pr}h{hh}', hh * 64, q_sb, k_sb, vtm, p_sb, pt_sb, nmx, sums, rcs, ao_w, list(range(NT)), lambda qt: qt * 128, lambda qt: (0, (qt + 1) * 128), gates, alibi_ap=am[:], alibi_base=lambda qt: 2176 - qt * 128)
                    tap(f'ao{pg}', ao[:], tok_ao_pg[-1:])
                    P.barrier()
                    with contextlib.ExitStack() as ew:
                        wstg2 = ew.enter_context(nc.sbuf_tensor(_U('pw'), [128, 4096], F32))
                        wbf2 = ew.enter_context(nc.sbuf_tensor(_U('pwb'), [128, 4096], BF16))
                        tmp = [ew.enter_context(nc.sbuf_tensor(_U(f'pt{i}'), [128, 512], BF16)) for i in range(2)]
                        pps = [ew.enter_context(nc.psum_tensor(_U(f'pp{i}'), [128, 512], F32)) for i in range(2)]
                        ptp = [ew.enter_context(nc.psum_tensor(_U(f'ptr{i}'), [128, 512], BF16)) for i in range(2)]
                        tev = {}
                        ttr = {}
                        cnt = 0
                        tw = dma_ch(wstg2[:], WPRJ[pg * 512:(pg + 1) * 512, :], tok_ao_pg[-1:])
                        P.do('g', lambda e, wbf2=wbf2, wstg2=wstg2: e.tensor_copy(wbf2[:], wstg2[:]), [tw])
                        tc = P.tok('g')
                        for of in range(8):
                            for sp in range(4):
                                b = cnt % 2
                                cnt += 1
                                w0 = [tc, tev.get(b)] + tok_ao_pg[-1:]
                                for k in range(4):
                                    P.do('t', lambda e, kk=k, oo=of, ss=sp, bb=b, pps=pps, wbf2=wbf2, ao=ao: e.matmul(pps[bb][:], wbf2[:, kk * 1024 + oo * 128:kk * 1024 + (oo + 1) * 128], ao[:, kk * T + ss * 512:kk * T + (ss + 1) * 512], start=kk == 0, stop=kk == 3), w0 if k == 0 else [])
                                tm = P.tok('t')
                                P.do('v', lambda e, bb=b, oo=of, ss=sp, tmp=tmp, pps=pps, gb=gb: e.tensor_tensor(tmp[bb][:], pps[bb][:], gb[:, oo * T + ss * 512:oo * T + (ss + 1) * 512], ALU.mult), [tm, tok_g0, ttr.get(bb if False else b)])
                                tv = P.tok('v')
                                tev[b] = tv
                                for j in range(4):
                                    P.do('t', lambda e, bb=b, jj=j, ptp=ptp, tmp=tmp, ident=ident: e.transpose(ptp[bb][:, jj * 128:(jj + 1) * 128], tmp[bb][:, jj * 128:(jj + 1) * 128], ident), [tv, t_xadd.get(b)] if j == 0 else [])
                                tt2 = P.tok('t')
                                ttr[b] = tt2
                                dstx = x_sb[:].rearrange('p (a d) -> p a d', a=NT)[:, sp * 4:sp * 4 + 4, of * 128:(of + 1) * 128]
                                P.do('v', lambda e, d=dstx, bb=b, ptp=ptp: e.tensor_tensor(d, d, ptp[bb][:].rearrange('p (j w) -> p j w', j=4), ALU.add), [tt2, tok_x])
                                t_xadd[b] = P.tok('v')
                    tok_g_pg = [t_xadd.get(0), t_xadd.get(1)]
            fin = [t for t in tok_g_pg if t]
            tap('xb0', x_sb[:], fin)
        if upto in ('ln', 'ffn0', 'b0'):
            if not fin:
                fin = xn_gate + ([tok_g0] if tok_g0 else [])
            P.do('y', lambda e, OUT=OUT, x_sb=x_sb: e.dma_start(out=OUT.rearrange('(a p) d -> p a d', p=128), in_=x_sb[:].rearrange('p (a d) -> p a d', a=NT)), fin + [tok_x], dma=True)
            tout = P.tok('y')
            P.do('g', lambda e: e.nop(), [tout] + tap_toks)
            P.do('y', lambda e: e.nop(), [tout] + tap_toks)
            return finalize(nc, P, er)
        P.barrier()
        ln1 = emit_ln_tm(xtile, NT, xn_dst, 'l1', EPS0, waits=fin)
        xn_gate = ln1
        tap('xn1', xn[:], ln1)
        P.do('y', lambda e, XSP=XSP, x_sb=x_sb: e.dma_start(out=XSP, in_=x_sb[:]), fin + ln1, dma=True)
        tok_spill = P.tok('y')
        fin = ln1 + [tok_spill]
    fin = emit_l1_ffn(nc, P, xn, gb, CB, ST, None, CON, tok_con, WQKV6, WO6, QS6, KS6, LNG6, LNB6, W1A6, W1G6, B1A6, B1G6, W26, B26, fin, tap, attn_head, emit_ln_fm, zc, dma_ch, upto, n_sub, WBuf, (WQ6BF, WO6BF, W1A6BF, W1G6BF, W26BF))
    if upto == 'ffn1':
        P.do('y', lambda e, OUT=OUT, XSP=XSP: e.dma_start(out=OUT.rearrange('t d -> (t d)'), in_=XSP.rearrange('p n -> (p n)')), fin, dma=True)
        fin = [P.tok('y')]
        P.do('g', lambda e: e.nop(), fin + tap_toks)
        P.do('y', lambda e: e.nop(), fin + tap_toks)
        return finalize(nc, P, er)
    emit_l1_attn(nc, P, xn, gb, CB, ST, None, tok_con, WQKV1, WO1, BQKV1, BO1, NQW, NKW, XSP, OUT, fin, tap, tap_toks, attn_head, zc, dma_ch, WBuf)
    return finalize(nc, P, er)

def finalize(nc, P, er=None):
    sems = {}
    with contextlib.ExitStack() as esem:
        for k in ('v', 's', 't', 'g', 'y', 'vd', 'sd', 'td', 'gd'):
            sems[k] = esem.enter_context(nc.semaphore('sem_' + k))
        with nc.Block() as block:
            P.emit(nc, block, sems)
    return nc

def emit_l1_ffn(nc, P, xn, gb, CB, ST, sqs, CON, tok_con, WQKV6, WO6, QS6, KS6, LNG6, LNB6, W1A6, W1G6, B1A6, B1G6, W26, B26, fin, tap, attn_head, emit_ln, zc, dma_ch, upto, n_sub, WBuf=None, WCACHE=None):
    ident = CB['ident'][:]
    with contextlib.ExitStack() as EZ:
        z_sb = EZ.enter_context(nc.sbuf_tensor(_U('z'), [128, NC * WZ], F32))
        nmx = EZ.enter_context(nc.sbuf_tensor(_U('znm'), [128, 16], F32))
        sums = EZ.enter_context(nc.sbuf_tensor(_U('zsm'), [128, 16], F32))
        rcs = EZ.enter_context(nc.sbuf_tensor(_U('zrc'), [128, 16], F32))
        lnbb = EZ.enter_context(nc.sbuf_tensor(_U('lnbb'), [128, 8], BF16))
        beffa = EZ.enter_context(nc.sbuf_tensor(_U('beffa'), [128, NK2], F32))
        beffg = EZ.enter_context(nc.sbuf_tensor(_U('beffg'), [128, NK2], F32))
        P.do('g', lambda e, z_sb=z_sb: e.memset(z_sb[:].rearrange('p (c w) -> p c w', c=NC)[:, :, 0:1], 0.0), fin)
        P.do('v', lambda e, z_sb=z_sb, xn=xn: e.tensor_copy(z_sb[:].rearrange('p (c w) -> p c w', c=NC)[:, :, 1:WZ], xn[:].rearrange('p (c w) -> p c w', c=NC)[:, :, 1:WZ]), fin)
        tz = P.tok('v')
        tzg = P.tok('g')
        z_gate = [tz, tzg]
        WQ6BF, WO6BF, W1A6BF, W1G6BF, W26BF = WCACHE
        pstg = [EZ.enter_context(nc.sbuf_tensor(_U(f'pws{i}'), [128, 1024], F32)) for i in range(2)]
        pbf0 = EZ.enter_context(nc.sbuf_tensor(_U('pwb0'), [128, 1024], BF16))
        pbf = [pbf0, pbf0]
        plg = EZ.enter_context(nc.sbuf_tensor(_U('plg'), [128, 8], F32))
        pst = {'f': [None, None], 'b': [None, None], 'lg': None, 'i': 0}

        def prep_l1f(li):
            def slab(src2d, dst2d, scale_col, extra=()):
                i = pst['i'] % 2
                pst['i'] += 1
                n = src2d.shape[1]
                P.do('g', lambda e, d=pstg[i][:, 0:n], s=src2d: e.dma_start(out=d, in_=s), [pst['f'][i]] + list(extra), dma=True)
                tin = P.tok('g')
                if scale_col is not None:
                    P.do('g', lambda e, ii=i, n=n, sc=scale_col, pbf=pbf, pstg=pstg: e.tensor_scalar(pbf[ii][:, 0:n], pstg[ii][:, 0:n], 1.0, sc, ALU.mult, ALU.mult), [tin, pst['b'][0], pst['b'][1]])
                else:
                    P.do('g', lambda e, ii=i, n=n, pbf=pbf, pstg=pstg: e.tensor_copy(pbf[ii][:, 0:n], pstg[ii][:, 0:n]), [tin, pst['b'][0], pst['b'][1]])
                tcst = P.tok('g')
                pst['f'][i] = tcst
                P.do('g', lambda e, d=dst2d, s=pbf[i][:, 0:n]: e.dma_start(out=d, in_=s), [tcst], dma=True)
                pst['b'][i] = P.tok('g')
                return tcst
            out = {}
            for r in range(0, D, 128):
                for c0 in range(0, 3 * D, 1024):
                    slab(WQKV6[li][r:r + 128, c0:c0 + 1024], WQ6BF[li][r:r + 128, c0:c0 + 1024], None)
            out['wq'] = pst['b'][(pst['i'] + 1) % 2]
            for r in range(0, D, 128):
                slab(WO6[li][r:r + 128, :], WO6BF[li][r:r + 128, :], None)
            out['wo'] = pst['b'][(pst['i'] + 1) % 2]
            P.do('g', lambda e, d=plg, s=LNG6[li]: e.dma_start(out=d[:], in_=s), [pst['lg']], dma=True)
            tlg = P.tok('g')
            last_sc = None
            for dk in range(8):
                for c0 in range(0, DIP, 1024):
                    w = min(1024, DIP - c0)
                    ex = [tlg] if dk == 0 and c0 == 0 else ()
                    last_sc = slab(W1A6[li][dk * 128:(dk + 1) * 128, c0:c0 + w], W1A6BF[li][dk * 128:(dk + 1) * 128, c0:c0 + w], plg[:, dk:dk + 1], ex)
            for dk in range(8):
                for c0 in range(0, DIP, 1024):
                    w = min(1024, DIP - c0)
                    last_sc = slab(W1G6[li][dk * 128:(dk + 1) * 128, c0:c0 + w], W1G6BF[li][dk * 128:(dk + 1) * 128, c0:c0 + w], plg[:, dk:dk + 1])
            pst['lg'] = last_sc
            out['w1'] = pst['b'][(pst['i'] + 1) % 2]
            for r in range(0, DIP, 128):
                slab(W26[li][r:r + 128, :], W26BF[li][r:r + 128, :], None)
            out['w2'] = pst['b'][(pst['i'] + 1) % 2]
            return out
        preps = [prep_l1f(0)]
        nsteps = int(round(n_sub * 2))
        for li in range((nsteps + 1) // 2):
            P.do('v', lambda e, gb=gb, z_sb=z_sb: e.tensor_copy(gb[:].rearrange('p (c w) -> p c w', c=NC), z_sb[:].rearrange('p (c w) -> p c w', c=NC)[:, :, 1:WZ]), z_gate)
            t_zbf = P.tok('v')

            def zb_rhs(k, t0, t1):
                if k < 4:
                    return gb[:, k * T + t0:k * T + t1]
                if t0 == 0:
                    return gb[:, k * T + 0:k * T + t1 - 1]
                return gb[:, k * T + t0 - 1:k * T + t1 - 1]
            tok_z1 = emit_l1f_attn(nc, P, z_sb, gb, zb_rhs, CB, CON, tok_con, nmx, sums, rcs, sqs, WQ6BF, WO6BF, QS6, KS6, li, [t_zbf] + z_gate, attn_head, dma_ch, preps[li])
            z_gate = tok_z1
            tap(f'z_a{li}', z_sb[:], tok_z1)
            if 2 * li + 1 >= nsteps:
                break
            P.barrier(('v', 's', 't', 'y'))
            if li + 1 < (nsteps + 1) // 2:
                preps.append(prep_l1f(li + 1))
            tok_z2 = emit_l1f_ffn(nc, P, z_sb, xn, CB, ST, sqs, tok_con, LNG6, LNB6, W1A6, W1G6, B1A6, B1G6, W26, B26, lnbb, beffa, beffg, li, z_gate, emit_ln, zc, dma_ch, (W1A6BF, W1G6BF, W26BF, preps[li]))
            z_gate = tok_z2
            tap(f'z_f{li}', z_sb[:], tok_z2)
        toks = []
        for c in range(NC):
            P.do('s', lambda e, cc=c, gb=gb, zc=zc, z_sb=z_sb: e.activation(gb[:, cc * T:(cc + 1) * T], zc(z_sb, cc, 0, T), AF.Silu), z_gate)
            toks.append(P.tok('s'))
        tap('g2', gb[:], toks)
    return toks

def emit_l1f_attn(nc, P, z_sb, gb, zb_rhs, CB, CON, tok_con, nmx, sums, rcs, sqs, WQ6BF, WO6BF, QS6, KS6, li, gate0, attn_head, dma_ch, prep=None):
    ident = CB['ident'][:]
    t_zadd = {}
    with contextlib.ExitStack() as EO:
        qsc = EO.enter_context(nc.sbuf_tensor(_U('qsc'), [128, 1], F32))
        ksc = EO.enter_context(nc.sbuf_tensor(_U('ksc'), [128, 1], F32))
        P.do('y', lambda e, qsc=qsc, QS6=QS6, li=li: e.dma_start(out=qsc[:], in_=QS6[li]), list(gate0), dma=True)
        P.do('y', lambda e, ksc=ksc, KS6=KS6, li=li: e.dma_start(out=ksc[:], in_=KS6[li]), dma=True)
        tok_qs = P.tok('y')
        for pg in range(2):
            P.barrier()
            tok_ao = []
            eg_outer = contextlib.ExitStack()
            ao = eg_outer.enter_context(nc.sbuf_tensor(_U('zao'), [128, 4 * T], BF16))
            with contextlib.ExitStack() as ep:
                rotb = {k: ep.enter_context(nc.sbuf_tensor(_U('rb_') + k, [128, T], BF16)) for k in ('rotc', 'rots', 'rotci', 'rotsi')}
                rtk = []
                for k in ('rotc', 'rots', 'rotci', 'rotsi'):
                    P.do('y', lambda e, d=rotb[k], s=CON[k]: e.dma_start(out=d[:], in_=s), list(gate0), dma=True)
                    rtk.append(P.tok('y'))
                tok_rot = rtk[-1]
                wbf = [ep.enter_context(nc.sbuf_tensor(_U(f'zwb{i}'), [128, 1024], BF16)) for i in range(2)]
                q_sb = ep.enter_context(nc.sbuf_tensor(_U('zq'), [128, T], BF16))
                k_sb = ep.enter_context(nc.sbuf_tensor(_U('zk'), [128, T], BF16))
                v_sb = ep.enter_context(nc.sbuf_tensor(_U('zv'), [128, T], BF16))
                vtm = ep.enter_context(nc.sbuf_tensor(_U('zvt'), [128, T], BF16))
                pt_sb = ep.enter_context(nc.sbuf_tensor(_U('zpt'), [128, 384], BF16))
                p_sb = ep.enter_context(nc.sbuf_tensor(_U('zp'), [128, 384], BF16))
                fs_b = [ep.enter_context(nc.sbuf_tensor(_U(f'zfb{i}'), [2, 512], BF16)) for i in range(2)]
                t1b = [ep.enter_context(nc.sbuf_tensor(_U(f'zt1{i}'), [128, 512], BF16)) for i in range(2)]
                t2b = [ep.enter_context(nc.sbuf_tensor(_U(f'zt2{i}'), [128, 512], BF16)) for i in range(2)]
                sq2 = [ep.enter_context(nc.sbuf_tensor(_U(f'zsq{i}'), [128, 512], BF16)) for i in range(2)]
                pw = prep.get('wq') if prep is not None else None
                wfree = {0: pw, 1: pw}
                gates = {}
                prevs = {}
                for pi in range(4):
                    pr = pg * 4 + pi
                    tjs = []
                    for j, off in enumerate((pr * 128, D + pr * 128, 2 * D + pr * 128)):
                        wi = (pr * 3 + j) % 2
                        tc = dma_ch(wbf[wi][:], WQ6BF[li][:, off:off + 128], [wfree[wi]])
                        wb_ = wbf[wi]
                        dst = (q_sb, k_sb, v_sb)[j]
                        with contextlib.ExitStack() as eq:
                            qps = [eq.enter_context(nc.psum_tensor(_U(f'zqp{i}'), [128, 512], F32)) for i in range(2)]
                            sqp = [eq.enter_context(nc.psum_tensor(_U(f'zsqp{i}'), [32, 512], F32)) for i in range(2)]
                            f2p = [eq.enter_context(nc.psum_tensor(_U(f'zf2p{i}'), [128, 512], F32)) for i in range(2)]
                            swp = [eq.enter_context(nc.psum_tensor(_U(f'zswp{i}'), [128, 512], F32)) for i in range(2)]
                            tev = prevs.setdefault('tev', {})
                            tsqd = prevs.setdefault('tsqd', {})
                            for sp in range(4):
                                b = sp % 2
                                t0 = sp * 512
                                w0 = [tc, tev.get(b), tsqd.get(b)] + gate0 + ([gates.get('prev_exp'), gates.get('prev_tav'), gates.get('prev_ao'), gates.get('prev_ptev')] if j == 0 else [])
                                for k in range(NC):
                                    sh = k >= 4
                                    rhs = zb_rhs(k, t0, t0 + 512)
                                    o0 = 1 if sh and t0 == 0 else 0
                                    P.do('t', lambda e, kk=k, bb=b, r=rhs, oo=o0, qps=qps, wb_=wb_: e.matmul(qps[bb][:, oo:512], wb_[:, kk * 128:(kk + 1) * 128], r, start=kk == 0, stop=kk == NC - 1), w0 if k == 0 else [])
                                tm = P.tok('t')
                                wfree[wi] = tm
                                if j == 2:
                                    P.do('s', lambda e, ss=sp, bb=b, v_sb=v_sb, qps=qps: e.copy(v_sb[:, ss * 512:(ss + 1) * 512], qps[bb][:]), [tm])
                                    tev[b] = P.tok('s')
                                    tjs.append(tev[b])
                                    continue
                                P.do('s', lambda e, bb=b, sq2=sq2, qps=qps: e.activation(sq2[bb][:], qps[bb][:], AF.Square), [tm, prevs.get(('tssp', b))])
                                tsq = P.tok('s')
                                tsqd[b] = tsq
                                scl = 8.0 if j == 0 else 1.0
                                scol = qsc if j == 0 else ksc
                                P.do('v', lambda e, d=dst, ss=sp, bb=b, s_=scl, sc=scol, qps=qps: e.tensor_scalar(d[:, ss * 512:(ss + 1) * 512], qps[bb][:], s_, sc[:], ALU.mult, ALU.mult), [tm, tok_qs, tsq])
                                tev[b] = P.tok('v')
                                P.do('t', lambda e, bb=b, sqp=sqp, CB=CB, sq2=sq2: e.matmul(sqp[bb][:], CB['bdones'][:], sq2[bb][:], start=True, stop=True), [tsq, tok_con, prevs.get(('tf1', b))])
                                tssp = P.tok('t')
                                prevs[('tssp', b)] = tssp
                                P.do('s', lambda e, bb=b, fs_b=fs_b, sqp=sqp: e.activation(fs_b[bb][:], sqp[bb][0:2, :], AF.Sqrt), [tssp, prevs.get(('tf5', b))])
                                tf1 = P.tok('s')
                                prevs[('tf1', b)] = tf1
                                def _rcp_bf(e, bb=b, fs_b=fs_b, nc=nc):
                                    with nc.allow_low_precision(reason='l2norm scale is rounded to bf16 for the PE broadcast anyway'):
                                        return e.reciprocal(fs_b[bb][:], fs_b[bb][:])
                                P.do('v', _rcp_bf, [tf1])
                                tf2 = P.tok('v')
                                P.do('v', lambda e, bb=b, fs_b=fs_b: e.tensor_scalar(fs_b[bb][:], fs_b[bb][:], 1e+12, None, ALU.min), [tf2])
                                tf4 = P.tok('v')
                                P.do('t', lambda e, bb=b, f2p=f2p, CB=CB, fs_b=fs_b: e.matmul(f2p[bb][:], CB['f2sel'][:], fs_b[bb][:], start=True, stop=True), [tf4, prevs.get(('tqn', b))])
                                tf5 = P.tok('t')
                                prevs[('tf5', b)] = tf5
                                P.do('v', lambda e, d=dst, ss=sp, bb=b, f2p=f2p: e.tensor_mul(d[:, ss * 512:(ss + 1) * 512], d[:, ss * 512:(ss + 1) * 512], f2p[bb][:]), [tf5])
                                tqn = P.tok('v')
                                prevs[('tqn', b)] = tqn
                                P.do('t', lambda e, d=dst, ss=sp, bb=b, swp=swp, CB=CB: e.matmul(swp[bb][:], CB['swapm'][:], d[:, ss * 512:(ss + 1) * 512], start=True, stop=True), [tqn, prevs.get(('tr2', b))])
                                tsw = P.tok('t')
                                rc_, rs_ = ('rotc', 'rots') if j == 0 else ('rotci', 'rotsi')
                                P.do('v', lambda e, d=dst, ss=sp, bb=b, rr=rc_, t1b=t1b, rotb=rotb: e.tensor_mul(t1b[bb][:], d[:, ss * 512:(ss + 1) * 512], rotb[rr][:, ss * 512:(ss + 1) * 512]), [tqn, tok_rot, prevs.get(('tadd', b))])
                                P.do('v', lambda e, ss=sp, bb=b, rr=rs_, t2b=t2b, swp=swp, rotb=rotb: e.tensor_mul(t2b[bb][:], swp[bb][:], rotb[rr][:, ss * 512:(ss + 1) * 512]), [tsw, prevs.get(('tadd', b))])
                                tr2 = P.tok('v')
                                prevs[('tr2', b)] = tr2
                                P.do('g', lambda e, d=dst, ss=sp, bb=b, t1b=t1b, t2b=t2b: e.tensor_add(d[:, ss * 512:(ss + 1) * 512], t1b[bb][:], t2b[bb][:]), [tr2])
                                tadd = P.tok('g')
                                prevs[('tadd', b)] = tadd
                                tjs.append(tadd)
                    with contextlib.ExitStack() as ev:
                        vtp = ev.enter_context(nc.psum_tensor(_U('zvp'), [128, 2048], BF16))
                        for a in range(NT):
                            P.do('t', lambda e, aa=a, vtp=vtp, v_sb=v_sb, ident=ident: e.transpose(vtp[:, aa * 128:(aa + 1) * 128], v_sb[:, aa * 128:(aa + 1) * 128], ident), tjs + [gates.get('prev_tav'), gates.get('prev_ao'), gates.get('prev_ptev')] if a == 0 else [])
                        tvt = P.tok('t')
                        P.do('v', lambda e, vtm=vtm, vtp=vtp: e.tensor_copy(vtm[:], vtp[:]), [tvt])
                        tok_vtm = P.tok('v')
                    gates['qkv'] = tjs + [tok_vtm]
                    gates['vtm'] = [tok_vtm]

                    def mask_fn(qt, sps, _dummy=None):
                        klen = (qt - max(0, qt - 2) + 1) * 128
                        ops = []
                        if qt >= 2:
                            ops.append((0, 'tri_lo'))
                        ops.append((klen - 128, 'tri_hi'))
                        for i, (c0, nm) in enumerate(ops):
                            P.do('t', lambda e, cc=c0, n=nm, last=i == len(ops) - 1, sps=sps, CB=CB: e.matmul(sps[:, cc:cc + 128], CB[n][:], CB['negi'][:], start=False, stop=last))
                    for hh in range(2 if L1F_PART in ('attn', 'full') else 0):

                        def ao_w(qt, avp, w, pi=pi, hh=hh):
                            P.do('v', lambda e, col=qt, h0=hh * 64, ao=ao, pi=pi, avp=avp: e.tensor_copy(ao[h0:h0 + 64, pi * T + col * 128:pi * T + (col + 1) * 128], avp[h0:h0 + 64, :]), w)
                            tok_ao.append(P.tok('v'))
                            return tok_ao[-1]
                        attn_head(f'z{pr}h{hh}', hh * 64, q_sb, k_sb, vtm, p_sb, pt_sb, nmx, sums, rcs, ao_w, list(range(NT)), lambda qt: qt * 128, lambda qt: (max(0, qt - 2) * 128, (qt - max(0, qt - 2) + 1) * 128), gates, mask_fn=mask_fn)
            P.barrier()
            if L1F_PART != 'full':
                eg_outer.close()
                continue
            with contextlib.ExitStack() as ew:
                wbf2 = ew.enter_context(nc.sbuf_tensor(_U('zwb2'), [128, 4096], BF16))
                pps = [ew.enter_context(nc.psum_tensor(_U(f'zpp{i}'), [128, 512], F32)) for i in range(2)]
                tc = dma_ch(wbf2[:], WO6BF[li][pg * 512:(pg + 1) * 512, :], tok_ao[-1:] + ([prep.get('wo')] if prep is not None else []))
                tev = {}
                cnt = 0
                for of in range(8):
                    for sp in range(4):
                        b = cnt % 2
                        cnt += 1
                        w0 = [tc, tev.get(b)] + tok_ao[-1:]
                        for k in range(4):
                            P.do('t', lambda e, kk=k, oo=of, ss=sp, bb=b, pps=pps, wbf2=wbf2, ao=ao: e.matmul(pps[bb][:], wbf2[:, kk * 1024 + oo * 128:kk * 1024 + (oo + 1) * 128], ao[:, kk * T + ss * 512:kk * T + (ss + 1) * 512], start=kk == 0, stop=kk == 3), w0 if k == 0 else [])
                        tm = P.tok('t')
                        P.do('v', lambda e, bb=b, oo=of, ss=sp, z_sb=z_sb, pps=pps: e.tensor_tensor(z_sb[:, oo * WZ + 1 + ss * 512:oo * WZ + 1 + (ss + 1) * 512], pps[bb][:], z_sb[:, oo * WZ + 1 + ss * 512:oo * WZ + 1 + (ss + 1) * 512], ALU.add), [tm, t_zadd.get((b, of, sp))])
                        tev[b] = P.tok('v')
                        t_zadd[b, of, sp] = tev[b]
            eg_outer.close()
    return [t for t in t_zadd.values()]

def emit_l1f_ffn(nc, P, z_sb, xn, CB, ST, sqs, tok_con, LNG6, LNB6, W1A6, W1G6, B1A6, B1G6, W26, B26, lnbb, beffa, beffg, li, gate0, emit_ln, zc, dma_ch, WCACHE=None):
    EPS1 = 1e-05
    W1ABF, W1GBF, W2BF, prep = WCACHE
    t_zw = {}
    with contextlib.ExitStack() as EF:
        xn1 = EF.enter_context(nc.sbuf_tensor(_U('xn1'), [128, NC * 512], BF16))
        h_bf = EF.enter_context(nc.sbuf_tensor(_U('zh'), [128, NK2 * 512], BF16))
        wba = EF.enter_context(nc.sbuf_tensor(_U('zfwa'), [128, 4096], BF16))
        wbg = EF.enter_context(nc.sbuf_tensor(_U('zfwg'), [128, 4096], BF16))
        gel = [EF.enter_context(nc.sbuf_tensor(_U(f'zgel{i}'), [128, 512], BF16)) for i in range(2)]
        lg = EF.enter_context(nc.sbuf_tensor(_U('zlg'), [128, 8], F32))
        lb = EF.enter_context(nc.sbuf_tensor(_U('zlb'), [128, 8], F32))
        lgr = EF.enter_context(nc.sbuf_tensor(_U('zlgr'), [128, 8], F32))
        ba = EF.enter_context(nc.sbuf_tensor(_U('zba'), [128, NK2], F32))
        bg = EF.enter_context(nc.sbuf_tensor(_U('zbg'), [128, NK2], F32))
        b2 = EF.enter_context(nc.sbuf_tensor(_U('zb2'), [128, 8], F32))
        tks = []
        for dst, srcw in ((lg, LNG6[li]), (lb, LNB6[li]), (ba, B1A6[li]), (bg, B1G6[li]), (b2, B26[li])):
            P.do('y', lambda e, d=dst, s=srcw: e.dma_start(out=d[:], in_=s), list(gate0), dma=True)
            tks.append(P.tok('y'))
        tok_cst = tks[-1]
        P.do('v', lambda e, lgr=lgr, lg=lg: e.reciprocal(lgr[:], lg[:]), [tok_cst])
        trl = P.tok('v')
        for k in range(NC):
            P.do('s', lambda e, kk=k, lnbb=lnbb, lgr=lgr, lb=lb: e.activation(lnbb[:, kk:kk + 1], lgr[:, kk:kk + 1], AF.Identity, scale=lb[:, kk:kk + 1]), [trl] if k == 0 else [])
        tok_lnbb = P.tok('s')
        for q in range(4):
            P.barrier(('v', 's', 't', 'y'))
            tq0 = q * 512

            def src_col(a, c, t0=tq0):
                return zc(z_sb, c, t0 + a * 128, t0 + (a + 1) * 128, shift=c >= 4)

            def dst_fn(a):
                return xn1[:].rearrange('p (c w) -> p c w', c=NC)[:, :, a * 128:(a + 1) * 128]
            lt = emit_ln(src_col, 4, dst_fn, f'zl{li}_{q}', EPS1, waits=gate0 + [t_zw.get('last')])
            t_hev = None
            with contextlib.ExitStack() as e1:
                psa = [e1.enter_context(nc.psum_tensor(_U(f'zpsa{i}'), [128, 512], F32)) for i in range(2)]
                psg = [e1.enter_context(nc.psum_tensor(_U(f'zpsg{i}'), [128, 512], F32)) for i in range(2)]
                psb = e1.enter_context(nc.psum_tensor(_U('zpsb'), [128, 16], F32))
                t_stt = {}
                t_gel = {}
                ocg = 0
                for os_ in range(DIP // 512):
                    pass
                nos = DIP // 512 + (1 if DIP % 512 else 0)
                for os_ in range(nos):
                    o0 = os_ * 512
                    ow = min(512, DIP - o0)
                    nch = ow // 128
                    pw1 = [prep.get('w1')] if prep is not None and q == 0 and os_ == 0 else []
                    tca = dma_ch(wba[:, 0:8 * ow], W1ABF[li][:, o0:o0 + ow],
                                 [t_zw.get('wafree')] + pw1)
                    tcg = dma_ch(wbg[:, 0:8 * ow], W1GBF[li][:, o0:o0 + ow],
                                 [t_zw.get('wgfree')])
                    if q == 0:
                        prev_b1 = None
                        for oc in range(nch):
                            gco = o0 // 128 + oc
                            for k in range(NC):
                                P.do('t', lambda e, kk=k, occ=oc, w=ow, psb=psb, wba=wba, lnbb=lnbb: e.matmul(psb[:, 0:1], wba[:, kk * w + occ * 128:kk * w + occ * 128 + 128], lnbb[:, kk:kk + 1], start=kk == 0, stop=kk == NC - 1), [tca, tcg, tok_lnbb, prev_b1] if k == 0 else [])
                            tb1 = P.tok('t')
                            P.do('v', lambda e, gc=gco, beffa=beffa, ba=ba, psb=psb: e.tensor_tensor(beffa[:, gc:gc + 1], ba[:, gc:gc + 1], psb[:, 0:1], ALU.add), [tb1, tok_cst])
                            tba = P.tok('v')
                            for k in range(NC):
                                P.do('t', lambda e, kk=k, occ=oc, w=ow, psb=psb, wbg=wbg, lnbb=lnbb: e.matmul(psb[:, 1:2], wbg[:, kk * w + occ * 128:kk * w + occ * 128 + 128], lnbb[:, kk:kk + 1], start=kk == 0, stop=kk == NC - 1), [tba] if k == 0 else [])
                            tb2 = P.tok('t')
                            P.do('v', lambda e, gc=gco, beffg=beffg, bg=bg, psb=psb: e.tensor_tensor(beffg[:, gc:gc + 1], bg[:, gc:gc + 1], psb[:, 1:2], ALU.add), [tb2])
                            prev_b1 = P.tok('v')
                        tbef = P.tok('v')
                    else:
                        tbef = None
                    tsc = tcg
                    for oc in range(nch):
                        gco = o0 // 128 + oc
                        b = ocg % 2
                        ocg += 1
                        w0 = [tsc, tca, t_stt.get(b)] + lt
                        for k in range(NC):
                            P.do('t', lambda e, kk=k, occ=oc, w=ow, bb=b, psa=psa, wba=wba, xn1=xn1: e.matmul(psa[bb][:], wba[:, kk * w + occ * 128:kk * w + occ * 128 + 128], xn1[:, kk * 512:(kk + 1) * 512], start=kk == 0, stop=kk == NC - 1), w0 if k == 0 else [])
                        for k in range(NC):
                            P.do('t', lambda e, kk=k, occ=oc, w=ow, bb=b, psg=psg, wbg=wbg, xn1=xn1: e.matmul(psg[bb][:], wbg[:, kk * w + occ * 128:kk * w + occ * 128 + 128], xn1[:, kk * 512:(kk + 1) * 512], start=kk == 0, stop=kk == NC - 1), [t_gel.get(b)] if k == 0 else [])
                        thg = P.tok('t')
                        P.do('s', lambda e, gc=gco, bb=b, gel=gel, psg=psg, beffg=beffg: e.activation(gel[bb][:], psg[bb][:], AF.Gelu, bias=beffg[:, gc:gc + 1]), [thg])
                        tge = P.tok('s')
                        t_gel[b] = tge
                        P.do('v', lambda e, gc=gco, bb=b, h_bf=h_bf, psa=psa, beffa=beffa, gel=gel: e.scalar_tensor_tensor(h_bf[:, gc * 512:(gc + 1) * 512], psa[bb][:], beffa[:, gc:gc + 1], gel[bb][:], ALU.add, ALU.mult), [tge])
                        t_hev = P.tok('v')
                        t_stt[b] = t_hev
                    t_zw['wafree'] = thg
                    t_zw['wgfree'] = thg
            P.barrier(('v', 's', 't', 'y'))
            with contextlib.ExitStack() as e2:
                zps = [e2.enter_context(nc.psum_tensor(_U(f'zzp{i}'), [128, 512], F32)) for i in range(8)]
                t_mm = {}
                for fs in range(6):
                    r0 = fs * 512
                    rw = min(512, DIP - r0)
                    nch = rw // 128
                    pw2 = [prep.get('w2')] if prep is not None and q == 0 and fs == 0 else []
                    tc2 = dma_ch(wba[:, 0:nch * 1024], W2BF[li][r0:r0 + rw, :],
                                 [t_zw.get('w2free')] + ([t_hev] if fs == 0 else []) + pw2)
                    last = None
                    for of in range(8):
                        for m in range(nch):
                            kc = fs * 4 + m
                            P.do('t', lambda e, oo=of, mm2=m, k=kc, zps=zps, wba=wba, h_bf=h_bf: e.matmul(zps[oo][:], wba[:, mm2 * 1024 + oo * 128:mm2 * 1024 + (oo + 1) * 128], h_bf[:, k * 512:(k + 1) * 512], start=k == 0, stop=k == NK2 - 1), [tc2, t_hev] if of == 0 and m == 0 else [])
                        last = P.tok('t')
                        t_mm[of] = last
                    t_zw['w2free'] = last
                tlast = []
                for of in range(8):
                    P.do('v', lambda e, oo=of, t0=tq0, zc=zc, z_sb=z_sb, zps=zps, b2=b2: e.scalar_tensor_tensor(zc(z_sb, oo, t0, t0 + 512), zps[oo][:], b2[:, oo:oo + 1], zc(z_sb, oo, t0, t0 + 512), ALU.add, ALU.add), [t_mm[of], tok_cst])
                    tlast.append(P.tok('v'))
                t_zw['last'] = tlast[-1]
        return tlast

def emit_l1_attn(nc, P, xn, gb, CB, ST, sqs, tok_con, WQKV1, WO1, BQKV1, BO1, NQW, NKW, XSP, OUT, fin, tap, tap_toks, attn_head, zc, dma_ch, WBuf=None):
    ident = CB['ident'][:]
    P.barrier()
    with contextlib.ExitStack() as EL:
        sqs = EL.enter_context(nc.sbuf_tensor(_U('sqs'), [128, 512], BF16))
        k_all = EL.enter_context(nc.sbuf_tensor(_U('lk'), [128, NC * T], BF16))
        nmx = EL.enter_context(nc.sbuf_tensor(_U('lnm'), [128, 16], F32))
        sums = EL.enter_context(nc.sbuf_tensor(_U('lsm'), [128, 16], F32))
        rcs = EL.enter_context(nc.sbuf_tensor(_U('lrc'), [128, 16], F32))
        bqkv = [EL.enter_context(nc.sbuf_tensor(_U(f'lbq{j}'), [128, 8], F32)) for j in range(3)]
        bo = EL.enter_context(nc.sbuf_tensor(_U('lbo'), [128, 8], F32))
        nw8 = EL.enter_context(nc.sbuf_tensor(_U('lnw'), [128, 8], F32))
        nkw_ = EL.enter_context(nc.sbuf_tensor(_U('lkw'), [128, 8], F32))
        rr_f = EL.enter_context(nc.sbuf_tensor(_U('lrr'), [1, 512], F32))
        rr_b = EL.enter_context(nc.sbuf_tensor(_U('lrb'), [1, 512], BF16))
        for j in range(3):
            P.do('y', lambda e, d=bqkv[j], BQKV1=BQKV1, j=j: e.dma_start(out=d[:], in_=BQKV1[j]), list(fin), dma=True)
        P.do('y', lambda e, bo=bo, BO1=BO1: e.dma_start(out=bo[:], in_=BO1), dma=True)
        P.do('y', lambda e, nw8=nw8, NQW=NQW: e.dma_start(out=nw8[:], in_=NQW), dma=True)
        P.do('y', lambda e, nkw_=nkw_, NKW=NKW: e.dma_start(out=nkw_[:], in_=NKW), dma=True)
        tok_b = P.tok('y')
        P.do('v', lambda e, nw8=nw8: e.tensor_scalar(nw8[:], nw8[:], 0.125, None, ALU.mult), [tok_b])
        tok_nw = P.tok('v')

        def qk_build(WCOL, bias_j, wcol_scale, dst_sb, dst_w, spans, tag, gate):
            toks = []
            with contextlib.ExitStack() as eb:
                wbq = WBuf(eb, tag + 'w', 1024)
                ssp = eb.enter_context(nc.psum_tensor(_U(tag + 'ss'), [1, 512], F32))
                f2p = eb.enter_context(nc.psum_tensor(_U(tag + 'f2'), [128, 512], F32))
                qps = [eb.enter_context(nc.psum_tensor(_U(tag + f'qp{i}'), [128, 512], F32)) for i in range(2)]
                t_wfree = gate[-1] if gate else None
                tevs = {}
                last_wmm = None
                prevs_qk = {}
                dst_off = min(spans) * 512
                for sp in spans:
                    t0 = sp * 512
                    lt0 = t0 - dst_off
                    tsqs = []
                    for oc in range(NC):
                        wbf, tc = wbq.load(WCOL(oc), [t_wfree] if t_wfree else ())
                        t_wfree = None
                        b = oc % 2
                        w0 = [tc, tevs.get(b)] + gate
                        for k in range(NC):
                            P.do('t', lambda e, kk=k, tt0=t0, bb=b, qps=qps, wbf=wbf, zc=zc, xn=xn: e.matmul(qps[bb][:], wbf[:, kk * 128:(kk + 1) * 128], zc(xn, kk, tt0, tt0 + 512), start=kk == 0, stop=kk == NC - 1), w0 if k == 0 else [])
                        tm = P.tok('t')
                        tevs['w', oc] = tm
                        last_wmm = tm
                        wbq.consumed(tm)
                        P.do('s', lambda e, bb=b, occ=oc, sqs=sqs, qps=qps, bqkv=bqkv, bias_j=bias_j: e.activation(sqs[:], qps[bb][:], AF.Square, bias=bqkv[bias_j][:, occ:occ + 1]), [tm, tok_b])
                        tsq = P.tok('s')
                        P.do('t', lambda e, occ=oc, ssp=ssp, CB=CB, sqs=sqs: e.matmul(ssp[:], CB['onescol'][:], sqs[:], start=occ == 0, stop=occ == NC - 1), [tsq, tok_con])
                        tsqs.append(P.tok('t'))
                        P.do('s', lambda e, bb=b, occ=oc, d=dst_sb, tt0=lt0, dst_w=dst_w, qps=qps, bqkv=bqkv, bias_j=bias_j: e.activation(d[:, occ * dst_w + tt0:occ * dst_w + tt0 + 512], qps[bb][:], AF.Identity, bias=bqkv[bias_j][:, occ:occ + 1]), [tm])
                        tevs[b] = P.tok('s')
                    P.do('s', lambda e, rr_f=rr_f, ssp=ssp, CB=CB: e.activation(rr_f[:, 0:512], ssp[:], AF.Sqrt, scale=CB['cpool'][0:1, 4:5], bias=CB['cpool'][0:1, 1:2]), [tsqs[-1], prevs_qk.get('tr4')])
                    tr1 = P.tok('s')
                    P.do('v', lambda e, rr_f=rr_f: e.reciprocal(rr_f[:, 0:512], rr_f[:, 0:512]), [tr1])
                    tr2 = P.tok('v')
                    P.do('s', lambda e, rr_b=rr_b, rr_f=rr_f: e.copy(rr_b[:, 0:512], rr_f[:, 0:512]), [tr2])
                    tr3 = P.tok('s')
                    P.do('t', lambda e, f2p=f2p, CB=CB, rr_b=rr_b, t0=t0: e.matmul(f2p[:], CB['onesrow'][:], rr_b[:, 0:512], start=True, stop=True), [tr3, tok_con, prevs_qk.get('apply')])
                    tr4 = P.tok('t')
                    prevs_qk['tr4'] = tr4
                    wcol = nw8 if wcol_scale else nkw_
                    for oc in range(NC):
                        P.do('v', lambda e, occ=oc, d=dst_sb, tt0=lt0, wc=wcol, dst_w=dst_w, f2p=f2p: e.scalar_tensor_tensor(d[:, occ * dst_w + tt0:occ * dst_w + tt0 + 512], d[:, occ * dst_w + tt0:occ * dst_w + tt0 + 512], wc[:, occ:occ + 1], f2p[:], ALU.mult, ALU.mult), [tr4, tok_nw] if oc == 0 else [tr4])
                        toks.append(P.tok('v'))
                        prevs_qk['apply'] = toks[-1]
            return toks
        tk = qk_build(lambda oc: WQKV1[:, D + oc * 128:D + (oc + 1) * 128], 1, False, k_all[:], T, range(4), 'lkb', fin)
        P.barrier()
        tap('l1k', k_all[:], tk)
        for th in range(2):
            P.barrier()
            h0 = th * 1024
            with contextlib.ExitStack() as EH:
                q_h = EH.enter_context(nc.sbuf_tensor(_U('lq'), [128, NC * 1024], BF16))
                ao = EH.enter_context(nc.sbuf_tensor(_U('lao'), [128, NC * 1024], BF16))
                tq = qk_build(lambda oc: WQKV1[:, oc * 128:(oc + 1) * 128], 0, True, q_h[:], 1024, [2 * th, 2 * th + 1], 'lqb', fin)
                P.barrier()
                tq = [tq[-1]]
                tok_ao = []
                with contextlib.ExitStack() as EP:
                    wstg = EP.enter_context(nc.sbuf_tensor(_U('lvw'), [128, 1024], F32))
                    wbf = EP.enter_context(nc.sbuf_tensor(_U('lvwb'), [128, 1024], BF16))
                    v_sb = EP.enter_context(nc.sbuf_tensor(_U('lv'), [128, T], BF16))
                    vtm = EP.enter_context(nc.sbuf_tensor(_U('lvt'), [128, T], BF16))
                    p_sb = EP.enter_context(nc.sbuf_tensor(_U('lp'), [128, T], BF16))
                    pt_sb = EP.enter_context(nc.sbuf_tensor(_U('lpt'), [128, T], BF16))
                    t_wfree = fin[-1] if fin else None
                    gates = {}
                    for pr in range(8):
                        tw = dma_ch(wstg[:], WQKV1[:, 2 * D + pr * 128:2 * D + (pr + 1) * 128], [t_wfree])
                        P.do('g', lambda e, wbf=wbf, wstg=wstg: e.tensor_copy(wbf[:], wstg[:]), [tw, gates.get('vfree')])
                        tc = P.tok('g')
                        t_wfree = tc
                        with contextlib.ExitStack() as eq:
                            qps = [eq.enter_context(nc.psum_tensor(_U(f'lvp{i}'), [128, 512], F32)) for i in range(2)]
                            tev = {}
                            for sp in range(4):
                                b = sp % 2
                                w0 = [tc, tev.get(b)] + fin + ([gates.get('prev_exp'), gates.get('prev_tav'), gates.get('prev_ao'), gates.get('prev_ptev')] if sp == 0 else [])
                                for k in range(NC):
                                    P.do('t', lambda e, kk=k, ss=sp, bb=b, qps=qps, wbf=wbf, zc=zc, xn=xn: e.matmul(qps[bb][:], wbf[:, kk * 128:(kk + 1) * 128], zc(xn, kk, ss * 512, (ss + 1) * 512), start=kk == 0, stop=kk == NC - 1), w0 if k == 0 else [])
                                tm = P.tok('t')
                                P.do('s', lambda e, ss=sp, bb=b, v_sb=v_sb, qps=qps, bqkv=bqkv, pr=pr: e.activation(v_sb[:, ss * 512:(ss + 1) * 512], qps[bb][:], AF.Identity, bias=bqkv[2][:, pr:pr + 1]), [tm, tok_b])
                                tev[b] = P.tok('s')
                            tvv = [tev.get(0), tev.get(1)]
                        with contextlib.ExitStack() as ev:
                            vtp = ev.enter_context(nc.psum_tensor(_U('lvtp'), [128, 2048], BF16))
                            for a in range(NT):
                                P.do('t', lambda e, aa=a, vtp=vtp, v_sb=v_sb, ident=ident: e.transpose(vtp[:, aa * 128:(aa + 1) * 128], v_sb[:, aa * 128:(aa + 1) * 128], ident), tvv + [gates.get('prev_tav'), gates.get('prev_ao'), gates.get('prev_ptev')] if a == 0 else [])
                            tvt = P.tok('t')
                            P.do('v', lambda e, vtm=vtm, vtp=vtp: e.tensor_copy(vtm[:], vtp[:]), [tvt])
                            tok_vtm = P.tok('v')
                        gates['vfree'] = tvt
                        gates['qkv'] = tq + tk + [tok_vtm]
                        gates['vtm'] = [tok_vtm]
                        qv = q_h[:, pr * 1024:(pr + 1) * 1024]
                        kv = k_all[:, pr * T:(pr + 1) * T]
                        for hh in range(2):

                            def ao_w(qt, avp, w, pr=pr, hh=hh):
                                P.do('v', lambda e, col=qt, hq=hh * 64, ao=ao, pr=pr, avp=avp: e.tensor_copy(ao[hq:hq + 64, pr * 1024 + col * 128:pr * 1024 + (col + 1) * 128], avp[hq:hq + 64, :]), w)
                                tok_ao.append(P.tok('v'))
                                return tok_ao[-1]
                            attn_head(f'l{th}_{pr}h{hh}', hh * 64, qv, kv, vtm, p_sb, pt_sb, nmx, sums, rcs, ao_w, list(range(8)), lambda qt: qt * 128, lambda qt: (0, T), gates)
                P.barrier()
                with contextlib.ExitStack() as EW:
                    x2 = EW.enter_context(nc.sbuf_tensor(_U('x2'), [128, 8 * 1024], F32))
                    wstg2 = EW.enter_context(nc.sbuf_tensor(_U('lw2'), [128, 4096], F32))
                    wbf2 = EW.enter_context(nc.sbuf_tensor(_U('lwb2'), [128, 4096], BF16))
                    tmp = [EW.enter_context(nc.sbuf_tensor(_U(f'ltm{i}'), [128, 512], BF16)) for i in range(2)]
                    pps = [EW.enter_context(nc.psum_tensor(_U(f'lpp{i}'), [128, 512], F32)) for i in range(2)]
                    ptp = [EW.enter_context(nc.psum_tensor(_U(f'lptr{i}'), [128, 512], BF16)) for i in range(2)]
                    P.do('y', lambda e, hh0=h0, x2=x2, XSP=XSP: e.dma_start(out=x2[:], in_=XSP[:, hh0 * 8:hh0 * 8 + 8192]), fin, dma=True)
                    tx2 = P.tok('y')
                    tev = {}
                    ttr = {}
                    tadd = {}
                    t_w2f = None
                    for wh in range(2):
                        tw = dma_ch(wstg2[:], WO1[wh * 512:(wh + 1) * 512, :], [t_w2f] + tok_ao[-1:])
                        P.do('g', lambda e, wbf2=wbf2, wstg2=wstg2: e.tensor_copy(wbf2[:], wstg2[:]), [tw, tadd.get('wfree')])
                        tc = P.tok('g')
                        t_w2f = tc
                        cnt = 0
                        for of in range(8):
                            for sp in range(2):
                                b = cnt % 2
                                cnt += 1
                                w0 = [tc, tev.get(b)] + tok_ao[-1:]
                                for k in range(4):
                                    kk2 = wh * 4 + k
                                    P.do('t', lambda e, kk=k, oo=of, ss=sp, bb=b, kg=kk2, pps=pps, wbf2=wbf2, ao=ao: e.matmul(pps[bb][:], wbf2[:, kk * 1024 + oo * 128:kk * 1024 + (oo + 1) * 128], ao[:, kg * 1024 + ss * 512:kg * 1024 + (ss + 1) * 512], start=kk == 0, stop=kk == 3), w0 if k == 0 else [])
                                tm = P.tok('t')
                                P.do('v', lambda e, bb=b, oo=of, ss=sp, hh0=h0, first=wh == 0, tmp=tmp, pps=pps, bo=bo, gb=gb: e.scalar_tensor_tensor(tmp[bb][:], pps[bb][:], bo[:, oo:oo + 1] if not first else 0.0, gb[:, oo * T + hh0 + ss * 512:oo * T + hh0 + (ss + 1) * 512], ALU.add, ALU.mult), [tm, tok_b, ttr.get(b)])
                                tv = P.tok('v')
                                tev[b] = tv
                                for j in range(4):
                                    P.do('t', lambda e, bb=b, jj=j, ptp=ptp, tmp=tmp, ident=ident: e.transpose(ptp[bb][:, jj * 128:(jj + 1) * 128], tmp[bb][:, jj * 128:(jj + 1) * 128], ident), [tv, tadd.get(b)] if j == 0 else [])
                                tt2 = P.tok('t')
                                ttr[b] = tt2
                                dstx = x2[:].rearrange('p (a d) -> p a d', a=8)[:, sp * 4:sp * 4 + 4, of * 128:(of + 1) * 128]
                                P.do('v', lambda e, d=dstx, bb=b, ptp=ptp: e.tensor_tensor(d, d, ptp[bb][:].rearrange('p (j w) -> p j w', j=4), ALU.add), [tt2, tx2])
                                tadd[b] = P.tok('v')
                        tadd['wfree'] = tm
                    P.do('y', lambda e, hh0=h0, OUT=OUT, x2=x2: e.dma_start(out=OUT[hh0:hh0 + 1024, :].rearrange('(a p) d -> p a d', p=128), in_=x2[:].rearrange('p (a d) -> p a d', a=8)), [tadd.get(0), tadd.get(1)], dma=True)
                    tout = P.tok('y')
                    if th == 1:
                        P.do('g', lambda e: e.nop(), [tout] + tap_toks)
                        P.do('y', lambda e: e.nop(), [tout] + tap_toks)
                    else:
                        out_tok_half = tout

def _chunkmaj(v):
    return np.ascontiguousarray(v.reshape(8, 128).T)

def _prep_inputs(I):
    A = np.ascontiguousarray
    W1 = I['L1_ffn_W1']
    W1a = np.zeros((6, D, DIP), np.float32)
    W1g = np.zeros((6, D, DIP), np.float32)
    W1a[:, :, :DI] = W1[:, :, :DI]
    W1g[:, :, :DI] = W1[:, :, DI:]
    b1 = I['L1_ffn_b1']
    b1a = np.zeros((6, 128, NK2), np.float32)
    b1g = np.zeros((6, 128, NK2), np.float32)
    for i in range(6):
        b1a[i].T.flat[:DI] = b1[i, :DI]
        b1g[i].T.flat[:DI] = b1[i, DI:]
    b1a = np.stack([np.pad(b1[i, :DI], (0, DIP - DI)).reshape(NK2, 128).T for i in range(6)])
    b1g = np.stack([np.pad(b1[i, DI:], (0, DIP - DI)).reshape(NK2, 128).T for i in range(6)])
    W2p = np.zeros((6, DIP, D), np.float32)
    W2p[:, :DI, :] = I['L1_ffn_W2']
    base = {'w_attnT': A(I['L0_attn_w_attn'].T), 'w_projT': A(I['L0_attn_w_proj'].T), 'w_fcT': A(I['L0_ffn_W_fc'].T), 'w_prT': A(I['L0_ffn_W_proj'].T), 'wqkv1T': A(np.concatenate([I['L1_attn_Wq'].T, I['L1_attn_Wk'].T, I['L1_attn_Wv'].T], 1)), 'wo1T': A(I['L1_attn_Wo'].T), 'bqkv1': np.stack([_chunkmaj(I['L1_attn_bq']), _chunkmaj(I['L1_attn_bk']), _chunkmaj(I['L1_attn_bv'])]), 'bo1': _chunkmaj(I['L1_attn_bo']), 'nqw': _chunkmaj(I['L1_attn_norm_q_w']), 'nkw': _chunkmaj(I['L1_attn_norm_k_w']), 'wqkv6': A(I['L1_ffn_Wqkv']), 'wo6': A(I['L1_ffn_Wo']), 'qs6': A(np.tile(I['L1_ffn_q_scale'], (1, 2))[:, :, None]), 'ks6': A(np.tile(I['L1_ffn_k_scale'], (1, 2))[:, :, None]), 'lng6': np.stack([_chunkmaj(I['L1_ffn_ln_g'][i]) for i in range(6)]), 'lnb6': np.stack([_chunkmaj(I['L1_ffn_ln_b'][i]) for i in range(6)]), 'w1a6': W1a, 'w1g6': W1g, 'b1a6': b1a, 'b1g6': b1g, 'w26': W2p, 'b26': np.stack([_chunkmaj(I['L1_ffn_b2'][i]) for i in range(6)])}
    for k, v in host_constants().items():
        if k not in ('identf', 'onescolf'):
            base['c_' + k] = v
    return base

def kernel(**inputs):
    base = _prep_inputs(inputs)
    x = np.ascontiguousarray(inputs['L0_attn_x']).astype(np.float32)
    nc = build('all')
    in_maps = [dict(base, x=np.ascontiguousarray(x[i])) for i in range(8)]
    res = run_bass_kernel_spmd(nc, in_maps, core_ids=list(range(8)))
    return np.stack([np.asarray(r['out']) for r in res.results])

# version 12
# speedup vs baseline: 1.1264x; 1.0772x over previous
import math
import os as _os
import contextlib
L1F_PART = _os.environ.get('L1F_PART', 'full')
L1F_PREP = _os.environ.get('L1F_PREP', 'all')
L1F_STAGES = ['qkv', 'sq', 'ev', 'ssp', 'sqrt', 'max', 'rcp', 'cpy', 'f2', 'f', 'all']
def _prep_lt(s):
    return L1F_STAGES.index(L1F_PREP) < L1F_STAGES.index(s)
import numpy as np
import concourse.bass as bass
import concourse.mybir as mybir
from concourse.bass_utils import run_bass_kernel_spmd
F32 = mybir.dt.float32
BF16 = mybir.dt.bfloat16
AF = mybir.ActivationFunctionType
ALU = mybir.AluOpType
AX = mybir.AxisListType
T = 2048
D = 1024
NT = 16
NC = 8
WZ = 2049
NEG = -3e+38
DI = 2730
DIP = 2816
NK2 = 22
EPS0 = 1e-05
_UC = [0]

def _U(name):
    _UC[0] += 1
    return f'{name}_{_UC[0]}'

class Stream:

    def __init__(self, key):
        self.key = key
        self.ops = []
        self.count = 0
        self.dcount = 0

    def tok(self):
        last = self.ops[-1]
        if last[2] is None:
            if last[3] and self.key != 'y':
                self.dcount += 16
                last[2] = ('d', self.dcount)
            else:
                self.count += 16 if last[3] else 1
                last[2] = ('c', self.count)
        kd, val = last[2]
        return (self.key + 'd' if kd == 'd' else self.key, val)

class Prog:

    def __init__(self):
        self.streams = {k: Stream(k) for k in ('v', 's', 't', 'g', 'y')}
        self.pending = {}

    def do(self, key, fn, waits=(), dma=False):
        st = self.streams[key]
        wd = dict(self.pending.pop(key, ()))
        for w in waits:
            if w is None:
                continue
            k, val = w
            if k == key:
                continue
            if wd.get(k, 0) < val:
                wd[k] = val
        st.ops.append([fn, wd, None, dma])
        if dma:
            st.tok()
        return st

    def tok(self, key):
        return self.streams[key].tok()

    def barrier(self, keys=('v', 's', 't', 'g', 'y')):
        toks = {}
        for k in keys:
            st = self.streams[k]
            if st.ops:
                toks[k] = st.tok()
        for k in keys:
            pend = self.pending.setdefault(k, {})
            for kk, t in toks.items():
                if kk == k:
                    continue
                rk, val = t
                if pend.get(rk, 0) < val:
                    pend[rk] = val
        return toks

    def emit(self, nc, block, sems):

        def make(stream):

            def run(eng):
                hw = {}
                for fn, waits, inc, is_dma in stream.ops:
                    for pk, val in waits.items():
                        if hw.get(pk, 0) < val:
                            eng.wait_ge(sems[pk], val)
                            hw[pk] = val
                    ins = fn(eng)
                    if inc is not None:
                        sk = stream.key + 'd' if inc[0] == 'd' else stream.key
                        ins.then_inc(sems[sk], 16 if is_dma else 1)
            return run
        block.vector(make(self.streams['v']))
        block.scalar(make(self.streams['s']))
        block.tensor(make(self.streams['t']))
        block.gpsimd(make(self.streams['g']))
        block.sync(make(self.streams['y']))

def alibi_slopes(n=16):
    start = 2.0 ** (-2.0 ** (-(math.log2(n) - 3)))
    return [start * start ** i for i in range(n)]

def host_constants():
    import ml_dtypes
    bf = lambda a: np.ascontiguousarray(a).astype(ml_dtypes.bfloat16)
    c = {}
    c['ident'] = bf(np.eye(128))
    c['identf'] = np.eye(128, dtype=np.float32)
    sw = np.zeros((128, 128), np.float32)
    for b in range(2):
        for i in range(32):
            sw[b * 64 + i, b * 64 + 32 + i] = 1.0
            sw[b * 64 + 32 + i, b * 64 + i] = 1.0
    c['swapm'] = bf(sw)
    k_ = np.arange(128)[:, None]
    r_ = np.arange(128)[None, :]
    c['tri_lo'] = bf((k_ < r_).astype(np.float32))
    c['tri_hi'] = bf((k_ > r_).astype(np.float32))
    c['negi'] = bf(np.eye(128) * NEG)
    bd = np.zeros((128, 32), np.float32)
    bd[0:64, 0] = 1.0
    bd[64:128, 1] = 1.0
    c['bdones'] = bf(bd)
    f2 = np.zeros((2, 128), np.float32)
    f2[0, 0:64] = 1.0
    f2[1, 64:128] = 1.0
    c['f2sel'] = bf(f2)
    c['onesrow'] = bf(np.ones((1, 128)))
    c['onescol'] = bf(np.ones((128, 1)))
    c['onescolf'] = np.ones((128, 1), np.float32)
    DH = 64
    half = np.arange(0, DH, 2, dtype=np.float64)
    inv_freq = 1.0 / 10000.0 ** (half / DH)
    t_ = np.arange(T, dtype=np.float64)
    freqs = np.concatenate([t_[:, None] * inv_freq[None, :]] * 2, -1)
    base_scale = (half + 0.4 * DH) / (1.4 * DH)
    power = (t_ - T // 2) / 128.0
    scale = np.concatenate([base_scale[None, :] ** power[:, None]] * 2, -1)
    sgn = np.where(np.arange(DH) < 32, -1.0, 1.0)[:, None]
    c['rotc'] = bf(np.vstack([(np.cos(freqs) * scale).T] * 2))
    c['rots'] = bf(np.vstack([(np.sin(freqs) * scale).T * sgn] * 2))
    c['rotci'] = bf(np.vstack([(np.cos(freqs) / scale).T] * 2))
    c['rotsi'] = bf(np.vstack([(np.sin(freqs) / scale).T * sgn] * 2))
    return c

def build(upto='all', taps=(), race_check=True, n_sub=6):
    nc = bass.Bass('TRN2', target_bir_lowering=False, detect_race_conditions=race_check)
    P = Prog()
    consts = host_constants()
    er = contextlib.ExitStack()

    def din(name, shape, dt=F32):
        return nc.dram_tensor(name, list(shape), dt, kind='ExternalInput').ap()
    X_in = din('x', (T, D))
    OUT = nc.dram_tensor('out', [T, D], F32, kind='ExternalOutput').ap()
    WATT = din('w_attnT', (D, 3 * D))
    WPRJ = din('w_projT', (D, D))
    WFC = din('w_fcT', (D, 4096))
    WPR2 = din('w_prT', (4096, D))
    WQKV1 = din('wqkv1T', (D, 3 * D))
    WO1 = din('wo1T', (D, D))
    BQKV1 = din('bqkv1', (3, 128, 8))
    BO1 = din('bo1', (128, 8))
    NQW = din('nqw', (128, 8))
    NKW = din('nkw', (128, 8))
    WQKV6 = din('wqkv6', (6, D, 3 * D))
    WO6 = din('wo6', (6, D, D))
    QS6 = din('qs6', (6, 128, 1))
    KS6 = din('ks6', (6, 128, 1))
    LNG6 = din('lng6', (6, 128, 8))
    LNB6 = din('lnb6', (6, 128, 8))
    W1A6 = din('w1a6', (6, D, DIP))
    W1G6 = din('w1g6', (6, D, DIP))
    B1A6 = din('b1a6', (6, 128, NK2))
    B1G6 = din('b1g6', (6, 128, NK2))
    W26 = din('w26', (6, DIP, D))
    B26 = din('b26', (6, 128, 8))
    CON = {k: din('c_' + k, v.shape, F32 if v.dtype == np.float32 else BF16) for k, v in consts.items() if k not in ('identf', 'onescolf')}
    XSP = nc.dram_tensor('xspill', [128, NC * T], F32, kind='Internal').ap()
    WFCBF = nc.dram_tensor('wfcbf', [D, 4096], BF16, kind='Internal').ap()
    WPRBF = nc.dram_tensor('wprbf', [4096, D], BF16, kind='Internal').ap()
    WQ6BF = nc.dram_tensor('wq6bf', [6, D, 3 * D], BF16, kind='Internal').ap()
    WO6BF = nc.dram_tensor('wo6bf', [6, D, D], BF16, kind='Internal').ap()
    W1A6BF = nc.dram_tensor('w1a6bf', [6, D, DIP], BF16, kind='Internal').ap()
    W1G6BF = nc.dram_tensor('w1g6bf', [6, D, DIP], BF16, kind='Internal').ap()
    W26BF = nc.dram_tensor('w26bf', [6, DIP, D], BF16, kind='Internal').ap()
    tap_toks = []

    def tap(name, ap, waits):
        if name not in taps:
            return
        tp = nc.dram_tensor('tap_' + name, list(ap.shape), ap.dtype, kind='ExternalOutput').ap()
        P.do('y', lambda e, o=tp, i=ap: e.dma_start(out=o, in_=i), waits, dma=True)
        tap_toks.append(P.tok('y'))
    sbp = lambda n, sh, dt: er.enter_context(nc.sbuf_tensor(n, list(sh), dt))
    xn = sbp('xn', (128, NC * WZ), BF16)
    gb = sbp('gb', (128, NC * T), BF16)
    CB = {k: sbp('cb_' + k, v.shape, F32 if v.dtype == np.float32 else BF16) for k, v in consts.items() if not k.startswith('rot') and k not in ('identf', 'onescolf')}
    ST = [sbp(f'st{i}', (128, 16), F32) for i in range(8)]
    sqw = sbp('sqw', (128, 1024), BF16)
    cpool = sbp('cpool', (128, 8), F32)
    CB['cpool'] = cpool
    for _ci, _cv in ((0, 1e-05), (1, 1e-06), (2, 0.2), (3, -1.0), (4, 1.0 / 1024)):
        P.do('g', lambda e, i=_ci, v=_cv, cpool=cpool: e.memset(cpool[:, i:i + 1], v))
    tok_cpool = P.tok('g')
    ctoks = []
    for k in CON:
        if k.startswith('rot'):
            continue
        P.do('y', lambda e, d=CB[k], s=CON[k]: e.dma_start(out=d[:], in_=s), dma=True)
        ctoks.append(P.tok('y'))
    tok_con = ctoks[-1]
    ident = CB['ident'][:]

    def zc(base, c, t0, t1, shift=False):
        off = 0 if shift else 1
        return base[:, c * WZ + off + t0:c * WZ + off + t1]

    class WBuf:
        def __init__(self, es, name, cols, n_stg=2, n_bf=2, engs=('g', 'v')):
            self.stg = [es.enter_context(nc.sbuf_tensor(_U(name + 'S'), [128, cols], F32))
                        for _ in range(n_stg)]
            self.bf = [es.enter_context(nc.sbuf_tensor(_U(name + 'B'), [128, cols], BF16))
                       for _ in range(n_bf)]
            self.stg_free = [None] * n_stg
            self.cast_tok = [None] * n_bf
            self.busy = [[] for _ in range(n_bf)]
            self.i = 0
            self.j = 0
            self.engs = engs
            self.cols = cols

        def load(self, src2d, extra=(), w=None):
            si = self.j % len(self.stg)
            i = self.i
            self.i = (self.i + 1) % len(self.bf)
            self.j += 1
            w = w if w is not None else self.cols
            tw = dma_ch(self.stg[si][:, 0:w], src2d,
                        [self.stg_free[si]] + list(extra))
            eng = self.engs[i % len(self.engs)]
            P.do(eng, lambda e, d=self.bf[i], s=self.stg[si], w=w:
                 e.tensor_copy(d[:, 0:w], s[:, 0:w]), [tw] + self.busy[i])
            self.cast_tok[i] = P.tok(eng)
            self.stg_free[si] = self.cast_tok[i]
            self.busy[i] = []
            self._last = i
            return self.bf[i], self.cast_tok[i]

        def load_bf(self, src2d_bf, extra=(), w=None):
            i = self.i
            self.i = (self.i + 1) % len(self.bf)
            w = w if w is not None else self.cols
            cc = src2d_bf.shape[0] // 128
            srcv = src2d_bf.rearrange('(c p) n -> p c n', p=128)
            dv = self.bf[i][:, 0:w].rearrange('p (c n) -> p c n', c=cc)
            P.do('y', lambda e, d=dv, s=srcv: e.dma_start(out=d, in_=s),
                 self.busy[i] + list(extra), dma=True)
            self.cast_tok[i] = P.tok('y')
            self.busy[i] = []
            self._last = i
            return self.bf[i], self.cast_tok[i]

        def writeback(self, dst2d_bf, after, w=None):
            i = self._last
            w = w if w is not None else self.cols
            cc = dst2d_bf.shape[0] // 128
            dstv = dst2d_bf.rearrange('(c p) n -> p c n', p=128)
            sv = self.bf[i][:, 0:w].rearrange('p (c n) -> p c n', c=cc)
            P.do('y', lambda e, d=dstv, s=sv: e.dma_start(out=d, in_=s),
                 list(after), dma=True)
            tok = P.tok('y')
            self.busy[i].append(tok)
            return tok

        def consumed(self, tok):
            self.busy[self._last].append(tok)

    def dma_ch(dst, src2d, waits=()):
        cc = src2d.shape[0] // 128
        src = src2d.rearrange('(c p) n -> p c n', p=128)
        dv = dst.rearrange('p (c n) -> p c n', c=cc)
        P.do('y', lambda e, d=dv, s=src: e.dma_start(out=d, in_=s), waits, dma=True)
        return P.tok('y')

    def _ln_combine(ntiles, eps, tv, ts_):
        su, ssq, meanb, varb, rstd, m2s, vr2, mr = [s[:] for s in ST]
        sl = slice(0, ntiles)
        P.do('v', lambda e, varb=varb, sl=sl, ssq=ssq: e.tensor_scalar(varb[:, sl], ssq[:, sl], 1.0 / 1024, None, ALU.mult), [ts_])
        P.do('s', lambda e, meanb=meanb, sl=sl, su=su, cpool=cpool: e.activation(meanb[:, sl], su[:, sl], AF.Identity, scale=cpool[:, 4:5]), [tv, tok_cpool])
        c2 = P.tok('s')
        P.do('v', lambda e, m2s=m2s, sl=sl, meanb=meanb: e.tensor_mul(m2s[:, sl], meanb[:, sl], meanb[:, sl]), [c2])
        c3 = P.tok('v')
        P.do('g', lambda e, vr2=vr2, sl=sl, varb=varb, m2s=m2s: e.tensor_sub(vr2[:, sl], varb[:, sl], m2s[:, sl]), [c3])
        c4 = P.tok('g')
        epcol = 0 if abs(eps - 1e-05) < 1e-12 else 1
        P.do('s', lambda e, ec=epcol, vr2=vr2, sl=sl, cpool=cpool: e.activation(vr2[:, sl], vr2[:, sl], AF.Sqrt, bias=cpool[:, ec:ec + 1]), [c4, tok_cpool])
        c5 = P.tok('s')
        P.do('v', lambda e, rstd=rstd, sl=sl, vr2=vr2: e.reciprocal(rstd[:, sl], vr2[:, sl]), [c5])
        c6 = P.tok('v')
        P.do('g', lambda e, mr=mr, sl=sl, meanb=meanb, rstd=rstd: e.tensor_tensor(mr[:, sl], meanb[:, sl], rstd[:, sl], ALU.mult), [c6, c2])
        c7 = P.tok('g')
        P.do('v', lambda e, m2s=m2s, sl=sl, mr=mr: e.tensor_scalar(m2s[:, sl], mr[:, sl], -1.0, None, ALU.mult), [c7])
        c8 = P.tok('v')
        return (c8, c6)

    def emit_ln_tm(src_tile, ntiles, dst_fn, tag, eps, waits=()):
        su, ssq, meanb, varb, rstd, m2s, vr2, mr = [s[:] for s in ST]
        out_toks = []
        with contextlib.ExitStack() as es:
            tmpb = [es.enter_context(nc.sbuf_tensor(_U(f'{tag}tm{i}'), [128, 1024], BF16)) for i in range(2)]
            bkp = [es.enter_context(nc.psum_tensor(_U(f'{tag}bk{i}'), [128, 1024], BF16)) for i in range(2)]
            tv = ts_ = None
            for a in range(ntiles):
                P.do('v', lambda e, aa=a, su=su, src_tile=src_tile: e.reduce_sum(su[:, aa:aa + 1], src_tile(aa), axis=AX.X), list(waits))
                tv = P.tok('v')
                P.do('s', lambda e, aa=a, sqw=sqw, src_tile=src_tile, ssq=ssq: e.activation(sqw[:], src_tile(aa), AF.Square, accum_out=ssq[:, aa:aa + 1]), list(waits))
                ts_ = P.tok('s')
            c8, c6 = _ln_combine(ntiles, eps, tv, ts_)
            bz = {}
            for a in range(ntiles):
                b = a % 2
                P.do('s', lambda e, aa=a, bb=b, tmpb=tmpb, src_tile=src_tile, m2s=m2s, rstd=rstd: e.activation(tmpb[bb][:], src_tile(aa), AF.Identity, bias=m2s[:, aa:aa + 1], scale=rstd[:, aa:aa + 1]), [c8, c6, bz.get((b, 'e'))])
                ta = P.tok('s')
                for c in range(NC):
                    P.do('t', lambda e, cc=c, bb=b, bkp=bkp, tmpb=tmpb, ident=ident: e.transpose(bkp[bb][:, cc * 128:(cc + 1) * 128], tmpb[bb][:, cc * 128:(cc + 1) * 128], ident), [ta, tok_con, bz.get((b, 'e'))] if c == 0 else [])
                tb = P.tok('t')
                P.do('v', lambda e, d=dst_fn(a), bb=b, bkp=bkp: e.tensor_copy(d, bkp[bb][:].rearrange('p (c w) -> p c w', c=NC)), [tb])
                te = P.tok('v')
                bz[b, 'e'] = te
                out_toks.append(te)
        return out_toks

    def emit_ln_fm(src_col, ntiles, dst_fn, tag, eps, waits=()):
        su, ssq, meanb, varb, rstd, m2s, vr2, mr = [s[:] for s in ST]
        out_toks = []
        with contextlib.ExitStack() as es:
            cst = [es.enter_context(nc.sbuf_tensor(_U(f'{tag}cs{i}'), [128, 1024], BF16)) for i in range(2)]
            trp = [es.enter_context(nc.psum_tensor(_U(f'{tag}tr{i}'), [128, 1024], BF16)) for i in range(2)]
            tmpb = [es.enter_context(nc.sbuf_tensor(_U(f'{tag}tm{i}'), [128, 1024], BF16)) for i in range(2)]
            bkp = [es.enter_context(nc.psum_tensor(_U(f'{tag}bk{i}'), [128, 1024], BF16)) for i in range(2)]
            bz = {}
            tv = ts_ = None

            def cast_and_transpose(a, b, w0):
                for c in range(NC):
                    P.do('v', lambda e, aa=a, cc=c, bb=b, cst=cst, src_col=src_col: e.tensor_copy(cst[bb][:, cc * 128:(cc + 1) * 128], src_col(aa, cc)), w0 if c == 0 else [])
                tcc = P.tok('v')
                for c in range(NC):
                    P.do('t', lambda e, cc=c, bb=b, trp=trp, cst=cst, ident=ident: e.transpose(trp[bb][:, cc * 128:(cc + 1) * 128], cst[bb][:, cc * 128:(cc + 1) * 128], ident), [tcc, tok_con] + w0 if c == 0 else [])
                return P.tok('t')
            for a in range(ntiles):
                b = a % 2
                w0 = list(waits) + [bz.get((b, 'v')), bz.get((b, 's'))]
                tt = cast_and_transpose(a, b, w0)
                P.do('v', lambda e, aa=a, bb=b, su=su, trp=trp: e.reduce_sum(su[:, aa:aa + 1], trp[bb][:], axis=AX.X), [tt])
                tv = P.tok('v')
                P.do('s', lambda e, bb=b, aa=a, sqw=sqw, trp=trp, ssq=ssq: e.activation(sqw[:], trp[bb][:], AF.Square, accum_out=ssq[:, aa:aa + 1]), [tt, tv])
                ts_ = P.tok('s')
                bz[b, 'v'] = tv
                bz[b, 's'] = ts_
            c8, c6 = _ln_combine(ntiles, eps, tv, ts_)
            bz2 = {}
            for a in range(ntiles):
                b = a % 2
                w2 = [c8, c6, bz.get((b, 'v')), bz.get((b, 's')), bz2.get((b, 'a'))]
                tt = cast_and_transpose(a, b, w2)
                P.do('s', lambda e, aa=a, bb=b, tmpb=tmpb, trp=trp, m2s=m2s, rstd=rstd: e.activation(tmpb[bb][:], trp[bb][:], AF.Identity, bias=m2s[:, aa:aa + 1], scale=rstd[:, aa:aa + 1]), [tt])
                ta = P.tok('s')
                bz[b, 'v'] = None
                bz[b, 's'] = ta
                for c in range(NC):
                    P.do('t', lambda e, cc=c, bb=b, bkp=bkp, tmpb=tmpb, ident=ident: e.transpose(bkp[bb][:, cc * 128:(cc + 1) * 128], tmpb[bb][:, cc * 128:(cc + 1) * 128], ident), [ta, bz2.get((b, 'e'))] if c == 0 else [])
                tb = P.tok('t')
                P.do('v', lambda e, d=dst_fn(a), bb=b, bkp=bkp: e.tensor_copy(d, bkp[bb][:].rearrange('p (c w) -> p c w', c=NC)), [tb])
                te = P.tok('v')
                bz2[b, 'a'] = ta
                bz2[b, 'e'] = te
                out_toks.append(te)
        return out_toks

    def attn_head(tag, hs, q_ap, k_ap, vtm_ap, p_sb, pt_sb, nmx, sums, rcs, ao_write, qt_list, q0_of, keyspec, gates, alibi_ap=None, alibi_base=None, mask_fn=None):
        he = hs + 64
        tp = (0, 0) if hs == 0 else (64, 0)
        max_klen = max(keyspec(q)[1] for q in qt_list)
        CW = 512 if max_klen <= 512 else 1024
        with contextlib.ExitStack() as eA:
            sps = [eA.enter_context(nc.psum_tensor(_U(f'{tag}sp{i}'), [128, CW], F32)) for i in range(2)]
            ptp = [eA.enter_context(nc.psum_tensor(_U(f'{tag}pp{i}'), [128, CW], BF16)) for i in range(2)]
            avp = [eA.enter_context(nc.psum_tensor(_U(f'{tag}av{i}'), [128, 128], F32)) for i in range(2)]
            nm2 = eA.enter_context(nc.sbuf_tensor(_U(tag + 'nm'), [128, 8], F32))
            sm2 = eA.enter_context(nc.sbuf_tensor(_U(tag + 'sm'), [128, 8], F32))
            sps_busy = [gates.get('prev_exp')] * 2
            ptp_busy = [gates.get('prev_ptev')] * 2
            avp_busy = [gates.get('prev_ao')] * 2
            nm_busy = [None] * 2
            tmp_busy = [None] * 2
            sm_busy = [None] * 2
            tpt_last = gates.get('prev_tpt')
            av_last = gates.get('prev_tav')
            first = True
            pend = None

            def stage_b2(stq):
                nonlocal tpt_last, av_last
                qt, kb, klen, chs = stq['qt'], stq['kb'], stq['klen'], stq['chs']
                qp = qt % 2
                ptevs = []
                for ci, ch in enumerate(chs):
                    c0, clen, _, _ = ch
                    pp = (qt + ci) % 2
                    for kt in range(clen // 128):
                        P.do('t', lambda e, kl=kt * 128, cc0=c0, pp=pp, ptp=ptp, p_sb=p_sb, ident=ident: e.transpose(ptp[pp][:, kl:kl + 128], p_sb[:, cc0 + kl:cc0 + kl + 128], ident), [stq['norm'][ci], ptp_busy[pp], tok_con] if kt == 0 else [])
                    tpt_last = P.tok('t')
                    P.do('v', lambda e, cc0=c0, cl=clen, pp=pp, pt_sb=pt_sb, ptp=ptp: e.tensor_copy(pt_sb[:, cc0:cc0 + cl], ptp[pp][:, 0:cl]), [tpt_last, av_last])
                    ptp_busy[pp] = P.tok('v')
                    ptevs.append(ptp_busy[pp])
                gates['prev_ptev'] = ptevs[-1]
                nkt = klen // 128
                for ci, ch in enumerate(chs):
                    c0, clen, _, _ = ch
                    for kl in range(clen // 128):
                        kt = c0 // 128 + kl
                        w_ = ([ptevs[ci], avp_busy[qp]] + list(gates.get('vtm', []))) if kl == 0 else []
                        P.do('t', lambda e, k0=kt * 128, kb0=kb, first_=kt == 0, last=kt == nkt - 1, qp=qp, avp=avp, hs=hs, he=he, vtm_ap=vtm_ap, pt_sb=pt_sb: e.matmul(avp[qp][hs:he, :], vtm_ap[:, kb0 + k0 + hs:kb0 + k0 + hs + 64], pt_sb[:, k0:k0 + 128], start=first_, stop=last, tile_position=(0, hs)), w_)
                av_last = P.tok('t')
                gates['prev_tav'] = av_last
                avp_busy[qp] = ao_write(qt, avp[qp], [av_last])
                gates['prev_ao'] = avp_busy[qp]
            for qt in qt_list:
                kb, klen = keyspec(qt)
                q0 = q0_of(qt)
                nch = (klen + CW - 1) // CW
                p2 = 2 * (qt % 2)
                chs = []
                for c in range(nch):
                    c0 = c * CW
                    clen = min(CW, klen - c0)
                    par = (qt + c) % 2
                    sp_ = sps[par]
                    w0 = [sps_busy[par]]
                    if first:
                        w0 += list(gates.get('qkv', []))
                    for ks in range((clen + 511) // 512):
                        n0 = ks * 512
                        n1 = min(clen, n0 + 512)
                        P.do('t', lambda e, a0=n0, a1=n1, g0=c0, qq=q0, kb0=kb, sp=mask_fn is None, sp_=sp_, q_ap=q_ap, hs=hs, he=he, k_ap=k_ap, tp=tp: e.matmul(sp_[:, a0:a1], q_ap[hs:he, qq:qq + 128], k_ap[hs:he, kb0 + g0 + a0:kb0 + g0 + a1], start=True, stop=sp, tile_position=tp), w0 if ks == 0 else [])
                    if mask_fn is not None:
                        mask_fn(qt, sp_)
                    tsc = P.tok('t')
                    first = False
                    mcol = nm2[:, p2 + c:p2 + c + 1]
                    wmn = [tsc] + list(nm_busy[qt % 2] or [])
                    if alibi_ap is not None:
                        b0 = alibi_base(qt)
                        P.do('v', lambda e, cl=clen, bb=b0 + c0, sp_=sp_, alibi_ap=alibi_ap: e.scalar_tensor_tensor(sp_[:, 0:cl], sp_[:, 0:cl], 1.0, alibi_ap[:, bb:bb + cl], ALU.mult, ALU.add), wmn + list(gates.get('alibi', [])))
                        P.do('v', lambda e, cl=clen, mc=mcol, sp_=sp_: e.tensor_scalar(sp_[:, 0:cl], sp_[:, 0:cl], 1.0, -3e+38, ALU.mult, ALU.max, accum_out=mc))
                    else:
                        P.do('v', lambda e, cl=clen, mc=mcol, sp_=sp_: e.tensor_scalar(sp_[:, 0:cl], sp_[:, 0:cl], 1.0, -3e+38, ALU.mult, ALU.max, accum_out=mc), wmn)
                    chs.append((c0, clen, par, P.tok('v')))
                if nch > 1:
                    tg2 = []
                    for c in range(2):
                        P.do('g', lambda e, pp2=p2, cc=c, nm2=nm2: e.tensor_scalar(nm2[:, 4 + pp2 + cc:5 + pp2 + cc], nm2[:, pp2 + cc:pp2 + cc + 1], -1.0, None, ALU.mult), [chs[c][3], tmp_busy[qt % 2]])
                        tg2.append(P.tok('g'))
                    P.do('v', lambda e, col=qt, pp2=p2, nmx=nmx, nm2=nm2: e.tensor_tensor(nmx[:, col:col + 1], nm2[:, 4 + pp2:5 + pp2], nm2[:, 5 + pp2:6 + pp2], ALU.min), tg2)
                    nm_busy[qt % 2] = tg2
                    tmp_busy[qt % 2] = P.tok('v')
                    tneg = tmp_busy[qt % 2]
                else:
                    P.do('g', lambda e, col=qt, pp2=p2, nmx=nmx, nm2=nm2: e.tensor_scalar(nmx[:, col:col + 1], nm2[:, pp2:pp2 + 1], -1.0, None, ALU.mult), [chs[0][3]])
                    nm_busy[qt % 2] = [P.tok('g')]
                    tneg = nm_busy[qt % 2][0]
                if pend is not None:
                    stage_b2(pend)
                exps = []
                norm = []
                for ci, ch in enumerate(chs):
                    c0, clen, par, mtok = ch
                    scol = sums[:, qt:qt + 1] if nch == 1 else sm2[:, p2 + ci:p2 + ci + 1]
                    we = [mtok, tneg, tpt_last, sm_busy[qt % 2] if nch > 1 else None]
                    P.do('s', lambda e, cc0=c0, cl=clen, col=qt, sc=scol, pp=par, p_sb=p_sb, sps=sps, nmx=nmx: e.activation(p_sb[:, cc0:cc0 + cl], sps[pp][:, 0:cl], AF.Exp, bias=nmx[:, col:col + 1], accum_out=sc), we)
                    sps_busy[par] = P.tok('s')
                    exps.append(sps_busy[par])
                gates['prev_exp'] = exps[-1]
                if nch > 1:
                    P.do('g', lambda e, col=qt, pp2=p2, sums=sums, sm2=sm2: e.tensor_tensor(sums[:, col:col + 1], sm2[:, pp2:pp2 + 1], sm2[:, pp2 + 1:pp2 + 2], ALU.add), exps)
                    sm_busy[qt % 2] = P.tok('g')
                    trc0 = sm_busy[qt % 2]
                else:
                    trc0 = exps[-1]
                P.do('v', lambda e, col=qt, rcs=rcs, sums=sums: e.reciprocal(rcs[:, col:col + 1], sums[:, col:col + 1]), [trc0])
                trc = P.tok('v')
                for ci, ch in enumerate(chs):
                    c0, clen, par, _ = ch
                    P.do('s', lambda e, cc0=c0, cl=clen, col=qt, p_sb=p_sb, rcs=rcs: e.activation(p_sb[:, cc0:cc0 + cl], p_sb[:, cc0:cc0 + cl], AF.Identity, scale=rcs[:, col:col + 1]), [trc])
                    norm.append(P.tok('s'))
                pend = dict(qt=qt, kb=kb, klen=klen, chs=chs, norm=norm)
            stage_b2(pend)
            gates['prev_tpt'] = tpt_last
    fin = []
    with contextlib.ExitStack() as EA:
        x_sb = EA.enter_context(nc.sbuf_tensor(_U('x0'), [128, NT * 1024], F32))
        P.do('y', lambda e, x_sb=x_sb, X_in=X_in: e.dma_start(out=x_sb[:].rearrange('p (a d) -> p a d', a=NT), in_=X_in.rearrange('(a p) d -> p a d', p=128)), dma=True)
        tok_x = P.tok('y')
        P.do('g', lambda e, xn=xn: e.memset(xn[:].rearrange('p (c w) -> p c w', c=NC)[:, :, 0:1], 0.0))
        tok_xnz = P.tok('g')
        xtile = lambda a: x_sb[:, a * 1024:(a + 1) * 1024]
        xn_dst = lambda a: xn[:].rearrange('p (c w) -> p c w', c=NC)[:, :, 1 + a * 128:1 + (a + 1) * 128]
        ln0 = emit_ln_tm(xtile, NT, xn_dst, 'l0', EPS0, waits=[tok_x])
        xn_gate = ln0 + [tok_xnz]
        tap('xn0', xn[:], xn_gate)
        P.barrier()
        tok_g0 = None
        if upto not in ('ln',):
            with contextlib.ExitStack() as ef:
                wb = WBuf(ef, 'fw', 4096, n_stg=1, n_bf=2)
                h_bf = ef.enter_context(nc.sbuf_tensor(_U('fh'), [128, 32 * 512], BF16))
                lanes = []
                for ln_ in range(2):
                    lanes.append(dict(
                        yt=ef.enter_context(nc.sbuf_tensor(_U(f'fy{ln_}'), [128, 256], F32)),
                        sp=ef.enter_context(nc.sbuf_tensor(_U(f'fs{ln_}'), [128, 256], F32)),
                        m0=ef.enter_context(nc.sbuf_tensor(_U(f'fm{ln_}a'), [128, 256], F32)),
                        fa=ef.enter_context(nc.sbuf_tensor(_U(f'fa{ln_}'), [128, 256], F32)),
                        free=None))
                t_wfree = xn_gate[-1]
                t_bffree = None
                t_ycopy = None
                t_spk = None
                for q in range(4):
                    tq0 = q * 512
                    t_hev = {}
                    t_hfree = t_ycopy if t_ycopy is not None else t_bffree
                    eh = contextlib.ExitStack()
                    hps = [eh.enter_context(nc.psum_tensor(_U(f'fhp{q}_{i}'), [128, 512], F32)) for i in range(2)]
                    for fs in range(8):
                        if q == 0:
                            wbf, tc = wb.load(WFC[:, fs * 512:(fs + 1) * 512],
                                              [t_wfree] if fs == 0 else ())
                            wb.writeback(WFCBF[:, fs * 512:(fs + 1) * 512], [tc])
                        else:
                            wbf, tc = wb.load_bf(WFCBF[:, fs * 512:(fs + 1) * 512])
                        t_wfree = None
                        last_mm = None
                        for m in range(4):
                            fc = fs * 4 + m
                            b = fc % 2
                            w0 = [tc, t_hev.get(b), t_hfree] + xn_gate
                            for k in range(NC):
                                P.do('t', lambda e, kk=k, mm2=m, bb=b, t0=tq0, hps=hps, wbf=wbf, zc=zc, xn=xn: e.matmul(hps[bb][:], wbf[:, kk * 512 + mm2 * 128:kk * 512 + (mm2 + 1) * 128], zc(xn, kk, t0, t0 + 512), start=kk == 0, stop=kk == NC - 1), w0 if k == 0 else [])
                            last_mm = P.tok('t')
                            wb.consumed(last_mm)
                            P.do('s', lambda e, f=fc, bb=b, h_bf=h_bf, hps=hps: e.activation(h_bf[:, f * 512:(f + 1) * 512], hps[bb][:], AF.Relu), [last_mm])
                            tr = P.tok('s')
                            P.do('v', lambda e, f=fc, h_bf=h_bf: e.tensor_mul(h_bf[:, f * 512:(f + 1) * 512], h_bf[:, f * 512:(f + 1) * 512], h_bf[:, f * 512:(f + 1) * 512]), [tr])
                            t_hev[b] = P.tok('v')
                        t_bffree = last_mm
                    th_all = [t_hev.get(0), t_hev.get(1)]
                    eh.close()
                    with contextlib.ExitStack() as ey:
                        yps = [ey.enter_context(nc.psum_tensor(_U(f'fyp{q}_{i}'), [128, 512], F32)) for i in range(8)]
                        t_ymm = {}
                        for fs in range(8):
                            if q == 0:
                                wbf, tc = wb.load(WPR2[fs * 512:(fs + 1) * 512, :])
                                wb.writeback(WPRBF[fs * 512:(fs + 1) * 512, :], [tc])
                            else:
                                wbf, tc = wb.load_bf(WPRBF[fs * 512:(fs + 1) * 512, :])
                            last_mm = None
                            for of in range(8):
                                w0 = [tc] + (th_all if fs == 0 else [])
                                for m in range(4):
                                    fc = fs * 4 + m
                                    P.do('t', lambda e, oo=of, mm2=m, f=fc, yps=yps, wbf=wbf, h_bf=h_bf: e.matmul(yps[oo][:], wbf[:, mm2 * 1024 + oo * 128:mm2 * 1024 + (oo + 1) * 128], h_bf[:, f * 512:(f + 1) * 512], start=f == 0, stop=f == 31), w0 if m == 0 else [])
                                last_mm = P.tok('t')
                                t_ymm[of] = last_mm
                            wb.consumed(last_mm)
                            t_bffree = last_mm
                        for of in range(8):
                            for hf in range(2):
                                ln_ = (of * 2 + hf) % 2
                                eng = ('g', 'v')[ln_]
                                L = lanes[ln_]
                                yt, spk, facc = (L['yt'], L['sp'], L['fa'])
                                P.do('s', lambda e, oo=of, h0=hf * 256, y_t=yt, yps=yps: e.copy(y_t[:], yps[oo][:, h0:h0 + 256]), [t_ymm[of], L['free']])
                                ty = P.tok('s')
                                P.do(eng, lambda e, spk=spk, y_t=yt: e.tensor_scalar(spk[:], y_t[:], 1.0, None, ALU.is_gt), [ty])
                                P.do(eng, lambda e, facc=facc, spk=spk: e.tensor_copy(facc[:], spk[:]))
                                for si in range(4):
                                    a_ = yt if si == 0 else L['m0']
                                    b_ = L['m0']
                                    P.do(eng, lambda e, aa=a_, bb=b_: e.tensor_scalar(bb[:], aa[:], 0.9, None, ALU.mult))
                                    P.do(eng, lambda e, bb=b_, y_t=yt: e.tensor_add(bb[:], bb[:], y_t[:]))
                                    P.do(eng, lambda e, bb=b_, spk=spk: e.tensor_sub(bb[:], bb[:], spk[:]))
                                    P.do(eng, lambda e, bb=b_, spk=spk: e.tensor_scalar(spk[:], bb[:], 1.0, None, ALU.is_gt))
                                    P.do(eng, lambda e, facc=facc, spk=spk: e.tensor_add(facc[:], facc[:], spk[:]))
                                tg = P.tok(eng)
                                P.do('s', lambda e, oo=of, t0=tq0 + hf * 256, gb=gb, facc=facc, cpool=cpool: e.activation(gb[:, oo * T + t0:oo * T + t0 + 256], facc[:], AF.Silu, scale=cpool[:, 2:3]), [tg, tok_cpool])
                                L['free'] = P.tok('s')
                                t_spk = L['free']
                                t_ycopy = ty
                tok_g0 = t_spk
            tap('g0', gb[:], [tok_g0])
        P.barrier()
        if upto not in ('ln', 'ffn0'):
            slopes = alibi_slopes(16)
            with contextlib.ExitStack() as ea:
                ao = ea.enter_context(nc.sbuf_tensor(_U('ao'), [128, 4 * T], BF16))
                nmx = ea.enter_context(nc.sbuf_tensor(_U('anm'), [128, 16], F32))
                sums = ea.enter_context(nc.sbuf_tensor(_U('asm'), [128, 16], F32))
                rcs = ea.enter_context(nc.sbuf_tensor(_U('arc'), [128, 16], F32))
                t_xadd = {}
                tok_g_pg = []
                for pg in range(2):
                    P.barrier()
                    tok_ao_pg = []
                    with contextlib.ExitStack() as ep:
                        relm = ep.enter_context(nc.sbuf_tensor(_U('arel'), [128, 2688], BF16))
                        am = ep.enter_context(nc.sbuf_tensor(_U('aam'), [128, 2688], BF16))
                        wba = WBuf(ep, 'aw', 1024)
                        q_sb = ep.enter_context(nc.sbuf_tensor(_U('aq'), [128, T], BF16))
                        k_sb = ep.enter_context(nc.sbuf_tensor(_U('ak'), [128, T], BF16))
                        v_sb = ep.enter_context(nc.sbuf_tensor(_U('av_'), [128, T], BF16))
                        vtm = ep.enter_context(nc.sbuf_tensor(_U('avt'), [128, T], BF16))
                        pt_sb = ep.enter_context(nc.sbuf_tensor(_U('apt'), [128, T], BF16))
                        P.do('g', lambda e, relm=relm: e.iota(relm[:], [[1, 2688]], base=-2176, channel_multiplier=-1, allow_small_or_imprecise_dtypes=True), [tok_g0] if tok_g0 else [])
                        P.do('g', lambda e, am=am, relm=relm: e.tensor_scalar(am[:], relm[:], 0.0, None, ALU.is_gt))
                        P.do('g', lambda e, am=am: e.tensor_scalar(am[:], am[:], NEG, None, ALU.mult))
                        P.do('g', lambda e, relm=relm, am=am: e.tensor_add(relm[:], relm[:], am[:]))
                        tok_rel = P.tok('g')
                        t_wfree = tok_g0
                        t_qkvfree = None
                        gates = {}
                        for pi in range(4):
                            pr = pg * 4 + pi
                            p_sb = v_sb
                            tjs = []
                            last_wmm = t_qkvfree
                            for j, off in enumerate((pr * 128, D + pr * 128, 2 * D + pr * 128)):
                                wbf, tc = wba.load(WATT[:, off:off + 128],
                                                   [t_wfree] if t_wfree else ())
                                t_wfree = None
                                dst, scl = ((q_sb, 0.125), (k_sb, None), (v_sb, None))[j]
                                with contextlib.ExitStack() as eq:
                                    qps = [eq.enter_context(nc.psum_tensor(_U(f'aqp{pr}_{j}_{i}'), [128, 512], F32)) for i in range(2)]
                                    tev = {}
                                    for sp in range(4):
                                        b = sp % 2
                                        w0 = [tc, tev.get(b)] + xn_gate + ([gates.get('prev_exp'), gates.get('prev_tav'), gates.get('prev_ao'), gates.get('prev_ptev')] if j == 0 else [])
                                        for k in range(NC):
                                            P.do('t', lambda e, kk=k, ss=sp, bb=b, qps=qps, wbf=wbf, zc=zc, xn=xn: e.matmul(qps[bb][:], wbf[:, kk * 128:(kk + 1) * 128], zc(xn, kk, ss * 512, (ss + 1) * 512), start=kk == 0, stop=kk == NC - 1), w0 if k == 0 else [])
                                        tm = P.tok('t')
                                        last_wmm = tm
                                        wba.consumed(tm)
                                        if scl is not None:
                                            P.do('v', lambda e, d=dst, ss=sp, bb=b, s_=scl, qps=qps: e.tensor_scalar(d[:, ss * 512:(ss + 1) * 512], qps[bb][:], s_, None, ALU.mult), [tm])
                                            tev[b] = P.tok('v')
                                        else:
                                            P.do('s', lambda e, d=dst, ss=sp, bb=b, qps=qps: e.copy(d[:, ss * 512:(ss + 1) * 512], qps[bb][:]), [tm])
                                            tev[b] = P.tok('s')
                                    tjs += [tev.get(0), tev.get(1)]
                            t_qkvfree = last_wmm
                            with contextlib.ExitStack() as ev:
                                vtp = ev.enter_context(nc.psum_tensor(_U(f'avp{pr}'), [128, 2048], BF16))
                                for a in range(NT):
                                    P.do('t', lambda e, aa=a, vtp=vtp, v_sb=v_sb, ident=ident: e.transpose(vtp[:, aa * 128:(aa + 1) * 128], v_sb[:, aa * 128:(aa + 1) * 128], ident), tjs + [gates.get('prev_tav'), gates.get('prev_ao'), gates.get('prev_ptev')] if a == 0 else [])
                                tvt = P.tok('t')
                                P.do('v', lambda e, vtm=vtm, vtp=vtp: e.tensor_copy(vtm[:], vtp[:]), [tvt])
                                tok_vtm = P.tok('v')
                            if pr == 0:
                                tap('q0', q_sb[:], tjs)
                                tap('k0', k_sb[:], tjs)
                                tap('vtm0', vtm[:], [tok_vtm])
                            for hh in range(2):
                                head = pr * 2 + hh
                                P.do('g', lambda e, sl=slopes[head], am=am, relm=relm: e.tensor_scalar(am[:], relm[:], sl, None, ALU.mult), [tok_rel, gates.get('prev_exp')])
                                tok_am = P.tok('g')
                                gates['qkv'] = tjs + [tok_vtm]
                                gates['vtm'] = [tok_vtm]
                                gates['alibi'] = [tok_am]

                                def ao_w(qt, avp, w, pi=pi, hh=hh):
                                    P.do('v', lambda e, col=qt, h0=hh * 64, ao=ao, pi=pi, avp=avp: e.tensor_copy(ao[h0:h0 + 64, pi * T + col * 128:pi * T + (col + 1) * 128], avp[h0:h0 + 64, :]), w)
                                    tok_ao_pg.append(P.tok('v'))
                                    return tok_ao_pg[-1]
                                attn_head(f'a{pr}h{hh}', hh * 64, q_sb, k_sb, vtm, p_sb, pt_sb, nmx, sums, rcs, ao_w, list(range(NT)), lambda qt: qt * 128, lambda qt: (0, (qt + 1) * 128), gates, alibi_ap=am[:], alibi_base=lambda qt: 2176 - qt * 128)
                    tap(f'ao{pg}', ao[:], tok_ao_pg[-1:])
                    P.barrier()
                    with contextlib.ExitStack() as ew:
                        wstg2 = ew.enter_context(nc.sbuf_tensor(_U('pw'), [128, 4096], F32))
                        wbf2 = ew.enter_context(nc.sbuf_tensor(_U('pwb'), [128, 4096], BF16))
                        tmp = [ew.enter_context(nc.sbuf_tensor(_U(f'pt{i}'), [128, 512], BF16)) for i in range(2)]
                        pps = [ew.enter_context(nc.psum_tensor(_U(f'pp{i}'), [128, 512], F32)) for i in range(2)]
                        ptp = [ew.enter_context(nc.psum_tensor(_U(f'ptr{i}'), [128, 512], BF16)) for i in range(2)]
                        tev = {}
                        ttr = {}
                        cnt = 0
                        tw = dma_ch(wstg2[:], WPRJ[pg * 512:(pg + 1) * 512, :], tok_ao_pg[-1:])
                        P.do('g', lambda e, wbf2=wbf2, wstg2=wstg2: e.tensor_copy(wbf2[:], wstg2[:]), [tw])
                        tc = P.tok('g')
                        for of in range(8):
                            for sp in range(4):
                                b = cnt % 2
                                cnt += 1
                                w0 = [tc, tev.get(b)] + tok_ao_pg[-1:]
                                for k in range(4):
                                    P.do('t', lambda e, kk=k, oo=of, ss=sp, bb=b, pps=pps, wbf2=wbf2, ao=ao: e.matmul(pps[bb][:], wbf2[:, kk * 1024 + oo * 128:kk * 1024 + (oo + 1) * 128], ao[:, kk * T + ss * 512:kk * T + (ss + 1) * 512], start=kk == 0, stop=kk == 3), w0 if k == 0 else [])
                                tm = P.tok('t')
                                P.do('v', lambda e, bb=b, oo=of, ss=sp, tmp=tmp, pps=pps, gb=gb: e.tensor_tensor(tmp[bb][:], pps[bb][:], gb[:, oo * T + ss * 512:oo * T + (ss + 1) * 512], ALU.mult), [tm, tok_g0, ttr.get(bb if False else b)])
                                tv = P.tok('v')
                                tev[b] = tv
                                for j in range(4):
                                    P.do('t', lambda e, bb=b, jj=j, ptp=ptp, tmp=tmp, ident=ident: e.transpose(ptp[bb][:, jj * 128:(jj + 1) * 128], tmp[bb][:, jj * 128:(jj + 1) * 128], ident), [tv, t_xadd.get(b)] if j == 0 else [])
                                tt2 = P.tok('t')
                                ttr[b] = tt2
                                dstx = x_sb[:].rearrange('p (a d) -> p a d', a=NT)[:, sp * 4:sp * 4 + 4, of * 128:(of + 1) * 128]
                                P.do('v', lambda e, d=dstx, bb=b, ptp=ptp: e.tensor_tensor(d, d, ptp[bb][:].rearrange('p (j w) -> p j w', j=4), ALU.add), [tt2, tok_x])
                                t_xadd[b] = P.tok('v')
                    tok_g_pg = [t_xadd.get(0), t_xadd.get(1)]
            fin = [t for t in tok_g_pg if t]
            tap('xb0', x_sb[:], fin)
        if upto in ('ln', 'ffn0', 'b0'):
            if not fin:
                fin = xn_gate + ([tok_g0] if tok_g0 else [])
            P.do('y', lambda e, OUT=OUT, x_sb=x_sb: e.dma_start(out=OUT.rearrange('(a p) d -> p a d', p=128), in_=x_sb[:].rearrange('p (a d) -> p a d', a=NT)), fin + [tok_x], dma=True)
            tout = P.tok('y')
            P.do('g', lambda e: e.nop(), [tout] + tap_toks)
            P.do('y', lambda e: e.nop(), [tout] + tap_toks)
            return finalize(nc, P, er)
        P.barrier()
        ln1 = emit_ln_tm(xtile, NT, xn_dst, 'l1', EPS0, waits=fin)
        xn_gate = ln1
        tap('xn1', xn[:], ln1)
        P.do('y', lambda e, XSP=XSP, x_sb=x_sb: e.dma_start(out=XSP, in_=x_sb[:]), fin + ln1, dma=True)
        tok_spill = P.tok('y')
        fin = ln1 + [tok_spill]
    fin = emit_l1_ffn(nc, P, xn, gb, CB, ST, None, CON, tok_con, WQKV6, WO6, QS6, KS6, LNG6, LNB6, W1A6, W1G6, B1A6, B1G6, W26, B26, fin, tap, attn_head, emit_ln_fm, zc, dma_ch, upto, n_sub, WBuf, (WQ6BF, WO6BF, W1A6BF, W1G6BF, W26BF))
    if upto == 'ffn1':
        P.do('y', lambda e, OUT=OUT, XSP=XSP: e.dma_start(out=OUT.rearrange('t d -> (t d)'), in_=XSP.rearrange('p n -> (p n)')), fin, dma=True)
        fin = [P.tok('y')]
        P.do('g', lambda e: e.nop(), fin + tap_toks)
        P.do('y', lambda e: e.nop(), fin + tap_toks)
        return finalize(nc, P, er)
    emit_l1_attn(nc, P, xn, gb, CB, ST, None, tok_con, WQKV1, WO1, BQKV1, BO1, NQW, NKW, XSP, OUT, fin, tap, tap_toks, attn_head, zc, dma_ch, WBuf)
    return finalize(nc, P, er)

def finalize(nc, P, er=None):
    sems = {}
    with contextlib.ExitStack() as esem:
        for k in ('v', 's', 't', 'g', 'y', 'vd', 'sd', 'td', 'gd'):
            sems[k] = esem.enter_context(nc.semaphore('sem_' + k))
        with nc.Block() as block:
            P.emit(nc, block, sems)
    return nc

def emit_l1_ffn(nc, P, xn, gb, CB, ST, sqs, CON, tok_con, WQKV6, WO6, QS6, KS6, LNG6, LNB6, W1A6, W1G6, B1A6, B1G6, W26, B26, fin, tap, attn_head, emit_ln, zc, dma_ch, upto, n_sub, WBuf=None, WCACHE=None):
    ident = CB['ident'][:]
    with contextlib.ExitStack() as EZ:
        z_sb = EZ.enter_context(nc.sbuf_tensor(_U('z'), [128, NC * WZ], F32))
        nmx = EZ.enter_context(nc.sbuf_tensor(_U('znm'), [128, 16], F32))
        sums = EZ.enter_context(nc.sbuf_tensor(_U('zsm'), [128, 16], F32))
        rcs = EZ.enter_context(nc.sbuf_tensor(_U('zrc'), [128, 16], F32))
        lnbb = EZ.enter_context(nc.sbuf_tensor(_U('lnbb'), [128, 8], BF16))
        beffa = EZ.enter_context(nc.sbuf_tensor(_U('beffa'), [128, NK2], F32))
        beffg = EZ.enter_context(nc.sbuf_tensor(_U('beffg'), [128, NK2], F32))
        P.do('g', lambda e, z_sb=z_sb: e.memset(z_sb[:].rearrange('p (c w) -> p c w', c=NC)[:, :, 0:1], 0.0), fin)
        P.do('v', lambda e, z_sb=z_sb, xn=xn: e.tensor_copy(z_sb[:].rearrange('p (c w) -> p c w', c=NC)[:, :, 1:WZ], xn[:].rearrange('p (c w) -> p c w', c=NC)[:, :, 1:WZ]), fin)
        tz = P.tok('v')
        tzg = P.tok('g')
        z_gate = [tz, tzg]
        WQ6BF, WO6BF, W1A6BF, W1G6BF, W26BF = WCACHE
        pstg = [EZ.enter_context(nc.sbuf_tensor(_U(f'pws{i}'), [128, 1024], F32)) for i in range(2)]
        pbf0 = EZ.enter_context(nc.sbuf_tensor(_U('pwb0'), [128, 1024], BF16))
        pbf = [pbf0, pbf0]
        plg = EZ.enter_context(nc.sbuf_tensor(_U('plg'), [128, 8], F32))
        pst = {'f': [None, None], 'b': [None, None], 'lg': None, 'i': 0}

        def prep_l1f(li):
            def slab(src2d, dst2d, scale_col, extra=()):
                i = pst['i'] % 2
                pst['i'] += 1
                n = src2d.shape[1]
                P.do('g', lambda e, d=pstg[i][:, 0:n], s=src2d: e.dma_start(out=d, in_=s), [pst['f'][i]] + list(extra), dma=True)
                tin = P.tok('g')
                if scale_col is not None:
                    P.do('g', lambda e, ii=i, n=n, sc=scale_col, pbf=pbf, pstg=pstg: e.tensor_scalar(pbf[ii][:, 0:n], pstg[ii][:, 0:n], 1.0, sc, ALU.mult, ALU.mult), [tin, pst['b'][0], pst['b'][1]])
                else:
                    P.do('g', lambda e, ii=i, n=n, pbf=pbf, pstg=pstg: e.tensor_copy(pbf[ii][:, 0:n], pstg[ii][:, 0:n]), [tin, pst['b'][0], pst['b'][1]])
                tcst = P.tok('g')
                pst['f'][i] = tcst
                P.do('g', lambda e, d=dst2d, s=pbf[i][:, 0:n]: e.dma_start(out=d, in_=s), [tcst], dma=True)
                pst['b'][i] = P.tok('g')
                return tcst
            out = {}

            def part_qo():
                for r in range(0, D, 128):
                    for c0 in range(0, 3 * D, 1024):
                        slab(WQKV6[li][r:r + 128, c0:c0 + 1024], WQ6BF[li][r:r + 128, c0:c0 + 1024], None)
                out['wq'] = pst['b'][(pst['i'] + 1) % 2]
                for r in range(0, D, 128):
                    slab(WO6[li][r:r + 128, :], WO6BF[li][r:r + 128, :], None)
                out['wo'] = pst['b'][(pst['i'] + 1) % 2]

            def part_w1a():
                P.do('g', lambda e, d=plg, s=LNG6[li]: e.dma_start(out=d[:], in_=s), [pst['lg']], dma=True)
                tlg = P.tok('g')
                for dk in range(8):
                    for c0 in range(0, DIP, 1024):
                        w = min(1024, DIP - c0)
                        ex = [tlg] if dk == 0 and c0 == 0 else ()
                        pst['lsc'] = slab(W1A6[li][dk * 128:(dk + 1) * 128, c0:c0 + w], W1A6BF[li][dk * 128:(dk + 1) * 128, c0:c0 + w], plg[:, dk:dk + 1], ex)

            def part_w1g():
                for dk in range(8):
                    for c0 in range(0, DIP, 1024):
                        w = min(1024, DIP - c0)
                        pst['lsc'] = slab(W1G6[li][dk * 128:(dk + 1) * 128, c0:c0 + w], W1G6BF[li][dk * 128:(dk + 1) * 128, c0:c0 + w], plg[:, dk:dk + 1])
                pst['lg'] = pst['lsc']
                out['w1'] = pst['b'][(pst['i'] + 1) % 2]

            def part_w2():
                for r in range(0, DIP, 128):
                    slab(W26[li][r:r + 128, :], W26BF[li][r:r + 128, :], None)
                out['w2'] = pst['b'][(pst['i'] + 1) % 2]
            return out, [part_qo, part_w1a, part_w1g, part_w2]
        out0, parts0 = prep_l1f(0)
        for _p in parts0:
            _p()
        preps = [out0]
        nsteps = int(round(n_sub * 2))
        for li in range((nsteps + 1) // 2):
            P.do('v', lambda e, gb=gb, z_sb=z_sb: e.tensor_copy(gb[:].rearrange('p (c w) -> p c w', c=NC), z_sb[:].rearrange('p (c w) -> p c w', c=NC)[:, :, 1:WZ]), z_gate)
            t_zbf = P.tok('v')

            def zb_rhs(k, t0, t1):
                if k < 4:
                    return gb[:, k * T + t0:k * T + t1]
                if t0 == 0:
                    return gb[:, k * T + 0:k * T + t1 - 1]
                return gb[:, k * T + t0 - 1:k * T + t1 - 1]
            tok_z1 = emit_l1f_attn(nc, P, z_sb, gb, zb_rhs, CB, CON, tok_con, nmx, sums, rcs, sqs, WQ6BF, WO6BF, QS6, KS6, li, [t_zbf] + z_gate, attn_head, dma_ch, preps[li])
            z_gate = tok_z1
            tap(f'z_a{li}', z_sb[:], tok_z1)
            if 2 * li + 1 >= nsteps:
                break
            P.barrier(('v', 's', 't', 'y'))
            pparts = None
            if li + 1 < (nsteps + 1) // 2:
                out_n, pparts = prep_l1f(li + 1)
                preps.append(out_n)
            tok_z2 = emit_l1f_ffn(nc, P, z_sb, xn, CB, ST, sqs, tok_con, LNG6, LNB6, W1A6, W1G6, B1A6, B1G6, W26, B26, lnbb, beffa, beffg, li, z_gate, emit_ln, zc, dma_ch, (W1A6BF, W1G6BF, W26BF, preps[li]), pparts)
            z_gate = tok_z2
            tap(f'z_f{li}', z_sb[:], tok_z2)
        toks = []
        for c in range(NC):
            P.do('s', lambda e, cc=c, gb=gb, zc=zc, z_sb=z_sb: e.activation(gb[:, cc * T:(cc + 1) * T], zc(z_sb, cc, 0, T), AF.Silu), z_gate)
            toks.append(P.tok('s'))
        tap('g2', gb[:], toks)
    return toks

def emit_l1f_attn(nc, P, z_sb, gb, zb_rhs, CB, CON, tok_con, nmx, sums, rcs, sqs, WQ6BF, WO6BF, QS6, KS6, li, gate0, attn_head, dma_ch, prep=None):
    ident = CB['ident'][:]
    t_zadd = {}
    with contextlib.ExitStack() as EO:
        qsc = EO.enter_context(nc.sbuf_tensor(_U('qsc'), [128, 1], F32))
        ksc = EO.enter_context(nc.sbuf_tensor(_U('ksc'), [128, 1], F32))
        P.do('y', lambda e, qsc=qsc, QS6=QS6, li=li: e.dma_start(out=qsc[:], in_=QS6[li]), list(gate0), dma=True)
        P.do('y', lambda e, ksc=ksc, KS6=KS6, li=li: e.dma_start(out=ksc[:], in_=KS6[li]), dma=True)
        tok_qs = P.tok('y')
        for pg in range(2):
            P.barrier()
            tok_ao = []
            eg_outer = contextlib.ExitStack()
            ao = eg_outer.enter_context(nc.sbuf_tensor(_U('zao'), [128, 4 * T], BF16))
            with contextlib.ExitStack() as ep:
                rotb = {k: ep.enter_context(nc.sbuf_tensor(_U('rb_') + k, [128, T], BF16)) for k in ('rotc', 'rots', 'rotci', 'rotsi')}
                rtk = []
                for k in ('rotc', 'rots', 'rotci', 'rotsi'):
                    P.do('y', lambda e, d=rotb[k], s=CON[k]: e.dma_start(out=d[:], in_=s), list(gate0), dma=True)
                    rtk.append(P.tok('y'))
                tok_rot = rtk[-1]
                wbf = [ep.enter_context(nc.sbuf_tensor(_U(f'zwb{i}'), [128, 1024], BF16)) for i in range(2)]
                q_sb = ep.enter_context(nc.sbuf_tensor(_U('zq'), [128, T], BF16))
                k_sb = ep.enter_context(nc.sbuf_tensor(_U('zk'), [128, T], BF16))
                v_sb = ep.enter_context(nc.sbuf_tensor(_U('zv'), [128, T], BF16))
                vtm = ep.enter_context(nc.sbuf_tensor(_U('zvt'), [128, T], BF16))
                pt_sb = ep.enter_context(nc.sbuf_tensor(_U('zpt'), [128, 384], BF16))
                p_sb = ep.enter_context(nc.sbuf_tensor(_U('zp'), [128, 384], BF16))
                fs_b = [ep.enter_context(nc.sbuf_tensor(_U(f'zfb{i}'), [2, 512], BF16)) for i in range(2)]
                t1b = [ep.enter_context(nc.sbuf_tensor(_U(f'zt1{i}'), [128, 512], BF16)) for i in range(2)]
                t2b = [ep.enter_context(nc.sbuf_tensor(_U(f'zt2{i}'), [128, 512], BF16)) for i in range(2)]
                sq2 = [ep.enter_context(nc.sbuf_tensor(_U(f'zsq{i}'), [128, 512], BF16)) for i in range(2)]
                pw = prep.get('wq') if prep is not None else None
                wfree = {0: pw, 1: pw}
                gates = {}
                prevs = {}
                for pi in range(4):
                    pr = pg * 4 + pi
                    tjs = []
                    for j, off in enumerate((pr * 128, D + pr * 128, 2 * D + pr * 128)):
                        wi = (pr * 3 + j) % 2
                        tc = dma_ch(wbf[wi][:], WQ6BF[li][:, off:off + 128], [wfree[wi]])
                        wb_ = wbf[wi]
                        dst = (q_sb, k_sb, v_sb)[j]
                        with contextlib.ExitStack() as eq:
                            qps = [eq.enter_context(nc.psum_tensor(_U(f'zqp{i}'), [128, 512], F32)) for i in range(2)]
                            sqp = [eq.enter_context(nc.psum_tensor(_U(f'zsqp{i}'), [32, 512], F32)) for i in range(2)]
                            f2p = [eq.enter_context(nc.psum_tensor(_U(f'zf2p{i}'), [128, 512], F32)) for i in range(2)]
                            swp = [eq.enter_context(nc.psum_tensor(_U(f'zswp{i}'), [128, 512], F32)) for i in range(2)]
                            tev = prevs.setdefault('tev', {})
                            tsqd = prevs.setdefault('tsqd', {})
                            for sp in range(4):
                                b = sp % 2
                                t0 = sp * 512
                                w0 = [tc, tev.get(b), tsqd.get(b)] + gate0 + ([gates.get('prev_exp'), gates.get('prev_tav'), gates.get('prev_ao'), gates.get('prev_ptev')] if j == 0 else [])
                                for k in range(NC):
                                    sh = k >= 4
                                    rhs = zb_rhs(k, t0, t0 + 512)
                                    o0 = 1 if sh and t0 == 0 else 0
                                    P.do('t', lambda e, kk=k, bb=b, r=rhs, oo=o0, qps=qps, wb_=wb_: e.matmul(qps[bb][:, oo:512], wb_[:, kk * 128:(kk + 1) * 128], r, start=kk == 0, stop=kk == NC - 1), w0 if k == 0 else [])
                                tm = P.tok('t')
                                wfree[wi] = tm
                                if j == 2:
                                    P.do('s', lambda e, ss=sp, bb=b, v_sb=v_sb, qps=qps: e.copy(v_sb[:, ss * 512:(ss + 1) * 512], qps[bb][:]), [tm])
                                    tev[b] = P.tok('s')
                                    tjs.append(tev[b])
                                    continue
                                P.do('s', lambda e, bb=b, sq2=sq2, qps=qps: e.activation(sq2[bb][:], qps[bb][:], AF.Square), [tm, prevs.get(('tssp', b))])
                                tsq = P.tok('s')
                                tsqd[b] = tsq
                                scl = 8.0 if j == 0 else 1.0
                                scol = qsc if j == 0 else ksc
                                P.do('v', lambda e, d=dst, ss=sp, bb=b, s_=scl, sc=scol, qps=qps: e.tensor_scalar(d[:, ss * 512:(ss + 1) * 512], qps[bb][:], s_, sc[:], ALU.mult, ALU.mult), [tm, tok_qs, tsq])
                                tev[b] = P.tok('v')
                                P.do('t', lambda e, bb=b, sqp=sqp, CB=CB, sq2=sq2: e.matmul(sqp[bb][:], CB['bdones'][:], sq2[bb][:], start=True, stop=True), [tsq, tok_con, prevs.get(('tf1', b))])
                                tssp = P.tok('t')
                                prevs[('tssp', b)] = tssp
                                P.do('s', lambda e, bb=b, fs_b=fs_b, sqp=sqp: e.activation(fs_b[bb][:], sqp[bb][0:2, :], AF.Sqrt), [tssp, prevs.get(('tf5', b))])
                                tf1 = P.tok('s')
                                prevs[('tf1', b)] = tf1
                                def _rcp_bf(e, bb=b, fs_b=fs_b, nc=nc):
                                    with nc.allow_low_precision(reason='l2norm scale is rounded to bf16 for the PE broadcast anyway'):
                                        return e.reciprocal(fs_b[bb][:], fs_b[bb][:])
                                P.do('v', _rcp_bf, [tf1])
                                tf2 = P.tok('v')
                                P.do('v', lambda e, bb=b, fs_b=fs_b: e.tensor_scalar(fs_b[bb][:], fs_b[bb][:], 1e+12, None, ALU.min), [tf2])
                                tf4 = P.tok('v')
                                P.do('t', lambda e, bb=b, f2p=f2p, CB=CB, fs_b=fs_b: e.matmul(f2p[bb][:], CB['f2sel'][:], fs_b[bb][:], start=True, stop=True), [tf4, prevs.get(('tqn', b))])
                                tf5 = P.tok('t')
                                prevs[('tf5', b)] = tf5
                                P.do('v', lambda e, d=dst, ss=sp, bb=b, f2p=f2p: e.tensor_mul(d[:, ss * 512:(ss + 1) * 512], d[:, ss * 512:(ss + 1) * 512], f2p[bb][:]), [tf5])
                                tqn = P.tok('v')
                                prevs[('tqn', b)] = tqn
                                P.do('t', lambda e, d=dst, ss=sp, bb=b, swp=swp, CB=CB: e.matmul(swp[bb][:], CB['swapm'][:], d[:, ss * 512:(ss + 1) * 512], start=True, stop=True), [tqn, prevs.get(('tr2', b))])
                                tsw = P.tok('t')
                                rc_, rs_ = ('rotc', 'rots') if j == 0 else ('rotci', 'rotsi')
                                P.do('v', lambda e, d=dst, ss=sp, bb=b, rr=rc_, t1b=t1b, rotb=rotb: e.tensor_mul(t1b[bb][:], d[:, ss * 512:(ss + 1) * 512], rotb[rr][:, ss * 512:(ss + 1) * 512]), [tqn, tok_rot, prevs.get(('tadd', b))])
                                P.do('v', lambda e, ss=sp, bb=b, rr=rs_, t2b=t2b, swp=swp, rotb=rotb: e.tensor_mul(t2b[bb][:], swp[bb][:], rotb[rr][:, ss * 512:(ss + 1) * 512]), [tsw, prevs.get(('tadd', b))])
                                tr2 = P.tok('v')
                                prevs[('tr2', b)] = tr2
                                P.do('g', lambda e, d=dst, ss=sp, bb=b, t1b=t1b, t2b=t2b: e.tensor_add(d[:, ss * 512:(ss + 1) * 512], t1b[bb][:], t2b[bb][:]), [tr2])
                                tadd = P.tok('g')
                                prevs[('tadd', b)] = tadd
                                tjs.append(tadd)
                    with contextlib.ExitStack() as ev:
                        vtp = ev.enter_context(nc.psum_tensor(_U('zvp'), [128, 2048], BF16))
                        for a in range(NT):
                            P.do('t', lambda e, aa=a, vtp=vtp, v_sb=v_sb, ident=ident: e.transpose(vtp[:, aa * 128:(aa + 1) * 128], v_sb[:, aa * 128:(aa + 1) * 128], ident), tjs + [gates.get('prev_tav'), gates.get('prev_ao'), gates.get('prev_ptev')] if a == 0 else [])
                        tvt = P.tok('t')
                        P.do('v', lambda e, vtm=vtm, vtp=vtp: e.tensor_copy(vtm[:], vtp[:]), [tvt])
                        tok_vtm = P.tok('v')
                    gates['qkv'] = tjs + [tok_vtm]
                    gates['vtm'] = [tok_vtm]

                    def mask_fn(qt, sps, _dummy=None):
                        klen = (qt - max(0, qt - 2) + 1) * 128
                        ops = []
                        if qt >= 2:
                            ops.append((0, 'tri_lo'))
                        ops.append((klen - 128, 'tri_hi'))
                        for i, (c0, nm) in enumerate(ops):
                            P.do('t', lambda e, cc=c0, n=nm, last=i == len(ops) - 1, sps=sps, CB=CB: e.matmul(sps[:, cc:cc + 128], CB[n][:], CB['negi'][:], start=False, stop=last))
                    for hh in range(2 if L1F_PART in ('attn', 'full') else 0):

                        def ao_w(qt, avp, w, pi=pi, hh=hh):
                            P.do('v', lambda e, col=qt, h0=hh * 64, ao=ao, pi=pi, avp=avp: e.tensor_copy(ao[h0:h0 + 64, pi * T + col * 128:pi * T + (col + 1) * 128], avp[h0:h0 + 64, :]), w)
                            tok_ao.append(P.tok('v'))
                            return tok_ao[-1]
                        attn_head(f'z{pr}h{hh}', hh * 64, q_sb, k_sb, vtm, p_sb, pt_sb, nmx, sums, rcs, ao_w, list(range(NT)), lambda qt: qt * 128, lambda qt: (max(0, qt - 2) * 128, (qt - max(0, qt - 2) + 1) * 128), gates, mask_fn=mask_fn)
            P.barrier()
            if L1F_PART != 'full':
                eg_outer.close()
                continue
            with contextlib.ExitStack() as ew:
                wbf2 = ew.enter_context(nc.sbuf_tensor(_U('zwb2'), [128, 4096], BF16))
                pps = [ew.enter_context(nc.psum_tensor(_U(f'zpp{i}'), [128, 512], F32)) for i in range(2)]
                tc = dma_ch(wbf2[:], WO6BF[li][pg * 512:(pg + 1) * 512, :], tok_ao[-1:] + ([prep.get('wo')] if prep is not None else []))
                tev = {}
                cnt = 0
                for of in range(8):
                    for sp in range(4):
                        b = cnt % 2
                        cnt += 1
                        w0 = [tc, tev.get(b)] + tok_ao[-1:]
                        for k in range(4):
                            P.do('t', lambda e, kk=k, oo=of, ss=sp, bb=b, pps=pps, wbf2=wbf2, ao=ao: e.matmul(pps[bb][:], wbf2[:, kk * 1024 + oo * 128:kk * 1024 + (oo + 1) * 128], ao[:, kk * T + ss * 512:kk * T + (ss + 1) * 512], start=kk == 0, stop=kk == 3), w0 if k == 0 else [])
                        tm = P.tok('t')
                        P.do('v', lambda e, bb=b, oo=of, ss=sp, z_sb=z_sb, pps=pps: e.tensor_tensor(z_sb[:, oo * WZ + 1 + ss * 512:oo * WZ + 1 + (ss + 1) * 512], pps[bb][:], z_sb[:, oo * WZ + 1 + ss * 512:oo * WZ + 1 + (ss + 1) * 512], ALU.add), [tm, t_zadd.get((b, of, sp))])
                        tev[b] = P.tok('v')
                        t_zadd[b, of, sp] = tev[b]
            eg_outer.close()
    return [t for t in t_zadd.values()]

def emit_l1f_ffn(nc, P, z_sb, xn, CB, ST, sqs, tok_con, LNG6, LNB6, W1A6, W1G6, B1A6, B1G6, W26, B26, lnbb, beffa, beffg, li, gate0, emit_ln, zc, dma_ch, WCACHE=None, prep_parts=None):
    EPS1 = 1e-05
    W1ABF, W1GBF, W2BF, prep = WCACHE
    t_zw = {}
    with contextlib.ExitStack() as EF:
        xn1 = EF.enter_context(nc.sbuf_tensor(_U('xn1'), [128, NC * 512], BF16))
        h_bf = EF.enter_context(nc.sbuf_tensor(_U('zh'), [128, NK2 * 512], BF16))
        wba = EF.enter_context(nc.sbuf_tensor(_U('zfwa'), [128, 4096], BF16))
        wbg = EF.enter_context(nc.sbuf_tensor(_U('zfwg'), [128, 4096], BF16))
        gel = [EF.enter_context(nc.sbuf_tensor(_U(f'zgel{i}'), [128, 512], BF16)) for i in range(2)]
        lg = EF.enter_context(nc.sbuf_tensor(_U('zlg'), [128, 8], F32))
        lb = EF.enter_context(nc.sbuf_tensor(_U('zlb'), [128, 8], F32))
        lgr = EF.enter_context(nc.sbuf_tensor(_U('zlgr'), [128, 8], F32))
        ba = EF.enter_context(nc.sbuf_tensor(_U('zba'), [128, NK2], F32))
        bg = EF.enter_context(nc.sbuf_tensor(_U('zbg'), [128, NK2], F32))
        b2 = EF.enter_context(nc.sbuf_tensor(_U('zb2'), [128, 8], F32))
        tks = []
        for dst, srcw in ((lg, LNG6[li]), (lb, LNB6[li]), (ba, B1A6[li]), (bg, B1G6[li]), (b2, B26[li])):
            P.do('y', lambda e, d=dst, s=srcw: e.dma_start(out=d[:], in_=s), list(gate0), dma=True)
            tks.append(P.tok('y'))
        tok_cst = tks[-1]
        P.do('v', lambda e, lgr=lgr, lg=lg: e.reciprocal(lgr[:], lg[:]), [tok_cst])
        trl = P.tok('v')
        for k in range(NC):
            P.do('s', lambda e, kk=k, lnbb=lnbb, lgr=lgr, lb=lb: e.activation(lnbb[:, kk:kk + 1], lgr[:, kk:kk + 1], AF.Identity, scale=lb[:, kk:kk + 1]), [trl] if k == 0 else [])
        tok_lnbb = P.tok('s')
        for q in range(4):
            P.barrier(('v', 's', 't', 'y'))
            if prep_parts:
                prep_parts.pop(0)()
            tq0 = q * 512

            def src_col(a, c, t0=tq0):
                return zc(z_sb, c, t0 + a * 128, t0 + (a + 1) * 128, shift=c >= 4)

            def dst_fn(a):
                return xn1[:].rearrange('p (c w) -> p c w', c=NC)[:, :, a * 128:(a + 1) * 128]
            lt = emit_ln(src_col, 4, dst_fn, f'zl{li}_{q}', EPS1, waits=gate0 + [t_zw.get('last')])
            t_hev = None
            with contextlib.ExitStack() as e1:
                psa = [e1.enter_context(nc.psum_tensor(_U(f'zpsa{i}'), [128, 512], F32)) for i in range(2)]
                psg = [e1.enter_context(nc.psum_tensor(_U(f'zpsg{i}'), [128, 512], F32)) for i in range(2)]
                psb = e1.enter_context(nc.psum_tensor(_U('zpsb'), [128, 16], F32))
                t_stt = {}
                t_gel = {}
                ocg = 0
                for os_ in range(DIP // 512):
                    pass
                nos = DIP // 512 + (1 if DIP % 512 else 0)
                for os_ in range(nos):
                    o0 = os_ * 512
                    ow = min(512, DIP - o0)
                    nch = ow // 128
                    pw1 = [prep.get('w1')] if prep is not None and q == 0 and os_ == 0 else []
                    tca = dma_ch(wba[:, 0:8 * ow], W1ABF[li][:, o0:o0 + ow],
                                 [t_zw.get('wafree')] + pw1)
                    tcg = dma_ch(wbg[:, 0:8 * ow], W1GBF[li][:, o0:o0 + ow],
                                 [t_zw.get('wgfree')])
                    if q == 0:
                        prev_b1 = None
                        for oc in range(nch):
                            gco = o0 // 128 + oc
                            for k in range(NC):
                                P.do('t', lambda e, kk=k, occ=oc, w=ow, psb=psb, wba=wba, lnbb=lnbb: e.matmul(psb[:, 0:1], wba[:, kk * w + occ * 128:kk * w + occ * 128 + 128], lnbb[:, kk:kk + 1], start=kk == 0, stop=kk == NC - 1), [tca, tcg, tok_lnbb, prev_b1] if k == 0 else [])
                            tb1 = P.tok('t')
                            P.do('v', lambda e, gc=gco, beffa=beffa, ba=ba, psb=psb: e.tensor_tensor(beffa[:, gc:gc + 1], ba[:, gc:gc + 1], psb[:, 0:1], ALU.add), [tb1, tok_cst])
                            tba = P.tok('v')
                            for k in range(NC):
                                P.do('t', lambda e, kk=k, occ=oc, w=ow, psb=psb, wbg=wbg, lnbb=lnbb: e.matmul(psb[:, 1:2], wbg[:, kk * w + occ * 128:kk * w + occ * 128 + 128], lnbb[:, kk:kk + 1], start=kk == 0, stop=kk == NC - 1), [tba] if k == 0 else [])
                            tb2 = P.tok('t')
                            P.do('v', lambda e, gc=gco, beffg=beffg, bg=bg, psb=psb: e.tensor_tensor(beffg[:, gc:gc + 1], bg[:, gc:gc + 1], psb[:, 1:2], ALU.add), [tb2])
                            prev_b1 = P.tok('v')
                        tbef = P.tok('v')
                    else:
                        tbef = None
                    tsc = tcg
                    for oc in range(nch):
                        gco = o0 // 128 + oc
                        b = ocg % 2
                        ocg += 1
                        w0 = [tsc, tca, t_stt.get(b)] + lt
                        for k in range(NC):
                            P.do('t', lambda e, kk=k, occ=oc, w=ow, bb=b, psa=psa, wba=wba, xn1=xn1: e.matmul(psa[bb][:], wba[:, kk * w + occ * 128:kk * w + occ * 128 + 128], xn1[:, kk * 512:(kk + 1) * 512], start=kk == 0, stop=kk == NC - 1), w0 if k == 0 else [])
                        for k in range(NC):
                            P.do('t', lambda e, kk=k, occ=oc, w=ow, bb=b, psg=psg, wbg=wbg, xn1=xn1: e.matmul(psg[bb][:], wbg[:, kk * w + occ * 128:kk * w + occ * 128 + 128], xn1[:, kk * 512:(kk + 1) * 512], start=kk == 0, stop=kk == NC - 1), [t_gel.get(b)] if k == 0 else [])
                        thg = P.tok('t')
                        P.do('s', lambda e, gc=gco, bb=b, gel=gel, psg=psg, beffg=beffg: e.activation(gel[bb][:], psg[bb][:], AF.Gelu, bias=beffg[:, gc:gc + 1]), [thg])
                        tge = P.tok('s')
                        t_gel[b] = tge
                        P.do('v', lambda e, gc=gco, bb=b, h_bf=h_bf, psa=psa, beffa=beffa, gel=gel: e.scalar_tensor_tensor(h_bf[:, gc * 512:(gc + 1) * 512], psa[bb][:], beffa[:, gc:gc + 1], gel[bb][:], ALU.add, ALU.mult), [tge])
                        t_hev = P.tok('v')
                        t_stt[b] = t_hev
                    t_zw['wafree'] = thg
                    t_zw['wgfree'] = thg
            P.barrier(('v', 's', 't', 'y'))
            with contextlib.ExitStack() as e2:
                zps = [e2.enter_context(nc.psum_tensor(_U(f'zzp{i}'), [128, 512], F32)) for i in range(8)]
                t_mm = {}
                for fs in range(6):
                    r0 = fs * 512
                    rw = min(512, DIP - r0)
                    nch = rw // 128
                    pw2 = [prep.get('w2')] if prep is not None and q == 0 and fs == 0 else []
                    tc2 = dma_ch(wba[:, 0:nch * 1024], W2BF[li][r0:r0 + rw, :],
                                 [t_zw.get('w2free')] + ([t_hev] if fs == 0 else []) + pw2)
                    last = None
                    for of in range(8):
                        for m in range(nch):
                            kc = fs * 4 + m
                            P.do('t', lambda e, oo=of, mm2=m, k=kc, zps=zps, wba=wba, h_bf=h_bf: e.matmul(zps[oo][:], wba[:, mm2 * 1024 + oo * 128:mm2 * 1024 + (oo + 1) * 128], h_bf[:, k * 512:(k + 1) * 512], start=k == 0, stop=k == NK2 - 1), [tc2, t_hev] if of == 0 and m == 0 else [])
                        last = P.tok('t')
                        t_mm[of] = last
                    t_zw['w2free'] = last
                tlast = []
                for of in range(8):
                    P.do('v', lambda e, oo=of, t0=tq0, zc=zc, z_sb=z_sb, zps=zps, b2=b2: e.scalar_tensor_tensor(zc(z_sb, oo, t0, t0 + 512), zps[oo][:], b2[:, oo:oo + 1], zc(z_sb, oo, t0, t0 + 512), ALU.add, ALU.add), [t_mm[of], tok_cst])
                    tlast.append(P.tok('v'))
                t_zw['last'] = tlast[-1]
        return tlast

def emit_l1_attn(nc, P, xn, gb, CB, ST, sqs, tok_con, WQKV1, WO1, BQKV1, BO1, NQW, NKW, XSP, OUT, fin, tap, tap_toks, attn_head, zc, dma_ch, WBuf=None):
    ident = CB['ident'][:]
    P.barrier()
    with contextlib.ExitStack() as EL:
        sqs = EL.enter_context(nc.sbuf_tensor(_U('sqs'), [128, 512], BF16))
        k_all = EL.enter_context(nc.sbuf_tensor(_U('lk'), [128, NC * T], BF16))
        nmx = EL.enter_context(nc.sbuf_tensor(_U('lnm'), [128, 16], F32))
        sums = EL.enter_context(nc.sbuf_tensor(_U('lsm'), [128, 16], F32))
        rcs = EL.enter_context(nc.sbuf_tensor(_U('lrc'), [128, 16], F32))
        bqkv = [EL.enter_context(nc.sbuf_tensor(_U(f'lbq{j}'), [128, 8], F32)) for j in range(3)]
        bo = EL.enter_context(nc.sbuf_tensor(_U('lbo'), [128, 8], F32))
        nw8 = EL.enter_context(nc.sbuf_tensor(_U('lnw'), [128, 8], F32))
        nkw_ = EL.enter_context(nc.sbuf_tensor(_U('lkw'), [128, 8], F32))
        rr_f = EL.enter_context(nc.sbuf_tensor(_U('lrr'), [1, 512], F32))
        rr_b = EL.enter_context(nc.sbuf_tensor(_U('lrb'), [1, 512], BF16))
        for j in range(3):
            P.do('y', lambda e, d=bqkv[j], BQKV1=BQKV1, j=j: e.dma_start(out=d[:], in_=BQKV1[j]), list(fin), dma=True)
        P.do('y', lambda e, bo=bo, BO1=BO1: e.dma_start(out=bo[:], in_=BO1), dma=True)
        P.do('y', lambda e, nw8=nw8, NQW=NQW: e.dma_start(out=nw8[:], in_=NQW), dma=True)
        P.do('y', lambda e, nkw_=nkw_, NKW=NKW: e.dma_start(out=nkw_[:], in_=NKW), dma=True)
        tok_b = P.tok('y')
        P.do('v', lambda e, nw8=nw8: e.tensor_scalar(nw8[:], nw8[:], 0.125, None, ALU.mult), [tok_b])
        tok_nw = P.tok('v')

        def qk_build(WCOL, bias_j, wcol_scale, dst_sb, dst_w, spans, tag, gate):
            toks = []
            with contextlib.ExitStack() as eb:
                wbq = WBuf(eb, tag + 'w', 1024)
                ssp = eb.enter_context(nc.psum_tensor(_U(tag + 'ss'), [1, 512], F32))
                f2p = eb.enter_context(nc.psum_tensor(_U(tag + 'f2'), [128, 512], F32))
                qps = [eb.enter_context(nc.psum_tensor(_U(tag + f'qp{i}'), [128, 512], F32)) for i in range(2)]
                t_wfree = gate[-1] if gate else None
                tevs = {}
                last_wmm = None
                prevs_qk = {}
                dst_off = min(spans) * 512
                for sp in spans:
                    t0 = sp * 512
                    lt0 = t0 - dst_off
                    tsqs = []
                    for oc in range(NC):
                        wbf, tc = wbq.load(WCOL(oc), [t_wfree] if t_wfree else ())
                        t_wfree = None
                        b = oc % 2
                        w0 = [tc, tevs.get(b)] + gate
                        for k in range(NC):
                            P.do('t', lambda e, kk=k, tt0=t0, bb=b, qps=qps, wbf=wbf, zc=zc, xn=xn: e.matmul(qps[bb][:], wbf[:, kk * 128:(kk + 1) * 128], zc(xn, kk, tt0, tt0 + 512), start=kk == 0, stop=kk == NC - 1), w0 if k == 0 else [])
                        tm = P.tok('t')
                        tevs['w', oc] = tm
                        last_wmm = tm
                        wbq.consumed(tm)
                        P.do('s', lambda e, bb=b, occ=oc, sqs=sqs, qps=qps, bqkv=bqkv, bias_j=bias_j: e.activation(sqs[:], qps[bb][:], AF.Square, bias=bqkv[bias_j][:, occ:occ + 1]), [tm, tok_b])
                        tsq = P.tok('s')
                        P.do('t', lambda e, occ=oc, ssp=ssp, CB=CB, sqs=sqs: e.matmul(ssp[:], CB['onescol'][:], sqs[:], start=occ == 0, stop=occ == NC - 1), [tsq, tok_con])
                        tsqs.append(P.tok('t'))
                        P.do('s', lambda e, bb=b, occ=oc, d=dst_sb, tt0=lt0, dst_w=dst_w, qps=qps, bqkv=bqkv, bias_j=bias_j: e.activation(d[:, occ * dst_w + tt0:occ * dst_w + tt0 + 512], qps[bb][:], AF.Identity, bias=bqkv[bias_j][:, occ:occ + 1]), [tm])
                        tevs[b] = P.tok('s')
                    P.do('s', lambda e, rr_f=rr_f, ssp=ssp, CB=CB: e.activation(rr_f[:, 0:512], ssp[:], AF.Sqrt, scale=CB['cpool'][0:1, 4:5], bias=CB['cpool'][0:1, 1:2]), [tsqs[-1], prevs_qk.get('tr4')])
                    tr1 = P.tok('s')
                    P.do('v', lambda e, rr_f=rr_f: e.reciprocal(rr_f[:, 0:512], rr_f[:, 0:512]), [tr1])
                    tr2 = P.tok('v')
                    P.do('s', lambda e, rr_b=rr_b, rr_f=rr_f: e.copy(rr_b[:, 0:512], rr_f[:, 0:512]), [tr2])
                    tr3 = P.tok('s')
                    P.do('t', lambda e, f2p=f2p, CB=CB, rr_b=rr_b, t0=t0: e.matmul(f2p[:], CB['onesrow'][:], rr_b[:, 0:512], start=True, stop=True), [tr3, tok_con, prevs_qk.get('apply')])
                    tr4 = P.tok('t')
                    prevs_qk['tr4'] = tr4
                    wcol = nw8 if wcol_scale else nkw_
                    for oc in range(NC):
                        P.do('v', lambda e, occ=oc, d=dst_sb, tt0=lt0, wc=wcol, dst_w=dst_w, f2p=f2p: e.scalar_tensor_tensor(d[:, occ * dst_w + tt0:occ * dst_w + tt0 + 512], d[:, occ * dst_w + tt0:occ * dst_w + tt0 + 512], wc[:, occ:occ + 1], f2p[:], ALU.mult, ALU.mult), [tr4, tok_nw] if oc == 0 else [tr4])
                        toks.append(P.tok('v'))
                        prevs_qk['apply'] = toks[-1]
            return toks
        tk = qk_build(lambda oc: WQKV1[:, D + oc * 128:D + (oc + 1) * 128], 1, False, k_all[:], T, range(4), 'lkb', fin)
        P.barrier()
        tap('l1k', k_all[:], tk)
        for th in range(2):
            P.barrier()
            h0 = th * 1024
            with contextlib.ExitStack() as EH:
                q_h = EH.enter_context(nc.sbuf_tensor(_U('lq'), [128, NC * 1024], BF16))
                ao = EH.enter_context(nc.sbuf_tensor(_U('lao'), [128, NC * 1024], BF16))
                tq = qk_build(lambda oc: WQKV1[:, oc * 128:(oc + 1) * 128], 0, True, q_h[:], 1024, [2 * th, 2 * th + 1], 'lqb', fin)
                P.barrier()
                tq = [tq[-1]]
                tok_ao = []
                with contextlib.ExitStack() as EP:
                    wstg = EP.enter_context(nc.sbuf_tensor(_U('lvw'), [128, 1024], F32))
                    wbf = EP.enter_context(nc.sbuf_tensor(_U('lvwb'), [128, 1024], BF16))
                    v_sb = EP.enter_context(nc.sbuf_tensor(_U('lv'), [128, T], BF16))
                    vtm = EP.enter_context(nc.sbuf_tensor(_U('lvt'), [128, T], BF16))
                    p_sb = EP.enter_context(nc.sbuf_tensor(_U('lp'), [128, T], BF16))
                    pt_sb = EP.enter_context(nc.sbuf_tensor(_U('lpt'), [128, T], BF16))
                    t_wfree = fin[-1] if fin else None
                    gates = {}
                    for pr in range(8):
                        tw = dma_ch(wstg[:], WQKV1[:, 2 * D + pr * 128:2 * D + (pr + 1) * 128], [t_wfree])
                        P.do('g', lambda e, wbf=wbf, wstg=wstg: e.tensor_copy(wbf[:], wstg[:]), [tw, gates.get('vfree')])
                        tc = P.tok('g')
                        t_wfree = tc
                        with contextlib.ExitStack() as eq:
                            qps = [eq.enter_context(nc.psum_tensor(_U(f'lvp{i}'), [128, 512], F32)) for i in range(2)]
                            tev = {}
                            for sp in range(4):
                                b = sp % 2
                                w0 = [tc, tev.get(b)] + fin + ([gates.get('prev_exp'), gates.get('prev_tav'), gates.get('prev_ao'), gates.get('prev_ptev')] if sp == 0 else [])
                                for k in range(NC):
                                    P.do('t', lambda e, kk=k, ss=sp, bb=b, qps=qps, wbf=wbf, zc=zc, xn=xn: e.matmul(qps[bb][:], wbf[:, kk * 128:(kk + 1) * 128], zc(xn, kk, ss * 512, (ss + 1) * 512), start=kk == 0, stop=kk == NC - 1), w0 if k == 0 else [])
                                tm = P.tok('t')
                                P.do('s', lambda e, ss=sp, bb=b, v_sb=v_sb, qps=qps, bqkv=bqkv, pr=pr: e.activation(v_sb[:, ss * 512:(ss + 1) * 512], qps[bb][:], AF.Identity, bias=bqkv[2][:, pr:pr + 1]), [tm, tok_b])
                                tev[b] = P.tok('s')
                            tvv = [tev.get(0), tev.get(1)]
                        with contextlib.ExitStack() as ev:
                            vtp = ev.enter_context(nc.psum_tensor(_U('lvtp'), [128, 2048], BF16))
                            for a in range(NT):
                                P.do('t', lambda e, aa=a, vtp=vtp, v_sb=v_sb, ident=ident: e.transpose(vtp[:, aa * 128:(aa + 1) * 128], v_sb[:, aa * 128:(aa + 1) * 128], ident), tvv + [gates.get('prev_tav'), gates.get('prev_ao'), gates.get('prev_ptev')] if a == 0 else [])
                            tvt = P.tok('t')
                            P.do('v', lambda e, vtm=vtm, vtp=vtp: e.tensor_copy(vtm[:], vtp[:]), [tvt])
                            tok_vtm = P.tok('v')
                        gates['vfree'] = tvt
                        gates['qkv'] = tq + tk + [tok_vtm]
                        gates['vtm'] = [tok_vtm]
                        qv = q_h[:, pr * 1024:(pr + 1) * 1024]
                        kv = k_all[:, pr * T:(pr + 1) * T]
                        for hh in range(2):

                            def ao_w(qt, avp, w, pr=pr, hh=hh):
                                P.do('v', lambda e, col=qt, hq=hh * 64, ao=ao, pr=pr, avp=avp: e.tensor_copy(ao[hq:hq + 64, pr * 1024 + col * 128:pr * 1024 + (col + 1) * 128], avp[hq:hq + 64, :]), w)
                                tok_ao.append(P.tok('v'))
                                return tok_ao[-1]
                            attn_head(f'l{th}_{pr}h{hh}', hh * 64, qv, kv, vtm, p_sb, pt_sb, nmx, sums, rcs, ao_w, list(range(8)), lambda qt: qt * 128, lambda qt: (0, T), gates)
                P.barrier()
                with contextlib.ExitStack() as EW:
                    x2 = EW.enter_context(nc.sbuf_tensor(_U('x2'), [128, 8 * 1024], F32))
                    wstg2 = EW.enter_context(nc.sbuf_tensor(_U('lw2'), [128, 4096], F32))
                    wbf2 = EW.enter_context(nc.sbuf_tensor(_U('lwb2'), [128, 4096], BF16))
                    tmp = [EW.enter_context(nc.sbuf_tensor(_U(f'ltm{i}'), [128, 512], BF16)) for i in range(2)]
                    pps = [EW.enter_context(nc.psum_tensor(_U(f'lpp{i}'), [128, 512], F32)) for i in range(2)]
                    ptp = [EW.enter_context(nc.psum_tensor(_U(f'lptr{i}'), [128, 512], BF16)) for i in range(2)]
                    P.do('y', lambda e, hh0=h0, x2=x2, XSP=XSP: e.dma_start(out=x2[:], in_=XSP[:, hh0 * 8:hh0 * 8 + 8192]), fin, dma=True)
                    tx2 = P.tok('y')
                    tev = {}
                    ttr = {}
                    tadd = {}
                    t_w2f = None
                    for wh in range(2):
                        tw = dma_ch(wstg2[:], WO1[wh * 512:(wh + 1) * 512, :], [t_w2f] + tok_ao[-1:])
                        P.do('g', lambda e, wbf2=wbf2, wstg2=wstg2: e.tensor_copy(wbf2[:], wstg2[:]), [tw, tadd.get('wfree')])
                        tc = P.tok('g')
                        t_w2f = tc
                        cnt = 0
                        for of in range(8):
                            for sp in range(2):
                                b = cnt % 2
                                cnt += 1
                                w0 = [tc, tev.get(b)] + tok_ao[-1:]
                                for k in range(4):
                                    kk2 = wh * 4 + k
                                    P.do('t', lambda e, kk=k, oo=of, ss=sp, bb=b, kg=kk2, pps=pps, wbf2=wbf2, ao=ao: e.matmul(pps[bb][:], wbf2[:, kk * 1024 + oo * 128:kk * 1024 + (oo + 1) * 128], ao[:, kg * 1024 + ss * 512:kg * 1024 + (ss + 1) * 512], start=kk == 0, stop=kk == 3), w0 if k == 0 else [])
                                tm = P.tok('t')
                                P.do('v', lambda e, bb=b, oo=of, ss=sp, hh0=h0, first=wh == 0, tmp=tmp, pps=pps, bo=bo, gb=gb: e.scalar_tensor_tensor(tmp[bb][:], pps[bb][:], bo[:, oo:oo + 1] if not first else 0.0, gb[:, oo * T + hh0 + ss * 512:oo * T + hh0 + (ss + 1) * 512], ALU.add, ALU.mult), [tm, tok_b, ttr.get(b)])
                                tv = P.tok('v')
                                tev[b] = tv
                                for j in range(4):
                                    P.do('t', lambda e, bb=b, jj=j, ptp=ptp, tmp=tmp, ident=ident: e.transpose(ptp[bb][:, jj * 128:(jj + 1) * 128], tmp[bb][:, jj * 128:(jj + 1) * 128], ident), [tv, tadd.get(b)] if j == 0 else [])
                                tt2 = P.tok('t')
                                ttr[b] = tt2
                                dstx = x2[:].rearrange('p (a d) -> p a d', a=8)[:, sp * 4:sp * 4 + 4, of * 128:(of + 1) * 128]
                                P.do('v', lambda e, d=dstx, bb=b, ptp=ptp: e.tensor_tensor(d, d, ptp[bb][:].rearrange('p (j w) -> p j w', j=4), ALU.add), [tt2, tx2])
                                tadd[b] = P.tok('v')
                        tadd['wfree'] = tm
                    P.do('y', lambda e, hh0=h0, OUT=OUT, x2=x2: e.dma_start(out=OUT[hh0:hh0 + 1024, :].rearrange('(a p) d -> p a d', p=128), in_=x2[:].rearrange('p (a d) -> p a d', a=8)), [tadd.get(0), tadd.get(1)], dma=True)
                    tout = P.tok('y')
                    if th == 1:
                        P.do('g', lambda e: e.nop(), [tout] + tap_toks)
                        P.do('y', lambda e: e.nop(), [tout] + tap_toks)
                    else:
                        out_tok_half = tout

def _chunkmaj(v):
    return np.ascontiguousarray(v.reshape(8, 128).T)

def _prep_inputs(I):
    A = np.ascontiguousarray
    W1 = I['L1_ffn_W1']
    W1a = np.zeros((6, D, DIP), np.float32)
    W1g = np.zeros((6, D, DIP), np.float32)
    W1a[:, :, :DI] = W1[:, :, :DI]
    W1g[:, :, :DI] = W1[:, :, DI:]
    b1 = I['L1_ffn_b1']
    b1a = np.zeros((6, 128, NK2), np.float32)
    b1g = np.zeros((6, 128, NK2), np.float32)
    for i in range(6):
        b1a[i].T.flat[:DI] = b1[i, :DI]
        b1g[i].T.flat[:DI] = b1[i, DI:]
    b1a = np.stack([np.pad(b1[i, :DI], (0, DIP - DI)).reshape(NK2, 128).T for i in range(6)])
    b1g = np.stack([np.pad(b1[i, DI:], (0, DIP - DI)).reshape(NK2, 128).T for i in range(6)])
    W2p = np.zeros((6, DIP, D), np.float32)
    W2p[:, :DI, :] = I['L1_ffn_W2']
    base = {'w_attnT': A(I['L0_attn_w_attn'].T), 'w_projT': A(I['L0_attn_w_proj'].T), 'w_fcT': A(I['L0_ffn_W_fc'].T), 'w_prT': A(I['L0_ffn_W_proj'].T), 'wqkv1T': A(np.concatenate([I['L1_attn_Wq'].T, I['L1_attn_Wk'].T, I['L1_attn_Wv'].T], 1)), 'wo1T': A(I['L1_attn_Wo'].T), 'bqkv1': np.stack([_chunkmaj(I['L1_attn_bq']), _chunkmaj(I['L1_attn_bk']), _chunkmaj(I['L1_attn_bv'])]), 'bo1': _chunkmaj(I['L1_attn_bo']), 'nqw': _chunkmaj(I['L1_attn_norm_q_w']), 'nkw': _chunkmaj(I['L1_attn_norm_k_w']), 'wqkv6': A(I['L1_ffn_Wqkv']), 'wo6': A(I['L1_ffn_Wo']), 'qs6': A(np.tile(I['L1_ffn_q_scale'], (1, 2))[:, :, None]), 'ks6': A(np.tile(I['L1_ffn_k_scale'], (1, 2))[:, :, None]), 'lng6': np.stack([_chunkmaj(I['L1_ffn_ln_g'][i]) for i in range(6)]), 'lnb6': np.stack([_chunkmaj(I['L1_ffn_ln_b'][i]) for i in range(6)]), 'w1a6': W1a, 'w1g6': W1g, 'b1a6': b1a, 'b1g6': b1g, 'w26': W2p, 'b26': np.stack([_chunkmaj(I['L1_ffn_b2'][i]) for i in range(6)])}
    for k, v in host_constants().items():
        if k not in ('identf', 'onescolf'):
            base['c_' + k] = v
    return base

def kernel(**inputs):
    base = _prep_inputs(inputs)
    x = np.ascontiguousarray(inputs['L0_attn_x']).astype(np.float32)
    nc = build('all')
    in_maps = [dict(base, x=np.ascontiguousarray(x[i])) for i in range(8)]
    res = run_bass_kernel_spmd(nc, in_maps, core_ids=list(range(8)))
    return np.stack([np.asarray(r['out']) for r in res.results])

# version 13 (final)
# speedup vs baseline: 1.1412x; 1.0132x over previous
import math
import os as _os
import contextlib
L1F_PART = _os.environ.get('L1F_PART', 'full')
L1F_PREP = _os.environ.get('L1F_PREP', 'all')
L1F_STAGES = ['qkv', 'sq', 'ev', 'ssp', 'sqrt', 'max', 'rcp', 'cpy', 'f2', 'f', 'all']
def _prep_lt(s):
    return L1F_STAGES.index(L1F_PREP) < L1F_STAGES.index(s)
import numpy as np
import concourse.bass as bass
import concourse.mybir as mybir
from concourse.bass_utils import run_bass_kernel_spmd
F32 = mybir.dt.float32
BF16 = mybir.dt.bfloat16
AF = mybir.ActivationFunctionType
ALU = mybir.AluOpType
AX = mybir.AxisListType
T = 2048
D = 1024
NT = 16
NC = 8
WZ = 2049
NEG = -3e+38
DI = 2730
DIP = 2816
NK2 = 22
EPS0 = 1e-05
_UC = [0]

def _U(name):
    _UC[0] += 1
    return f'{name}_{_UC[0]}'

class Stream:

    def __init__(self, key):
        self.key = key
        self.ops = []
        self.count = 0
        self.dcount = 0

    def tok(self):
        last = self.ops[-1]
        if last[2] is None:
            if last[3] and self.key != 'y':
                self.dcount += 16
                last[2] = ('d', self.dcount)
            else:
                self.count += 16 if last[3] else 1
                last[2] = ('c', self.count)
        kd, val = last[2]
        return (self.key + 'd' if kd == 'd' else self.key, val)

class Prog:

    def __init__(self):
        self.streams = {k: Stream(k) for k in ('v', 's', 't', 'g', 'y')}
        self.pending = {}

    def do(self, key, fn, waits=(), dma=False):
        st = self.streams[key]
        wd = dict(self.pending.pop(key, ()))
        for w in waits:
            if w is None:
                continue
            k, val = w
            if k == key:
                continue
            if wd.get(k, 0) < val:
                wd[k] = val
        st.ops.append([fn, wd, None, dma])
        if dma:
            st.tok()
        return st

    def tok(self, key):
        return self.streams[key].tok()

    def barrier(self, keys=('v', 's', 't', 'g', 'y')):
        toks = {}
        for k in keys:
            st = self.streams[k]
            if st.ops:
                toks[k] = st.tok()
        for k in keys:
            pend = self.pending.setdefault(k, {})
            for kk, t in toks.items():
                if kk == k:
                    continue
                rk, val = t
                if pend.get(rk, 0) < val:
                    pend[rk] = val
        return toks

    def emit(self, nc, block, sems):

        def make(stream):

            def run(eng):
                hw = {}
                for fn, waits, inc, is_dma in stream.ops:
                    for pk, val in waits.items():
                        if hw.get(pk, 0) < val:
                            eng.wait_ge(sems[pk], val)
                            hw[pk] = val
                    ins = fn(eng)
                    if inc is not None:
                        sk = stream.key + 'd' if inc[0] == 'd' else stream.key
                        ins.then_inc(sems[sk], 16 if is_dma else 1)
            return run
        block.vector(make(self.streams['v']))
        block.scalar(make(self.streams['s']))
        block.tensor(make(self.streams['t']))
        block.gpsimd(make(self.streams['g']))
        block.sync(make(self.streams['y']))

def alibi_slopes(n=16):
    start = 2.0 ** (-2.0 ** (-(math.log2(n) - 3)))
    return [start * start ** i for i in range(n)]

def host_constants():
    import ml_dtypes
    bf = lambda a: np.ascontiguousarray(a).astype(ml_dtypes.bfloat16)
    c = {}
    c['ident'] = bf(np.eye(128))
    c['identf'] = np.eye(128, dtype=np.float32)
    sw = np.zeros((128, 128), np.float32)
    for b in range(2):
        for i in range(32):
            sw[b * 64 + i, b * 64 + 32 + i] = 1.0
            sw[b * 64 + 32 + i, b * 64 + i] = 1.0
    c['swapm'] = bf(sw)
    k_ = np.arange(128)[:, None]
    r_ = np.arange(128)[None, :]
    c['tri_lo'] = bf((k_ < r_).astype(np.float32))
    c['tri_hi'] = bf((k_ > r_).astype(np.float32))
    c['negi'] = bf(np.eye(128) * NEG)
    bd = np.zeros((128, 32), np.float32)
    bd[0:64, 0] = 1.0
    bd[64:128, 1] = 1.0
    c['bdones'] = bf(bd)
    f2 = np.zeros((2, 128), np.float32)
    f2[0, 0:64] = 1.0
    f2[1, 64:128] = 1.0
    c['f2sel'] = bf(f2)
    c['onesrow'] = bf(np.ones((1, 128)))
    c['onescol'] = bf(np.ones((128, 1)))
    c['onescolf'] = np.ones((128, 1), np.float32)
    DH = 64
    half = np.arange(0, DH, 2, dtype=np.float64)
    inv_freq = 1.0 / 10000.0 ** (half / DH)
    t_ = np.arange(T, dtype=np.float64)
    freqs = np.concatenate([t_[:, None] * inv_freq[None, :]] * 2, -1)
    base_scale = (half + 0.4 * DH) / (1.4 * DH)
    power = (t_ - T // 2) / 128.0
    scale = np.concatenate([base_scale[None, :] ** power[:, None]] * 2, -1)
    sgn = np.where(np.arange(DH) < 32, -1.0, 1.0)[:, None]
    c['rotc'] = bf(np.vstack([(np.cos(freqs) * scale).T] * 2))
    c['rots'] = bf(np.vstack([(np.sin(freqs) * scale).T * sgn] * 2))
    c['rotci'] = bf(np.vstack([(np.cos(freqs) / scale).T] * 2))
    c['rotsi'] = bf(np.vstack([(np.sin(freqs) / scale).T * sgn] * 2))
    return c

def build(upto='all', taps=(), race_check=True, n_sub=6):
    nc = bass.Bass('TRN2', target_bir_lowering=False, detect_race_conditions=race_check)
    P = Prog()
    consts = host_constants()
    er = contextlib.ExitStack()

    def din(name, shape, dt=F32):
        return nc.dram_tensor(name, list(shape), dt, kind='ExternalInput').ap()
    X_in = din('x', (T, D))
    OUT = nc.dram_tensor('out', [T, D], F32, kind='ExternalOutput').ap()
    WATT = din('w_attnT', (D, 3 * D))
    WPRJ = din('w_projT', (D, D))
    WFC = din('w_fcT', (D, 4096))
    WPR2 = din('w_prT', (4096, D))
    WQKV1 = din('wqkv1T', (D, 3 * D))
    WO1 = din('wo1T', (D, D))
    BQKV1 = din('bqkv1', (3, 128, 8))
    BO1 = din('bo1', (128, 8))
    NQW = din('nqw', (128, 8))
    NKW = din('nkw', (128, 8))
    WQKV6 = din('wqkv6', (6, D, 3 * D))
    WO6 = din('wo6', (6, D, D))
    QS6 = din('qs6', (6, 128, 1))
    KS6 = din('ks6', (6, 128, 1))
    LNG6 = din('lng6', (6, 128, 8))
    LNB6 = din('lnb6', (6, 128, 8))
    W1A6 = din('w1a6', (6, D, DIP))
    W1G6 = din('w1g6', (6, D, DIP))
    B1A6 = din('b1a6', (6, 128, NK2))
    B1G6 = din('b1g6', (6, 128, NK2))
    W26 = din('w26', (6, DIP, D))
    B26 = din('b26', (6, 128, 8))
    CON = {k: din('c_' + k, v.shape, F32 if v.dtype == np.float32 else BF16) for k, v in consts.items() if k not in ('identf', 'onescolf')}
    XSP = nc.dram_tensor('xspill', [128, NC * T], F32, kind='Internal').ap()
    WFCBF = nc.dram_tensor('wfcbf', [D, 4096], BF16, kind='Internal').ap()
    WPRBF = nc.dram_tensor('wprbf', [4096, D], BF16, kind='Internal').ap()
    WQ6BF = nc.dram_tensor('wq6bf', [6, D, 3 * D], BF16, kind='Internal').ap()
    WO6BF = nc.dram_tensor('wo6bf', [6, D, D], BF16, kind='Internal').ap()
    W1A6BF = nc.dram_tensor('w1a6bf', [6, D, DIP], BF16, kind='Internal').ap()
    W1G6BF = nc.dram_tensor('w1g6bf', [6, D, DIP], BF16, kind='Internal').ap()
    W26BF = nc.dram_tensor('w26bf', [6, DIP, D], BF16, kind='Internal').ap()
    tap_toks = []

    def tap(name, ap, waits):
        if name not in taps:
            return
        tp = nc.dram_tensor('tap_' + name, list(ap.shape), ap.dtype, kind='ExternalOutput').ap()
        P.do('y', lambda e, o=tp, i=ap: e.dma_start(out=o, in_=i), waits, dma=True)
        tap_toks.append(P.tok('y'))
    sbp = lambda n, sh, dt: er.enter_context(nc.sbuf_tensor(n, list(sh), dt))
    xn = sbp('xn', (128, NC * WZ), BF16)
    gb = sbp('gb', (128, NC * T), BF16)
    CB = {k: sbp('cb_' + k, v.shape, F32 if v.dtype == np.float32 else BF16) for k, v in consts.items() if not k.startswith('rot') and k not in ('identf', 'onescolf')}
    ST = [sbp(f'st{i}', (128, 16), F32) for i in range(8)]
    sqw = sbp('sqw', (128, 1024), BF16)
    cpool = sbp('cpool', (128, 8), F32)
    CB['cpool'] = cpool
    for _ci, _cv in ((0, 1e-05), (1, 1e-06), (2, 0.2), (3, -1.0), (4, 1.0 / 1024)):
        P.do('g', lambda e, i=_ci, v=_cv, cpool=cpool: e.memset(cpool[:, i:i + 1], v))
    tok_cpool = P.tok('g')
    ctoks = []
    for k in CON:
        if k.startswith('rot'):
            continue
        P.do('y', lambda e, d=CB[k], s=CON[k]: e.dma_start(out=d[:], in_=s), dma=True)
        ctoks.append(P.tok('y'))
    tok_con = ctoks[-1]
    ident = CB['ident'][:]

    def zc(base, c, t0, t1, shift=False):
        off = 0 if shift else 1
        return base[:, c * WZ + off + t0:c * WZ + off + t1]

    class WBuf:
        def __init__(self, es, name, cols, n_stg=2, n_bf=2, engs=('g', 'v')):
            self.stg = [es.enter_context(nc.sbuf_tensor(_U(name + 'S'), [128, cols], F32))
                        for _ in range(n_stg)]
            self.bf = [es.enter_context(nc.sbuf_tensor(_U(name + 'B'), [128, cols], BF16))
                       for _ in range(n_bf)]
            self.stg_free = [None] * n_stg
            self.cast_tok = [None] * n_bf
            self.busy = [[] for _ in range(n_bf)]
            self.i = 0
            self.j = 0
            self.engs = engs
            self.cols = cols

        def load(self, src2d, extra=(), w=None):
            si = self.j % len(self.stg)
            i = self.i
            self.i = (self.i + 1) % len(self.bf)
            self.j += 1
            w = w if w is not None else self.cols
            tw = dma_ch(self.stg[si][:, 0:w], src2d,
                        [self.stg_free[si]] + list(extra))
            eng = self.engs[i % len(self.engs)]
            P.do(eng, lambda e, d=self.bf[i], s=self.stg[si], w=w:
                 e.tensor_copy(d[:, 0:w], s[:, 0:w]), [tw] + self.busy[i])
            self.cast_tok[i] = P.tok(eng)
            self.stg_free[si] = self.cast_tok[i]
            self.busy[i] = []
            self._last = i
            return self.bf[i], self.cast_tok[i]

        def load_bf(self, src2d_bf, extra=(), w=None):
            i = self.i
            self.i = (self.i + 1) % len(self.bf)
            w = w if w is not None else self.cols
            cc = src2d_bf.shape[0] // 128
            srcv = src2d_bf.rearrange('(c p) n -> p c n', p=128)
            dv = self.bf[i][:, 0:w].rearrange('p (c n) -> p c n', c=cc)
            P.do('y', lambda e, d=dv, s=srcv: e.dma_start(out=d, in_=s),
                 self.busy[i] + list(extra), dma=True)
            self.cast_tok[i] = P.tok('y')
            self.busy[i] = []
            self._last = i
            return self.bf[i], self.cast_tok[i]

        def writeback(self, dst2d_bf, after, w=None):
            i = self._last
            w = w if w is not None else self.cols
            cc = dst2d_bf.shape[0] // 128
            dstv = dst2d_bf.rearrange('(c p) n -> p c n', p=128)
            sv = self.bf[i][:, 0:w].rearrange('p (c n) -> p c n', c=cc)
            P.do('y', lambda e, d=dstv, s=sv: e.dma_start(out=d, in_=s),
                 list(after), dma=True)
            tok = P.tok('y')
            self.busy[i].append(tok)
            return tok

        def consumed(self, tok):
            self.busy[self._last].append(tok)

    def dma_ch(dst, src2d, waits=()):
        cc = src2d.shape[0] // 128
        src = src2d.rearrange('(c p) n -> p c n', p=128)
        dv = dst.rearrange('p (c n) -> p c n', c=cc)
        P.do('y', lambda e, d=dv, s=src: e.dma_start(out=d, in_=s), waits, dma=True)
        return P.tok('y')

    def _ln_combine(ntiles, eps, tv, ts_):
        su, ssq, meanb, varb, rstd, m2s, vr2, mr = [s[:] for s in ST]
        sl = slice(0, ntiles)
        P.do('v', lambda e, varb=varb, sl=sl, ssq=ssq: e.tensor_scalar(varb[:, sl], ssq[:, sl], 1.0 / 1024, None, ALU.mult), [ts_])
        P.do('s', lambda e, meanb=meanb, sl=sl, su=su, cpool=cpool: e.activation(meanb[:, sl], su[:, sl], AF.Identity, scale=cpool[:, 4:5]), [tv, tok_cpool])
        c2 = P.tok('s')
        P.do('v', lambda e, m2s=m2s, sl=sl, meanb=meanb: e.tensor_mul(m2s[:, sl], meanb[:, sl], meanb[:, sl]), [c2])
        c3 = P.tok('v')
        P.do('g', lambda e, vr2=vr2, sl=sl, varb=varb, m2s=m2s: e.tensor_sub(vr2[:, sl], varb[:, sl], m2s[:, sl]), [c3])
        c4 = P.tok('g')
        epcol = 0 if abs(eps - 1e-05) < 1e-12 else 1
        P.do('s', lambda e, ec=epcol, vr2=vr2, sl=sl, cpool=cpool: e.activation(vr2[:, sl], vr2[:, sl], AF.Sqrt, bias=cpool[:, ec:ec + 1]), [c4, tok_cpool])
        c5 = P.tok('s')
        P.do('v', lambda e, rstd=rstd, sl=sl, vr2=vr2: e.reciprocal(rstd[:, sl], vr2[:, sl]), [c5])
        c6 = P.tok('v')
        P.do('g', lambda e, mr=mr, sl=sl, meanb=meanb, rstd=rstd: e.tensor_tensor(mr[:, sl], meanb[:, sl], rstd[:, sl], ALU.mult), [c6, c2])
        c7 = P.tok('g')
        P.do('v', lambda e, m2s=m2s, sl=sl, mr=mr: e.tensor_scalar(m2s[:, sl], mr[:, sl], -1.0, None, ALU.mult), [c7])
        c8 = P.tok('v')
        return (c8, c6)

    def emit_ln_tm(src_tile, ntiles, dst_fn, tag, eps, waits=()):
        su, ssq, meanb, varb, rstd, m2s, vr2, mr = [s[:] for s in ST]
        out_toks = []
        with contextlib.ExitStack() as es:
            tmpb = [es.enter_context(nc.sbuf_tensor(_U(f'{tag}tm{i}'), [128, 1024], BF16)) for i in range(2)]
            bkp = [es.enter_context(nc.psum_tensor(_U(f'{tag}bk{i}'), [128, 1024], BF16)) for i in range(2)]
            tv = ts_ = None
            for a in range(ntiles):
                P.do('v', lambda e, aa=a, su=su, src_tile=src_tile: e.reduce_sum(su[:, aa:aa + 1], src_tile(aa), axis=AX.X), list(waits))
                tv = P.tok('v')
                P.do('s', lambda e, aa=a, sqw=sqw, src_tile=src_tile, ssq=ssq: e.activation(sqw[:], src_tile(aa), AF.Square, accum_out=ssq[:, aa:aa + 1]), list(waits))
                ts_ = P.tok('s')
            c8, c6 = _ln_combine(ntiles, eps, tv, ts_)
            bz = {}
            for a in range(ntiles):
                b = a % 2
                P.do('s', lambda e, aa=a, bb=b, tmpb=tmpb, src_tile=src_tile, m2s=m2s, rstd=rstd: e.activation(tmpb[bb][:], src_tile(aa), AF.Identity, bias=m2s[:, aa:aa + 1], scale=rstd[:, aa:aa + 1]), [c8, c6, bz.get((b, 'e'))])
                ta = P.tok('s')
                for c in range(NC):
                    P.do('t', lambda e, cc=c, bb=b, bkp=bkp, tmpb=tmpb, ident=ident: e.transpose(bkp[bb][:, cc * 128:(cc + 1) * 128], tmpb[bb][:, cc * 128:(cc + 1) * 128], ident), [ta, tok_con, bz.get((b, 'e'))] if c == 0 else [])
                tb = P.tok('t')
                P.do('v', lambda e, d=dst_fn(a), bb=b, bkp=bkp: e.tensor_copy(d, bkp[bb][:].rearrange('p (c w) -> p c w', c=NC)), [tb])
                te = P.tok('v')
                bz[b, 'e'] = te
                out_toks.append(te)
        return out_toks

    def emit_ln_fm(src_col, ntiles, dst_fn, tag, eps, waits=()):
        su, ssq, meanb, varb, rstd, m2s, vr2, mr = [s[:] for s in ST]
        out_toks = []
        with contextlib.ExitStack() as es:
            cst = [es.enter_context(nc.sbuf_tensor(_U(f'{tag}cs{i}'), [128, 1024], BF16)) for i in range(2)]
            trp = [es.enter_context(nc.psum_tensor(_U(f'{tag}tr{i}'), [128, 1024], BF16)) for i in range(2)]
            tmpb = [es.enter_context(nc.sbuf_tensor(_U(f'{tag}tm{i}'), [128, 1024], BF16)) for i in range(2)]
            bkp = [es.enter_context(nc.psum_tensor(_U(f'{tag}bk{i}'), [128, 1024], BF16)) for i in range(2)]
            bz = {}
            tv = ts_ = None

            def cast_and_transpose(a, b, w0):
                for c in range(NC):
                    P.do('v', lambda e, aa=a, cc=c, bb=b, cst=cst, src_col=src_col: e.tensor_copy(cst[bb][:, cc * 128:(cc + 1) * 128], src_col(aa, cc)), w0 if c == 0 else [])
                tcc = P.tok('v')
                for c in range(NC):
                    P.do('t', lambda e, cc=c, bb=b, trp=trp, cst=cst, ident=ident: e.transpose(trp[bb][:, cc * 128:(cc + 1) * 128], cst[bb][:, cc * 128:(cc + 1) * 128], ident), [tcc, tok_con] + w0 if c == 0 else [])
                return P.tok('t')
            for a in range(ntiles):
                b = a % 2
                w0 = list(waits) + [bz.get((b, 'v')), bz.get((b, 's'))]
                tt = cast_and_transpose(a, b, w0)
                P.do('v', lambda e, aa=a, bb=b, su=su, trp=trp: e.reduce_sum(su[:, aa:aa + 1], trp[bb][:], axis=AX.X), [tt])
                tv = P.tok('v')
                P.do('s', lambda e, bb=b, aa=a, sqw=sqw, trp=trp, ssq=ssq: e.activation(sqw[:], trp[bb][:], AF.Square, accum_out=ssq[:, aa:aa + 1]), [tt, tv])
                ts_ = P.tok('s')
                bz[b, 'v'] = tv
                bz[b, 's'] = ts_
            c8, c6 = _ln_combine(ntiles, eps, tv, ts_)
            bz2 = {}
            for a in range(ntiles):
                b = a % 2
                w2 = [c8, c6, bz.get((b, 'v')), bz.get((b, 's')), bz2.get((b, 'a'))]
                tt = cast_and_transpose(a, b, w2)
                P.do('s', lambda e, aa=a, bb=b, tmpb=tmpb, trp=trp, m2s=m2s, rstd=rstd: e.activation(tmpb[bb][:], trp[bb][:], AF.Identity, bias=m2s[:, aa:aa + 1], scale=rstd[:, aa:aa + 1]), [tt])
                ta = P.tok('s')
                bz[b, 'v'] = None
                bz[b, 's'] = ta
                for c in range(NC):
                    P.do('t', lambda e, cc=c, bb=b, bkp=bkp, tmpb=tmpb, ident=ident: e.transpose(bkp[bb][:, cc * 128:(cc + 1) * 128], tmpb[bb][:, cc * 128:(cc + 1) * 128], ident), [ta, bz2.get((b, 'e'))] if c == 0 else [])
                tb = P.tok('t')
                P.do('v', lambda e, d=dst_fn(a), bb=b, bkp=bkp: e.tensor_copy(d, bkp[bb][:].rearrange('p (c w) -> p c w', c=NC)), [tb])
                te = P.tok('v')
                bz2[b, 'a'] = ta
                bz2[b, 'e'] = te
                out_toks.append(te)
        return out_toks

    def attn_head(tag, hs, q_ap, k_ap, vtm_ap, p_sb, pt_sb, nmx, sums, rcs, ao_write, qt_list, q0_of, keyspec, gates, alibi_ap=None, alibi_base=None, mask_fn=None):
        he = hs + 64
        tp = (0, 0) if hs == 0 else (64, 0)
        max_klen = max(keyspec(q)[1] for q in qt_list)
        CW = 512 if max_klen <= 512 else 1024
        with contextlib.ExitStack() as eA:
            sps = [eA.enter_context(nc.psum_tensor(_U(f'{tag}sp{i}'), [128, CW], F32)) for i in range(2)]
            ptp = [eA.enter_context(nc.psum_tensor(_U(f'{tag}pp{i}'), [128, CW], BF16)) for i in range(2)]
            avp = [eA.enter_context(nc.psum_tensor(_U(f'{tag}av{i}'), [128, 128], F32)) for i in range(2)]
            nm2 = eA.enter_context(nc.sbuf_tensor(_U(tag + 'nm'), [128, 8], F32))
            sm2 = eA.enter_context(nc.sbuf_tensor(_U(tag + 'sm'), [128, 8], F32))
            sps_busy = [gates.get('prev_exp')] * 2
            ptp_busy = [gates.get('prev_ptev')] * 2
            avp_busy = [gates.get('prev_ao')] * 2
            nm_busy = [None] * 2
            tmp_busy = [None] * 2
            sm_busy = [None] * 2
            tpt_last = gates.get('prev_tpt')
            av_last = gates.get('prev_tav')
            first = True
            pend = None

            def stage_b2(stq):
                nonlocal tpt_last, av_last
                qt, kb, klen, chs = stq['qt'], stq['kb'], stq['klen'], stq['chs']
                qp = qt % 2
                ptevs = []
                for ci, ch in enumerate(chs):
                    c0, clen, _, _ = ch
                    pp = (qt + ci) % 2
                    for kt in range(clen // 128):
                        P.do('t', lambda e, kl=kt * 128, cc0=c0, pp=pp, ptp=ptp, p_sb=p_sb, ident=ident: e.transpose(ptp[pp][:, kl:kl + 128], p_sb[:, cc0 + kl:cc0 + kl + 128], ident), [stq['norm'][ci], ptp_busy[pp], tok_con] if kt == 0 else [])
                    tpt_last = P.tok('t')
                    P.do('v', lambda e, cc0=c0, cl=clen, pp=pp, pt_sb=pt_sb, ptp=ptp: e.tensor_copy(pt_sb[:, cc0:cc0 + cl], ptp[pp][:, 0:cl]), [tpt_last, av_last])
                    ptp_busy[pp] = P.tok('v')
                    ptevs.append(ptp_busy[pp])
                gates['prev_ptev'] = ptevs[-1]
                nkt = klen // 128
                for ci, ch in enumerate(chs):
                    c0, clen, _, _ = ch
                    for kl in range(clen // 128):
                        kt = c0 // 128 + kl
                        w_ = ([ptevs[ci], avp_busy[qp]] + list(gates.get('vtm', []))) if kl == 0 else []
                        P.do('t', lambda e, k0=kt * 128, kb0=kb, first_=kt == 0, last=kt == nkt - 1, qp=qp, avp=avp, hs=hs, he=he, vtm_ap=vtm_ap, pt_sb=pt_sb: e.matmul(avp[qp][hs:he, :], vtm_ap[:, kb0 + k0 + hs:kb0 + k0 + hs + 64], pt_sb[:, k0:k0 + 128], start=first_, stop=last, tile_position=(0, hs)), w_)
                av_last = P.tok('t')
                gates['prev_tav'] = av_last
                avp_busy[qp] = ao_write(qt, avp[qp], [av_last])
                gates['prev_ao'] = avp_busy[qp]
            for qt in qt_list:
                kb, klen = keyspec(qt)
                q0 = q0_of(qt)
                nch = (klen + CW - 1) // CW
                p2 = 2 * (qt % 2)
                chs = []
                for c in range(nch):
                    c0 = c * CW
                    clen = min(CW, klen - c0)
                    par = (qt + c) % 2
                    sp_ = sps[par]
                    w0 = [sps_busy[par]]
                    if first:
                        w0 += list(gates.get('qkv', []))
                    for ks in range((clen + 511) // 512):
                        n0 = ks * 512
                        n1 = min(clen, n0 + 512)
                        P.do('t', lambda e, a0=n0, a1=n1, g0=c0, qq=q0, kb0=kb, sp=mask_fn is None, sp_=sp_, q_ap=q_ap, hs=hs, he=he, k_ap=k_ap, tp=tp: e.matmul(sp_[:, a0:a1], q_ap[hs:he, qq:qq + 128], k_ap[hs:he, kb0 + g0 + a0:kb0 + g0 + a1], start=True, stop=sp, tile_position=tp), w0 if ks == 0 else [])
                    if mask_fn is not None:
                        mask_fn(qt, sp_)
                    tsc = P.tok('t')
                    first = False
                    mcol = nm2[:, p2 + c:p2 + c + 1]
                    wmn = [tsc] + list(nm_busy[qt % 2] or [])
                    if alibi_ap is not None:
                        b0 = alibi_base(qt)
                        P.do('v', lambda e, cl=clen, bb=b0 + c0, sp_=sp_, alibi_ap=alibi_ap: e.scalar_tensor_tensor(sp_[:, 0:cl], sp_[:, 0:cl], 1.0, alibi_ap[:, bb:bb + cl], ALU.mult, ALU.add), wmn + list(gates.get('alibi', [])))
                        P.do('v', lambda e, cl=clen, mc=mcol, sp_=sp_: e.tensor_scalar(sp_[:, 0:cl], sp_[:, 0:cl], 1.0, -3e+38, ALU.mult, ALU.max, accum_out=mc))
                    else:
                        P.do('v', lambda e, cl=clen, mc=mcol, sp_=sp_: e.tensor_scalar(sp_[:, 0:cl], sp_[:, 0:cl], 1.0, -3e+38, ALU.mult, ALU.max, accum_out=mc), wmn)
                    chs.append((c0, clen, par, P.tok('v')))
                if nch > 1:
                    tg2 = []
                    for c in range(2):
                        P.do('g', lambda e, pp2=p2, cc=c, nm2=nm2: e.tensor_scalar(nm2[:, 4 + pp2 + cc:5 + pp2 + cc], nm2[:, pp2 + cc:pp2 + cc + 1], -1.0, None, ALU.mult), [chs[c][3], tmp_busy[qt % 2]])
                        tg2.append(P.tok('g'))
                    P.do('v', lambda e, col=qt, pp2=p2, nmx=nmx, nm2=nm2: e.tensor_tensor(nmx[:, col:col + 1], nm2[:, 4 + pp2:5 + pp2], nm2[:, 5 + pp2:6 + pp2], ALU.min), tg2)
                    nm_busy[qt % 2] = tg2
                    tmp_busy[qt % 2] = P.tok('v')
                    tneg = tmp_busy[qt % 2]
                else:
                    P.do('g', lambda e, col=qt, pp2=p2, nmx=nmx, nm2=nm2: e.tensor_scalar(nmx[:, col:col + 1], nm2[:, pp2:pp2 + 1], -1.0, None, ALU.mult), [chs[0][3]])
                    nm_busy[qt % 2] = [P.tok('g')]
                    tneg = nm_busy[qt % 2][0]
                if pend is not None:
                    stage_b2(pend)
                exps = []
                norm = []
                for ci, ch in enumerate(chs):
                    c0, clen, par, mtok = ch
                    scol = sums[:, qt:qt + 1] if nch == 1 else sm2[:, p2 + ci:p2 + ci + 1]
                    we = [mtok, tneg, tpt_last, sm_busy[qt % 2] if nch > 1 else None]
                    P.do('s', lambda e, cc0=c0, cl=clen, col=qt, sc=scol, pp=par, p_sb=p_sb, sps=sps, nmx=nmx: e.activation(p_sb[:, cc0:cc0 + cl], sps[pp][:, 0:cl], AF.Exp, bias=nmx[:, col:col + 1], accum_out=sc), we)
                    sps_busy[par] = P.tok('s')
                    exps.append(sps_busy[par])
                gates['prev_exp'] = exps[-1]
                if nch > 1:
                    P.do('g', lambda e, col=qt, pp2=p2, sums=sums, sm2=sm2: e.tensor_tensor(sums[:, col:col + 1], sm2[:, pp2:pp2 + 1], sm2[:, pp2 + 1:pp2 + 2], ALU.add), exps)
                    sm_busy[qt % 2] = P.tok('g')
                    trc0 = sm_busy[qt % 2]
                else:
                    trc0 = exps[-1]
                P.do('v', lambda e, col=qt, rcs=rcs, sums=sums: e.reciprocal(rcs[:, col:col + 1], sums[:, col:col + 1]), [trc0])
                trc = P.tok('v')
                for ci, ch in enumerate(chs):
                    c0, clen, par, _ = ch
                    P.do('s', lambda e, cc0=c0, cl=clen, col=qt, p_sb=p_sb, rcs=rcs: e.activation(p_sb[:, cc0:cc0 + cl], p_sb[:, cc0:cc0 + cl], AF.Identity, scale=rcs[:, col:col + 1]), [trc])
                    norm.append(P.tok('s'))
                pend = dict(qt=qt, kb=kb, klen=klen, chs=chs, norm=norm)
            stage_b2(pend)
            gates['prev_tpt'] = tpt_last
    fin = []
    with contextlib.ExitStack() as EA:
        x_sb = EA.enter_context(nc.sbuf_tensor(_U('x0'), [128, NT * 1024], F32))
        P.do('y', lambda e, x_sb=x_sb, X_in=X_in: e.dma_start(out=x_sb[:].rearrange('p (a d) -> p a d', a=NT), in_=X_in.rearrange('(a p) d -> p a d', p=128)), dma=True)
        tok_x = P.tok('y')
        P.do('g', lambda e, xn=xn: e.memset(xn[:].rearrange('p (c w) -> p c w', c=NC)[:, :, 0:1], 0.0))
        tok_xnz = P.tok('g')
        xtile = lambda a: x_sb[:, a * 1024:(a + 1) * 1024]
        xn_dst = lambda a: xn[:].rearrange('p (c w) -> p c w', c=NC)[:, :, 1 + a * 128:1 + (a + 1) * 128]
        ln0 = emit_ln_tm(xtile, NT, xn_dst, 'l0', EPS0, waits=[tok_x])
        xn_gate = ln0 + [tok_xnz]
        tap('xn0', xn[:], xn_gate)
        P.barrier()
        tok_g0 = None
        if upto not in ('ln',):
            with contextlib.ExitStack() as ef:
                wb = WBuf(ef, 'fw', 4096, n_stg=1, n_bf=2)
                h_bf = ef.enter_context(nc.sbuf_tensor(_U('fh'), [128, 32 * 512], BF16))
                lanes = []
                for ln_ in range(2):
                    lanes.append(dict(
                        yt=ef.enter_context(nc.sbuf_tensor(_U(f'fy{ln_}'), [128, 256], F32)),
                        sp=ef.enter_context(nc.sbuf_tensor(_U(f'fs{ln_}'), [128, 256], F32)),
                        m0=ef.enter_context(nc.sbuf_tensor(_U(f'fm{ln_}a'), [128, 256], F32)),
                        fa=ef.enter_context(nc.sbuf_tensor(_U(f'fa{ln_}'), [128, 256], F32)),
                        free=None))
                t_wfree = xn_gate[-1]
                t_bffree = None
                t_ycopy = None
                t_spk = None
                for q in range(4):
                    tq0 = q * 512
                    t_hev = {}
                    t_hfree = t_ycopy if t_ycopy is not None else t_bffree
                    eh = contextlib.ExitStack()
                    hps = [eh.enter_context(nc.psum_tensor(_U(f'fhp{q}_{i}'), [128, 512], F32)) for i in range(2)]
                    for fs in range(8):
                        if q == 0:
                            wbf, tc = wb.load(WFC[:, fs * 512:(fs + 1) * 512],
                                              [t_wfree] if fs == 0 else ())
                            wb.writeback(WFCBF[:, fs * 512:(fs + 1) * 512], [tc])
                        else:
                            wbf, tc = wb.load_bf(WFCBF[:, fs * 512:(fs + 1) * 512])
                        t_wfree = None
                        last_mm = None
                        for m in range(4):
                            fc = fs * 4 + m
                            b = fc % 2
                            w0 = [tc, t_hev.get(b), t_hfree] + xn_gate
                            for k in range(NC):
                                P.do('t', lambda e, kk=k, mm2=m, bb=b, t0=tq0, hps=hps, wbf=wbf, zc=zc, xn=xn: e.matmul(hps[bb][:], wbf[:, kk * 512 + mm2 * 128:kk * 512 + (mm2 + 1) * 128], zc(xn, kk, t0, t0 + 512), start=kk == 0, stop=kk == NC - 1), w0 if k == 0 else [])
                            last_mm = P.tok('t')
                            wb.consumed(last_mm)
                            P.do('s', lambda e, f=fc, bb=b, h_bf=h_bf, hps=hps: e.activation(h_bf[:, f * 512:(f + 1) * 512], hps[bb][:], AF.Relu), [last_mm])
                            tr = P.tok('s')
                            P.do('v', lambda e, f=fc, h_bf=h_bf: e.tensor_mul(h_bf[:, f * 512:(f + 1) * 512], h_bf[:, f * 512:(f + 1) * 512], h_bf[:, f * 512:(f + 1) * 512]), [tr])
                            t_hev[b] = P.tok('v')
                        t_bffree = last_mm
                    th_all = [t_hev.get(0), t_hev.get(1)]
                    eh.close()
                    with contextlib.ExitStack() as ey:
                        yps = [ey.enter_context(nc.psum_tensor(_U(f'fyp{q}_{i}'), [128, 512], F32)) for i in range(8)]
                        t_ymm = {}
                        for fs in range(8):
                            if q == 0:
                                wbf, tc = wb.load(WPR2[fs * 512:(fs + 1) * 512, :])
                                wb.writeback(WPRBF[fs * 512:(fs + 1) * 512, :], [tc])
                            else:
                                wbf, tc = wb.load_bf(WPRBF[fs * 512:(fs + 1) * 512, :])
                            last_mm = None
                            for of in range(8):
                                w0 = [tc] + (th_all if fs == 0 else [])
                                for m in range(4):
                                    fc = fs * 4 + m
                                    P.do('t', lambda e, oo=of, mm2=m, f=fc, yps=yps, wbf=wbf, h_bf=h_bf: e.matmul(yps[oo][:], wbf[:, mm2 * 1024 + oo * 128:mm2 * 1024 + (oo + 1) * 128], h_bf[:, f * 512:(f + 1) * 512], start=f == 0, stop=f == 31), w0 if m == 0 else [])
                                last_mm = P.tok('t')
                                t_ymm[of] = last_mm
                            wb.consumed(last_mm)
                            t_bffree = last_mm
                        for of in range(8):
                            for hf in range(2):
                                ln_ = 0 if (of * 2 + hf) % 3 == 0 else 1
                                eng = ('g', 'v')[ln_]
                                L = lanes[ln_]
                                yt, spk, facc = (L['yt'], L['sp'], L['fa'])
                                P.do('s', lambda e, oo=of, h0=hf * 256, y_t=yt, yps=yps: e.copy(y_t[:], yps[oo][:, h0:h0 + 256]), [t_ymm[of], L['free']])
                                ty = P.tok('s')
                                P.do(eng, lambda e, spk=spk, y_t=yt: e.tensor_scalar(spk[:], y_t[:], 1.0, None, ALU.is_gt), [ty])
                                P.do(eng, lambda e, facc=facc, spk=spk: e.tensor_copy(facc[:], spk[:]))
                                for si in range(4):
                                    a_ = yt if si == 0 else L['m0']
                                    b_ = L['m0']
                                    P.do(eng, lambda e, aa=a_, bb=b_: e.tensor_scalar(bb[:], aa[:], 0.9, None, ALU.mult))
                                    P.do(eng, lambda e, bb=b_, y_t=yt: e.tensor_add(bb[:], bb[:], y_t[:]))
                                    P.do(eng, lambda e, bb=b_, spk=spk: e.tensor_sub(bb[:], bb[:], spk[:]))
                                    P.do(eng, lambda e, bb=b_, spk=spk: e.tensor_scalar(spk[:], bb[:], 1.0, None, ALU.is_gt))
                                    P.do(eng, lambda e, facc=facc, spk=spk: e.tensor_add(facc[:], facc[:], spk[:]))
                                tg = P.tok(eng)
                                P.do('s', lambda e, oo=of, t0=tq0 + hf * 256, gb=gb, facc=facc, cpool=cpool: e.activation(gb[:, oo * T + t0:oo * T + t0 + 256], facc[:], AF.Silu, scale=cpool[:, 2:3]), [tg, tok_cpool])
                                L['free'] = P.tok('s')
                                t_spk = L['free']
                                t_ycopy = ty
                tok_g0 = t_spk
            tap('g0', gb[:], [tok_g0])
        P.barrier()
        if upto not in ('ln', 'ffn0'):
            slopes = alibi_slopes(16)
            with contextlib.ExitStack() as ea:
                ao = ea.enter_context(nc.sbuf_tensor(_U('ao'), [128, 4 * T], BF16))
                nmx = ea.enter_context(nc.sbuf_tensor(_U('anm'), [128, 16], F32))
                sums = ea.enter_context(nc.sbuf_tensor(_U('asm'), [128, 16], F32))
                rcs = ea.enter_context(nc.sbuf_tensor(_U('arc'), [128, 16], F32))
                t_xadd = {}
                tok_g_pg = []
                for pg in range(2):
                    P.barrier()
                    tok_ao_pg = []
                    with contextlib.ExitStack() as ep:
                        relm = ep.enter_context(nc.sbuf_tensor(_U('arel'), [128, 2688], BF16))
                        am = ep.enter_context(nc.sbuf_tensor(_U('aam'), [128, 2688], BF16))
                        wba = WBuf(ep, 'aw', 1024)
                        q_sb = ep.enter_context(nc.sbuf_tensor(_U('aq'), [128, T], BF16))
                        k_sb = ep.enter_context(nc.sbuf_tensor(_U('ak'), [128, T], BF16))
                        v_sb = ep.enter_context(nc.sbuf_tensor(_U('av_'), [128, T], BF16))
                        vtm = ep.enter_context(nc.sbuf_tensor(_U('avt'), [128, T], BF16))
                        pt_sb = ep.enter_context(nc.sbuf_tensor(_U('apt'), [128, T], BF16))
                        P.do('g', lambda e, relm=relm: e.iota(relm[:], [[1, 2688]], base=-2176, channel_multiplier=-1, allow_small_or_imprecise_dtypes=True), [tok_g0] if tok_g0 else [])
                        P.do('g', lambda e, am=am, relm=relm: e.tensor_scalar(am[:], relm[:], 0.0, None, ALU.is_gt))
                        P.do('g', lambda e, am=am: e.tensor_scalar(am[:], am[:], NEG, None, ALU.mult))
                        P.do('g', lambda e, relm=relm, am=am: e.tensor_add(relm[:], relm[:], am[:]))
                        tok_rel = P.tok('g')
                        t_wfree = tok_g0
                        t_qkvfree = None
                        gates = {}
                        for pi in range(4):
                            pr = pg * 4 + pi
                            p_sb = v_sb
                            tjs = []
                            last_wmm = t_qkvfree
                            for j, off in enumerate((pr * 128, D + pr * 128, 2 * D + pr * 128)):
                                wbf, tc = wba.load(WATT[:, off:off + 128],
                                                   [t_wfree] if t_wfree else ())
                                t_wfree = None
                                dst, scl = ((q_sb, 0.125), (k_sb, None), (v_sb, None))[j]
                                with contextlib.ExitStack() as eq:
                                    qps = [eq.enter_context(nc.psum_tensor(_U(f'aqp{pr}_{j}_{i}'), [128, 512], F32)) for i in range(2)]
                                    tev = {}
                                    for sp in range(4):
                                        b = sp % 2
                                        w0 = [tc, tev.get(b)] + xn_gate + ([gates.get('prev_exp'), gates.get('prev_tav'), gates.get('prev_ao'), gates.get('prev_ptev')] if j == 0 else [])
                                        for k in range(NC):
                                            P.do('t', lambda e, kk=k, ss=sp, bb=b, qps=qps, wbf=wbf, zc=zc, xn=xn: e.matmul(qps[bb][:], wbf[:, kk * 128:(kk + 1) * 128], zc(xn, kk, ss * 512, (ss + 1) * 512), start=kk == 0, stop=kk == NC - 1), w0 if k == 0 else [])
                                        tm = P.tok('t')
                                        last_wmm = tm
                                        wba.consumed(tm)
                                        if scl is not None:
                                            P.do('v', lambda e, d=dst, ss=sp, bb=b, s_=scl, qps=qps: e.tensor_scalar(d[:, ss * 512:(ss + 1) * 512], qps[bb][:], s_, None, ALU.mult), [tm])
                                            tev[b] = P.tok('v')
                                        else:
                                            P.do('s', lambda e, d=dst, ss=sp, bb=b, qps=qps: e.copy(d[:, ss * 512:(ss + 1) * 512], qps[bb][:]), [tm])
                                            tev[b] = P.tok('s')
                                    tjs += [tev.get(0), tev.get(1)]
                            t_qkvfree = last_wmm
                            with contextlib.ExitStack() as ev:
                                vtp = ev.enter_context(nc.psum_tensor(_U(f'avp{pr}'), [128, 2048], BF16))
                                for a in range(NT):
                                    P.do('t', lambda e, aa=a, vtp=vtp, v_sb=v_sb, ident=ident: e.transpose(vtp[:, aa * 128:(aa + 1) * 128], v_sb[:, aa * 128:(aa + 1) * 128], ident), tjs + [gates.get('prev_tav'), gates.get('prev_ao'), gates.get('prev_ptev')] if a == 0 else [])
                                tvt = P.tok('t')
                                P.do('v', lambda e, vtm=vtm, vtp=vtp: e.tensor_copy(vtm[:], vtp[:]), [tvt])
                                tok_vtm = P.tok('v')
                            if pr == 0:
                                tap('q0', q_sb[:], tjs)
                                tap('k0', k_sb[:], tjs)
                                tap('vtm0', vtm[:], [tok_vtm])
                            for hh in range(2):
                                head = pr * 2 + hh
                                P.do('g', lambda e, sl=slopes[head], am=am, relm=relm: e.tensor_scalar(am[:], relm[:], sl, None, ALU.mult), [tok_rel, gates.get('prev_exp')])
                                tok_am = P.tok('g')
                                gates['qkv'] = tjs + [tok_vtm]
                                gates['vtm'] = [tok_vtm]
                                gates['alibi'] = [tok_am]

                                def ao_w(qt, avp, w, pi=pi, hh=hh):
                                    P.do('v', lambda e, col=qt, h0=hh * 64, ao=ao, pi=pi, avp=avp: e.tensor_copy(ao[h0:h0 + 64, pi * T + col * 128:pi * T + (col + 1) * 128], avp[h0:h0 + 64, :]), w)
                                    tok_ao_pg.append(P.tok('v'))
                                    return tok_ao_pg[-1]
                                attn_head(f'a{pr}h{hh}', hh * 64, q_sb, k_sb, vtm, p_sb, pt_sb, nmx, sums, rcs, ao_w, list(range(NT)), lambda qt: qt * 128, lambda qt: (0, (qt + 1) * 128), gates, alibi_ap=am[:], alibi_base=lambda qt: 2176 - qt * 128)
                    tap(f'ao{pg}', ao[:], tok_ao_pg[-1:])
                    P.barrier()
                    with contextlib.ExitStack() as ew:
                        wstg2 = ew.enter_context(nc.sbuf_tensor(_U('pw'), [128, 4096], F32))
                        wbf2 = ew.enter_context(nc.sbuf_tensor(_U('pwb'), [128, 4096], BF16))
                        tmp = [ew.enter_context(nc.sbuf_tensor(_U(f'pt{i}'), [128, 512], BF16)) for i in range(2)]
                        pps = [ew.enter_context(nc.psum_tensor(_U(f'pp{i}'), [128, 512], F32)) for i in range(2)]
                        ptp = [ew.enter_context(nc.psum_tensor(_U(f'ptr{i}'), [128, 512], BF16)) for i in range(2)]
                        tev = {}
                        ttr = {}
                        cnt = 0
                        tw = dma_ch(wstg2[:], WPRJ[pg * 512:(pg + 1) * 512, :], tok_ao_pg[-1:])
                        P.do('g', lambda e, wbf2=wbf2, wstg2=wstg2: e.tensor_copy(wbf2[:], wstg2[:]), [tw])
                        tc = P.tok('g')
                        for of in range(8):
                            for sp in range(4):
                                b = cnt % 2
                                cnt += 1
                                w0 = [tc, tev.get(b)] + tok_ao_pg[-1:]
                                for k in range(4):
                                    P.do('t', lambda e, kk=k, oo=of, ss=sp, bb=b, pps=pps, wbf2=wbf2, ao=ao: e.matmul(pps[bb][:], wbf2[:, kk * 1024 + oo * 128:kk * 1024 + (oo + 1) * 128], ao[:, kk * T + ss * 512:kk * T + (ss + 1) * 512], start=kk == 0, stop=kk == 3), w0 if k == 0 else [])
                                tm = P.tok('t')
                                P.do('v', lambda e, bb=b, oo=of, ss=sp, tmp=tmp, pps=pps, gb=gb: e.tensor_tensor(tmp[bb][:], pps[bb][:], gb[:, oo * T + ss * 512:oo * T + (ss + 1) * 512], ALU.mult), [tm, tok_g0, ttr.get(bb if False else b)])
                                tv = P.tok('v')
                                tev[b] = tv
                                for j in range(4):
                                    P.do('t', lambda e, bb=b, jj=j, ptp=ptp, tmp=tmp, ident=ident: e.transpose(ptp[bb][:, jj * 128:(jj + 1) * 128], tmp[bb][:, jj * 128:(jj + 1) * 128], ident), [tv, t_xadd.get(b)] if j == 0 else [])
                                tt2 = P.tok('t')
                                ttr[b] = tt2
                                dstx = x_sb[:].rearrange('p (a d) -> p a d', a=NT)[:, sp * 4:sp * 4 + 4, of * 128:(of + 1) * 128]
                                P.do('v', lambda e, d=dstx, bb=b, ptp=ptp: e.tensor_tensor(d, d, ptp[bb][:].rearrange('p (j w) -> p j w', j=4), ALU.add), [tt2, tok_x])
                                t_xadd[b] = P.tok('v')
                    tok_g_pg = [t_xadd.get(0), t_xadd.get(1)]
            fin = [t for t in tok_g_pg if t]
            tap('xb0', x_sb[:], fin)
        if upto in ('ln', 'ffn0', 'b0'):
            if not fin:
                fin = xn_gate + ([tok_g0] if tok_g0 else [])
            P.do('y', lambda e, OUT=OUT, x_sb=x_sb: e.dma_start(out=OUT.rearrange('(a p) d -> p a d', p=128), in_=x_sb[:].rearrange('p (a d) -> p a d', a=NT)), fin + [tok_x], dma=True)
            tout = P.tok('y')
            P.do('g', lambda e: e.nop(), [tout] + tap_toks)
            P.do('y', lambda e: e.nop(), [tout] + tap_toks)
            return finalize(nc, P, er)
        P.barrier()
        ln1 = emit_ln_tm(xtile, NT, xn_dst, 'l1', EPS0, waits=fin)
        xn_gate = ln1
        tap('xn1', xn[:], ln1)
        P.do('y', lambda e, XSP=XSP, x_sb=x_sb: e.dma_start(out=XSP, in_=x_sb[:]), fin + ln1, dma=True)
        tok_spill = P.tok('y')
        fin = ln1 + [tok_spill]
    fin = emit_l1_ffn(nc, P, xn, gb, CB, ST, None, CON, tok_con, WQKV6, WO6, QS6, KS6, LNG6, LNB6, W1A6, W1G6, B1A6, B1G6, W26, B26, fin, tap, attn_head, emit_ln_fm, zc, dma_ch, upto, n_sub, WBuf, (WQ6BF, WO6BF, W1A6BF, W1G6BF, W26BF))
    if upto == 'ffn1':
        P.do('y', lambda e, OUT=OUT, XSP=XSP: e.dma_start(out=OUT.rearrange('t d -> (t d)'), in_=XSP.rearrange('p n -> (p n)')), fin, dma=True)
        fin = [P.tok('y')]
        P.do('g', lambda e: e.nop(), fin + tap_toks)
        P.do('y', lambda e: e.nop(), fin + tap_toks)
        return finalize(nc, P, er)
    emit_l1_attn(nc, P, xn, gb, CB, ST, None, tok_con, WQKV1, WO1, BQKV1, BO1, NQW, NKW, XSP, OUT, fin, tap, tap_toks, attn_head, zc, dma_ch, WBuf)
    return finalize(nc, P, er)

def finalize(nc, P, er=None):
    sems = {}
    with contextlib.ExitStack() as esem:
        for k in ('v', 's', 't', 'g', 'y', 'vd', 'sd', 'td', 'gd'):
            sems[k] = esem.enter_context(nc.semaphore('sem_' + k))
        with nc.Block() as block:
            P.emit(nc, block, sems)
    return nc

def emit_l1_ffn(nc, P, xn, gb, CB, ST, sqs, CON, tok_con, WQKV6, WO6, QS6, KS6, LNG6, LNB6, W1A6, W1G6, B1A6, B1G6, W26, B26, fin, tap, attn_head, emit_ln, zc, dma_ch, upto, n_sub, WBuf=None, WCACHE=None):
    ident = CB['ident'][:]
    with contextlib.ExitStack() as EZ:
        z_sb = EZ.enter_context(nc.sbuf_tensor(_U('z'), [128, NC * WZ], F32))
        nmx = EZ.enter_context(nc.sbuf_tensor(_U('znm'), [128, 16], F32))
        sums = EZ.enter_context(nc.sbuf_tensor(_U('zsm'), [128, 16], F32))
        rcs = EZ.enter_context(nc.sbuf_tensor(_U('zrc'), [128, 16], F32))
        lnbb = EZ.enter_context(nc.sbuf_tensor(_U('lnbb'), [128, 8], BF16))
        beffa = EZ.enter_context(nc.sbuf_tensor(_U('beffa'), [128, NK2], F32))
        beffg = EZ.enter_context(nc.sbuf_tensor(_U('beffg'), [128, NK2], F32))
        P.do('g', lambda e, z_sb=z_sb: e.memset(z_sb[:].rearrange('p (c w) -> p c w', c=NC)[:, :, 0:1], 0.0), fin)
        P.do('v', lambda e, z_sb=z_sb, xn=xn: e.tensor_copy(z_sb[:].rearrange('p (c w) -> p c w', c=NC)[:, :, 1:WZ], xn[:].rearrange('p (c w) -> p c w', c=NC)[:, :, 1:WZ]), fin)
        tz = P.tok('v')
        tzg = P.tok('g')
        z_gate = [tz, tzg]
        WQ6BF, WO6BF, W1A6BF, W1G6BF, W26BF = WCACHE
        pstg = [EZ.enter_context(nc.sbuf_tensor(_U(f'pws{i}'), [128, 1024], F32)) for i in range(2)]
        pbf0 = EZ.enter_context(nc.sbuf_tensor(_U('pwb0'), [128, 1024], BF16))
        pbf = [pbf0, pbf0]
        plg = EZ.enter_context(nc.sbuf_tensor(_U('plg'), [128, 8], F32))
        pst = {'f': [None, None], 'b': [None, None], 'lg': None, 'i': 0}

        def prep_l1f(li):
            def slab(src2d, dst2d, scale_col, extra=()):
                i = pst['i'] % 2
                pst['i'] += 1
                n = src2d.shape[1]
                P.do('g', lambda e, d=pstg[i][:, 0:n], s=src2d: e.dma_start(out=d, in_=s), [pst['f'][i]] + list(extra), dma=True)
                tin = P.tok('g')
                if scale_col is not None:
                    P.do('g', lambda e, ii=i, n=n, sc=scale_col, pbf=pbf, pstg=pstg: e.tensor_scalar(pbf[ii][:, 0:n], pstg[ii][:, 0:n], 1.0, sc, ALU.mult, ALU.mult), [tin, pst['b'][0], pst['b'][1]])
                else:
                    P.do('g', lambda e, ii=i, n=n, pbf=pbf, pstg=pstg: e.tensor_copy(pbf[ii][:, 0:n], pstg[ii][:, 0:n]), [tin, pst['b'][0], pst['b'][1]])
                tcst = P.tok('g')
                pst['f'][i] = tcst
                P.do('g', lambda e, d=dst2d, s=pbf[i][:, 0:n]: e.dma_start(out=d, in_=s), [tcst], dma=True)
                pst['b'][i] = P.tok('g')
                return tcst
            out = {}

            def part_qo():
                for r in range(0, D, 128):
                    for c0 in range(0, 3 * D, 1024):
                        slab(WQKV6[li][r:r + 128, c0:c0 + 1024], WQ6BF[li][r:r + 128, c0:c0 + 1024], None)
                out['wq'] = pst['b'][(pst['i'] + 1) % 2]
                for r in range(0, D, 128):
                    slab(WO6[li][r:r + 128, :], WO6BF[li][r:r + 128, :], None)
                out['wo'] = pst['b'][(pst['i'] + 1) % 2]

            def part_w1a():
                P.do('g', lambda e, d=plg, s=LNG6[li]: e.dma_start(out=d[:], in_=s), [pst['lg']], dma=True)
                tlg = P.tok('g')
                for dk in range(8):
                    for c0 in range(0, DIP, 1024):
                        w = min(1024, DIP - c0)
                        ex = [tlg] if dk == 0 and c0 == 0 else ()
                        pst['lsc'] = slab(W1A6[li][dk * 128:(dk + 1) * 128, c0:c0 + w], W1A6BF[li][dk * 128:(dk + 1) * 128, c0:c0 + w], plg[:, dk:dk + 1], ex)

            def part_w1g():
                for dk in range(8):
                    for c0 in range(0, DIP, 1024):
                        w = min(1024, DIP - c0)
                        pst['lsc'] = slab(W1G6[li][dk * 128:(dk + 1) * 128, c0:c0 + w], W1G6BF[li][dk * 128:(dk + 1) * 128, c0:c0 + w], plg[:, dk:dk + 1])
                pst['lg'] = pst['lsc']
                out['w1'] = pst['b'][(pst['i'] + 1) % 2]

            def part_w2():
                for r in range(0, DIP, 128):
                    slab(W26[li][r:r + 128, :], W26BF[li][r:r + 128, :], None)
                out['w2'] = pst['b'][(pst['i'] + 1) % 2]
            return out, [part_qo, part_w1a, part_w1g, part_w2]
        out0, parts0 = prep_l1f(0)
        for _p in parts0:
            _p()
        preps = [out0]
        nsteps = int(round(n_sub * 2))
        for li in range((nsteps + 1) // 2):
            P.do('v', lambda e, gb=gb, z_sb=z_sb: e.tensor_copy(gb[:].rearrange('p (c w) -> p c w', c=NC), z_sb[:].rearrange('p (c w) -> p c w', c=NC)[:, :, 1:WZ]), z_gate)
            t_zbf = P.tok('v')

            def zb_rhs(k, t0, t1):
                if k < 4:
                    return gb[:, k * T + t0:k * T + t1]
                if t0 == 0:
                    return gb[:, k * T + 0:k * T + t1 - 1]
                return gb[:, k * T + t0 - 1:k * T + t1 - 1]
            tok_z1 = emit_l1f_attn(nc, P, z_sb, gb, zb_rhs, CB, CON, tok_con, nmx, sums, rcs, sqs, WQ6BF, WO6BF, QS6, KS6, li, [t_zbf] + z_gate, attn_head, dma_ch, preps[li])
            z_gate = tok_z1
            tap(f'z_a{li}', z_sb[:], tok_z1)
            if 2 * li + 1 >= nsteps:
                break
            P.barrier(('v', 's', 't', 'y'))
            pparts = None
            if li + 1 < (nsteps + 1) // 2:
                out_n, pparts = prep_l1f(li + 1)
                preps.append(out_n)
            tok_z2 = emit_l1f_ffn(nc, P, z_sb, xn, CB, ST, sqs, tok_con, LNG6, LNB6, W1A6, W1G6, B1A6, B1G6, W26, B26, lnbb, beffa, beffg, li, z_gate, emit_ln, zc, dma_ch, (W1A6BF, W1G6BF, W26BF, preps[li]), pparts)
            z_gate = tok_z2
            tap(f'z_f{li}', z_sb[:], tok_z2)
        toks = []
        for c in range(NC):
            P.do('s', lambda e, cc=c, gb=gb, zc=zc, z_sb=z_sb: e.activation(gb[:, cc * T:(cc + 1) * T], zc(z_sb, cc, 0, T), AF.Silu), z_gate)
            toks.append(P.tok('s'))
        tap('g2', gb[:], toks)
    return toks

def emit_l1f_attn(nc, P, z_sb, gb, zb_rhs, CB, CON, tok_con, nmx, sums, rcs, sqs, WQ6BF, WO6BF, QS6, KS6, li, gate0, attn_head, dma_ch, prep=None):
    ident = CB['ident'][:]
    t_zadd = {}
    with contextlib.ExitStack() as EO:
        qsc = EO.enter_context(nc.sbuf_tensor(_U('qsc'), [128, 1], F32))
        ksc = EO.enter_context(nc.sbuf_tensor(_U('ksc'), [128, 1], F32))
        P.do('y', lambda e, qsc=qsc, QS6=QS6, li=li: e.dma_start(out=qsc[:], in_=QS6[li]), list(gate0), dma=True)
        P.do('y', lambda e, ksc=ksc, KS6=KS6, li=li: e.dma_start(out=ksc[:], in_=KS6[li]), dma=True)
        tok_qs = P.tok('y')
        for pg in range(2):
            P.barrier()
            tok_ao = []
            eg_outer = contextlib.ExitStack()
            ao = eg_outer.enter_context(nc.sbuf_tensor(_U('zao'), [128, 4 * T], BF16))
            with contextlib.ExitStack() as ep:
                rotb = {k: ep.enter_context(nc.sbuf_tensor(_U('rb_') + k, [128, T], BF16)) for k in ('rotc', 'rots', 'rotci', 'rotsi')}
                rtk = []
                for k in ('rotc', 'rots', 'rotci', 'rotsi'):
                    P.do('y', lambda e, d=rotb[k], s=CON[k]: e.dma_start(out=d[:], in_=s), list(gate0), dma=True)
                    rtk.append(P.tok('y'))
                tok_rot = rtk[-1]
                wbf = [ep.enter_context(nc.sbuf_tensor(_U(f'zwb{i}'), [128, 1024], BF16)) for i in range(2)]
                q_sb = ep.enter_context(nc.sbuf_tensor(_U('zq'), [128, T], BF16))
                k_sb = ep.enter_context(nc.sbuf_tensor(_U('zk'), [128, T], BF16))
                v_sb = ep.enter_context(nc.sbuf_tensor(_U('zv'), [128, T], BF16))
                vtm = ep.enter_context(nc.sbuf_tensor(_U('zvt'), [128, T], BF16))
                pt_sb = ep.enter_context(nc.sbuf_tensor(_U('zpt'), [128, 384], BF16))
                p_sb = ep.enter_context(nc.sbuf_tensor(_U('zp'), [128, 384], BF16))
                fs_b = [ep.enter_context(nc.sbuf_tensor(_U(f'zfb{i}'), [2, 512], BF16)) for i in range(2)]
                t1b = [ep.enter_context(nc.sbuf_tensor(_U(f'zt1{i}'), [128, 512], BF16)) for i in range(2)]
                t2b = [ep.enter_context(nc.sbuf_tensor(_U(f'zt2{i}'), [128, 512], BF16)) for i in range(2)]
                sq2 = [ep.enter_context(nc.sbuf_tensor(_U(f'zsq{i}'), [128, 512], BF16)) for i in range(2)]
                pw = prep.get('wq') if prep is not None else None
                wfree = {0: pw, 1: pw}
                gates = {}
                prevs = {}
                for pi in range(4):
                    pr = pg * 4 + pi
                    tjs = []
                    for j, off in enumerate((pr * 128, D + pr * 128, 2 * D + pr * 128)):
                        wi = (pr * 3 + j) % 2
                        tc = dma_ch(wbf[wi][:], WQ6BF[li][:, off:off + 128], [wfree[wi]])
                        wb_ = wbf[wi]
                        dst = (q_sb, k_sb, v_sb)[j]
                        with contextlib.ExitStack() as eq:
                            qps = [eq.enter_context(nc.psum_tensor(_U(f'zqp{i}'), [128, 512], F32)) for i in range(2)]
                            sqp = [eq.enter_context(nc.psum_tensor(_U(f'zsqp{i}'), [32, 512], F32)) for i in range(2)]
                            f2p = [eq.enter_context(nc.psum_tensor(_U(f'zf2p{i}'), [128, 512], F32)) for i in range(2)]
                            swp = [eq.enter_context(nc.psum_tensor(_U(f'zswp{i}'), [128, 512], F32)) for i in range(2)]
                            tev = prevs.setdefault('tev', {})
                            tsqd = prevs.setdefault('tsqd', {})
                            for sp in range(4):
                                b = sp % 2
                                t0 = sp * 512
                                w0 = [tc, tev.get(b), tsqd.get(b)] + gate0 + ([gates.get('prev_exp'), gates.get('prev_tav'), gates.get('prev_ao'), gates.get('prev_ptev')] if j == 0 else [])
                                for k in range(NC):
                                    sh = k >= 4
                                    rhs = zb_rhs(k, t0, t0 + 512)
                                    o0 = 1 if sh and t0 == 0 else 0
                                    P.do('t', lambda e, kk=k, bb=b, r=rhs, oo=o0, qps=qps, wb_=wb_: e.matmul(qps[bb][:, oo:512], wb_[:, kk * 128:(kk + 1) * 128], r, start=kk == 0, stop=kk == NC - 1), w0 if k == 0 else [])
                                tm = P.tok('t')
                                wfree[wi] = tm
                                if j == 2:
                                    P.do('s', lambda e, ss=sp, bb=b, v_sb=v_sb, qps=qps: e.copy(v_sb[:, ss * 512:(ss + 1) * 512], qps[bb][:]), [tm])
                                    tev[b] = P.tok('s')
                                    tjs.append(tev[b])
                                    continue
                                P.do('s', lambda e, bb=b, sq2=sq2, qps=qps: e.activation(sq2[bb][:], qps[bb][:], AF.Square), [tm, prevs.get(('tssp', b))])
                                tsq = P.tok('s')
                                tsqd[b] = tsq
                                scl = 8.0 if j == 0 else 1.0
                                scol = qsc if j == 0 else ksc
                                P.do('v', lambda e, d=dst, ss=sp, bb=b, s_=scl, sc=scol, qps=qps: e.tensor_scalar(d[:, ss * 512:(ss + 1) * 512], qps[bb][:], s_, sc[:], ALU.mult, ALU.mult), [tm, tok_qs, tsq])
                                tev[b] = P.tok('v')
                                P.do('t', lambda e, bb=b, sqp=sqp, CB=CB, sq2=sq2: e.matmul(sqp[bb][:], CB['bdones'][:], sq2[bb][:], start=True, stop=True), [tsq, tok_con, prevs.get(('tf1', b))])
                                tssp = P.tok('t')
                                prevs[('tssp', b)] = tssp
                                P.do('s', lambda e, bb=b, fs_b=fs_b, sqp=sqp: e.activation(fs_b[bb][:], sqp[bb][0:2, :], AF.Sqrt), [tssp, prevs.get(('tf5', b))])
                                tf1 = P.tok('s')
                                prevs[('tf1', b)] = tf1
                                def _rcp_bf(e, bb=b, fs_b=fs_b, nc=nc):
                                    with nc.allow_low_precision(reason='l2norm scale is rounded to bf16 for the PE broadcast anyway'):
                                        return e.reciprocal(fs_b[bb][:], fs_b[bb][:])
                                P.do('v', _rcp_bf, [tf1])
                                tf2 = P.tok('v')
                                P.do('v', lambda e, bb=b, fs_b=fs_b: e.tensor_scalar(fs_b[bb][:], fs_b[bb][:], 1e+12, None, ALU.min), [tf2])
                                tf4 = P.tok('v')
                                P.do('t', lambda e, bb=b, f2p=f2p, CB=CB, fs_b=fs_b: e.matmul(f2p[bb][:], CB['f2sel'][:], fs_b[bb][:], start=True, stop=True), [tf4, prevs.get(('tqn', b))])
                                tf5 = P.tok('t')
                                prevs[('tf5', b)] = tf5
                                P.do('v', lambda e, d=dst, ss=sp, bb=b, f2p=f2p: e.tensor_mul(d[:, ss * 512:(ss + 1) * 512], d[:, ss * 512:(ss + 1) * 512], f2p[bb][:]), [tf5])
                                tqn = P.tok('v')
                                prevs[('tqn', b)] = tqn
                                P.do('t', lambda e, d=dst, ss=sp, bb=b, swp=swp, CB=CB: e.matmul(swp[bb][:], CB['swapm'][:], d[:, ss * 512:(ss + 1) * 512], start=True, stop=True), [tqn, prevs.get(('tr2', b))])
                                tsw = P.tok('t')
                                rc_, rs_ = ('rotc', 'rots') if j == 0 else ('rotci', 'rotsi')
                                P.do('v', lambda e, d=dst, ss=sp, bb=b, rr=rc_, t1b=t1b, rotb=rotb: e.tensor_mul(t1b[bb][:], d[:, ss * 512:(ss + 1) * 512], rotb[rr][:, ss * 512:(ss + 1) * 512]), [tqn, tok_rot, prevs.get(('tadd', b))])
                                P.do('v', lambda e, ss=sp, bb=b, rr=rs_, t2b=t2b, swp=swp, rotb=rotb: e.tensor_mul(t2b[bb][:], swp[bb][:], rotb[rr][:, ss * 512:(ss + 1) * 512]), [tsw, prevs.get(('tadd', b))])
                                tr2 = P.tok('v')
                                prevs[('tr2', b)] = tr2
                                P.do('g', lambda e, d=dst, ss=sp, bb=b, t1b=t1b, t2b=t2b: e.tensor_add(d[:, ss * 512:(ss + 1) * 512], t1b[bb][:], t2b[bb][:]), [tr2])
                                tadd = P.tok('g')
                                prevs[('tadd', b)] = tadd
                                tjs.append(tadd)
                    with contextlib.ExitStack() as ev:
                        vtp = ev.enter_context(nc.psum_tensor(_U('zvp'), [128, 2048], BF16))
                        for a in range(NT):
                            P.do('t', lambda e, aa=a, vtp=vtp, v_sb=v_sb, ident=ident: e.transpose(vtp[:, aa * 128:(aa + 1) * 128], v_sb[:, aa * 128:(aa + 1) * 128], ident), tjs + [gates.get('prev_tav'), gates.get('prev_ao'), gates.get('prev_ptev')] if a == 0 else [])
                        tvt = P.tok('t')
                        P.do('v', lambda e, vtm=vtm, vtp=vtp: e.tensor_copy(vtm[:], vtp[:]), [tvt])
                        tok_vtm = P.tok('v')
                    gates['qkv'] = tjs + [tok_vtm]
                    gates['vtm'] = [tok_vtm]

                    def mask_fn(qt, sps, _dummy=None):
                        klen = (qt - max(0, qt - 2) + 1) * 128
                        ops = []
                        if qt >= 2:
                            ops.append((0, 'tri_lo'))
                        ops.append((klen - 128, 'tri_hi'))
                        for i, (c0, nm) in enumerate(ops):
                            P.do('t', lambda e, cc=c0, n=nm, last=i == len(ops) - 1, sps=sps, CB=CB: e.matmul(sps[:, cc:cc + 128], CB[n][:], CB['negi'][:], start=False, stop=last))
                    for hh in range(2 if L1F_PART in ('attn', 'full') else 0):

                        def ao_w(qt, avp, w, pi=pi, hh=hh):
                            P.do('v', lambda e, col=qt, h0=hh * 64, ao=ao, pi=pi, avp=avp: e.tensor_copy(ao[h0:h0 + 64, pi * T + col * 128:pi * T + (col + 1) * 128], avp[h0:h0 + 64, :]), w)
                            tok_ao.append(P.tok('v'))
                            return tok_ao[-1]
                        attn_head(f'z{pr}h{hh}', hh * 64, q_sb, k_sb, vtm, p_sb, pt_sb, nmx, sums, rcs, ao_w, list(range(NT)), lambda qt: qt * 128, lambda qt: (max(0, qt - 2) * 128, (qt - max(0, qt - 2) + 1) * 128), gates, mask_fn=mask_fn)
            P.barrier()
            if L1F_PART != 'full':
                eg_outer.close()
                continue
            with contextlib.ExitStack() as ew:
                wbf2 = ew.enter_context(nc.sbuf_tensor(_U('zwb2'), [128, 4096], BF16))
                pps = [ew.enter_context(nc.psum_tensor(_U(f'zpp{i}'), [128, 512], F32)) for i in range(2)]
                tc = dma_ch(wbf2[:], WO6BF[li][pg * 512:(pg + 1) * 512, :], tok_ao[-1:] + ([prep.get('wo')] if prep is not None else []))
                tev = {}
                cnt = 0
                for of in range(8):
                    for sp in range(4):
                        b = cnt % 2
                        cnt += 1
                        w0 = [tc, tev.get(b)] + tok_ao[-1:]
                        for k in range(4):
                            P.do('t', lambda e, kk=k, oo=of, ss=sp, bb=b, pps=pps, wbf2=wbf2, ao=ao: e.matmul(pps[bb][:], wbf2[:, kk * 1024 + oo * 128:kk * 1024 + (oo + 1) * 128], ao[:, kk * T + ss * 512:kk * T + (ss + 1) * 512], start=kk == 0, stop=kk == 3), w0 if k == 0 else [])
                        tm = P.tok('t')
                        P.do('v', lambda e, bb=b, oo=of, ss=sp, z_sb=z_sb, pps=pps: e.tensor_tensor(z_sb[:, oo * WZ + 1 + ss * 512:oo * WZ + 1 + (ss + 1) * 512], pps[bb][:], z_sb[:, oo * WZ + 1 + ss * 512:oo * WZ + 1 + (ss + 1) * 512], ALU.add), [tm, t_zadd.get((b, of, sp))])
                        tev[b] = P.tok('v')
                        t_zadd[b, of, sp] = tev[b]
            eg_outer.close()
    return [t for t in t_zadd.values()]

def emit_l1f_ffn(nc, P, z_sb, xn, CB, ST, sqs, tok_con, LNG6, LNB6, W1A6, W1G6, B1A6, B1G6, W26, B26, lnbb, beffa, beffg, li, gate0, emit_ln, zc, dma_ch, WCACHE=None, prep_parts=None):
    EPS1 = 1e-05
    W1ABF, W1GBF, W2BF, prep = WCACHE
    t_zw = {}
    with contextlib.ExitStack() as EF:
        xn1 = EF.enter_context(nc.sbuf_tensor(_U('xn1'), [128, NC * 512], BF16))
        h_bf = EF.enter_context(nc.sbuf_tensor(_U('zh'), [128, NK2 * 512], BF16))
        wba = EF.enter_context(nc.sbuf_tensor(_U('zfwa'), [128, 4096], BF16))
        wbg = EF.enter_context(nc.sbuf_tensor(_U('zfwg'), [128, 4096], BF16))
        gel = [EF.enter_context(nc.sbuf_tensor(_U(f'zgel{i}'), [128, 512], BF16)) for i in range(2)]
        lg = EF.enter_context(nc.sbuf_tensor(_U('zlg'), [128, 8], F32))
        lb = EF.enter_context(nc.sbuf_tensor(_U('zlb'), [128, 8], F32))
        lgr = EF.enter_context(nc.sbuf_tensor(_U('zlgr'), [128, 8], F32))
        ba = EF.enter_context(nc.sbuf_tensor(_U('zba'), [128, NK2], F32))
        bg = EF.enter_context(nc.sbuf_tensor(_U('zbg'), [128, NK2], F32))
        b2 = EF.enter_context(nc.sbuf_tensor(_U('zb2'), [128, 8], F32))
        tks = []
        for dst, srcw in ((lg, LNG6[li]), (lb, LNB6[li]), (ba, B1A6[li]), (bg, B1G6[li]), (b2, B26[li])):
            P.do('y', lambda e, d=dst, s=srcw: e.dma_start(out=d[:], in_=s), list(gate0), dma=True)
            tks.append(P.tok('y'))
        tok_cst = tks[-1]
        P.do('v', lambda e, lgr=lgr, lg=lg: e.reciprocal(lgr[:], lg[:]), [tok_cst])
        trl = P.tok('v')
        for k in range(NC):
            P.do('s', lambda e, kk=k, lnbb=lnbb, lgr=lgr, lb=lb: e.activation(lnbb[:, kk:kk + 1], lgr[:, kk:kk + 1], AF.Identity, scale=lb[:, kk:kk + 1]), [trl] if k == 0 else [])
        tok_lnbb = P.tok('s')
        for q in range(4):
            P.barrier(('v', 's', 't', 'y'))
            if prep_parts:
                prep_parts.pop(0)()
            tq0 = q * 512

            def src_col(a, c, t0=tq0):
                return zc(z_sb, c, t0 + a * 128, t0 + (a + 1) * 128, shift=c >= 4)

            def dst_fn(a):
                return xn1[:].rearrange('p (c w) -> p c w', c=NC)[:, :, a * 128:(a + 1) * 128]
            lt = emit_ln(src_col, 4, dst_fn, f'zl{li}_{q}', EPS1, waits=gate0 + [t_zw.get('last')])
            t_hev = None
            with contextlib.ExitStack() as e1:
                psa = [e1.enter_context(nc.psum_tensor(_U(f'zpsa{i}'), [128, 512], F32)) for i in range(2)]
                psg = [e1.enter_context(nc.psum_tensor(_U(f'zpsg{i}'), [128, 512], F32)) for i in range(2)]
                psb = e1.enter_context(nc.psum_tensor(_U('zpsb'), [128, 16], F32))
                t_stt = {}
                t_gel = {}
                ocg = 0
                for os_ in range(DIP // 512):
                    pass
                nos = DIP // 512 + (1 if DIP % 512 else 0)
                for os_ in range(nos):
                    o0 = os_ * 512
                    ow = min(512, DIP - o0)
                    nch = ow // 128
                    pw1 = [prep.get('w1')] if prep is not None and q == 0 and os_ == 0 else []
                    tca = dma_ch(wba[:, 0:8 * ow], W1ABF[li][:, o0:o0 + ow],
                                 [t_zw.get('wafree')] + pw1)
                    tcg = dma_ch(wbg[:, 0:8 * ow], W1GBF[li][:, o0:o0 + ow],
                                 [t_zw.get('wgfree')])
                    if q == 0:
                        prev_b1 = None
                        for oc in range(nch):
                            gco = o0 // 128 + oc
                            for k in range(NC):
                                P.do('t', lambda e, kk=k, occ=oc, w=ow, psb=psb, wba=wba, lnbb=lnbb: e.matmul(psb[:, 0:1], wba[:, kk * w + occ * 128:kk * w + occ * 128 + 128], lnbb[:, kk:kk + 1], start=kk == 0, stop=kk == NC - 1), [tca, tcg, tok_lnbb, prev_b1] if k == 0 else [])
                            tb1 = P.tok('t')
                            P.do('v', lambda e, gc=gco, beffa=beffa, ba=ba, psb=psb: e.tensor_tensor(beffa[:, gc:gc + 1], ba[:, gc:gc + 1], psb[:, 0:1], ALU.add), [tb1, tok_cst])
                            tba = P.tok('v')
                            for k in range(NC):
                                P.do('t', lambda e, kk=k, occ=oc, w=ow, psb=psb, wbg=wbg, lnbb=lnbb: e.matmul(psb[:, 1:2], wbg[:, kk * w + occ * 128:kk * w + occ * 128 + 128], lnbb[:, kk:kk + 1], start=kk == 0, stop=kk == NC - 1), [tba] if k == 0 else [])
                            tb2 = P.tok('t')
                            P.do('v', lambda e, gc=gco, beffg=beffg, bg=bg, psb=psb: e.tensor_tensor(beffg[:, gc:gc + 1], bg[:, gc:gc + 1], psb[:, 1:2], ALU.add), [tb2])
                            prev_b1 = P.tok('v')
                        tbef = P.tok('v')
                    else:
                        tbef = None
                    tsc = tcg
                    for oc in range(nch):
                        gco = o0 // 128 + oc
                        b = ocg % 2
                        ocg += 1
                        w0 = [tsc, tca, t_stt.get(b)] + lt
                        for k in range(NC):
                            P.do('t', lambda e, kk=k, occ=oc, w=ow, bb=b, psa=psa, wba=wba, xn1=xn1: e.matmul(psa[bb][:], wba[:, kk * w + occ * 128:kk * w + occ * 128 + 128], xn1[:, kk * 512:(kk + 1) * 512], start=kk == 0, stop=kk == NC - 1), w0 if k == 0 else [])
                        for k in range(NC):
                            P.do('t', lambda e, kk=k, occ=oc, w=ow, bb=b, psg=psg, wbg=wbg, xn1=xn1: e.matmul(psg[bb][:], wbg[:, kk * w + occ * 128:kk * w + occ * 128 + 128], xn1[:, kk * 512:(kk + 1) * 512], start=kk == 0, stop=kk == NC - 1), [t_gel.get(b)] if k == 0 else [])
                        thg = P.tok('t')
                        P.do('s', lambda e, gc=gco, bb=b, gel=gel, psg=psg, beffg=beffg: e.activation(gel[bb][:], psg[bb][:], AF.Gelu, bias=beffg[:, gc:gc + 1]), [thg])
                        tge = P.tok('s')
                        t_gel[b] = tge
                        P.do('v', lambda e, gc=gco, bb=b, h_bf=h_bf, psa=psa, beffa=beffa, gel=gel: e.scalar_tensor_tensor(h_bf[:, gc * 512:(gc + 1) * 512], psa[bb][:], beffa[:, gc:gc + 1], gel[bb][:], ALU.add, ALU.mult), [tge])
                        t_hev = P.tok('v')
                        t_stt[b] = t_hev
                    t_zw['wafree'] = thg
                    t_zw['wgfree'] = thg
            P.barrier(('v', 's', 't', 'y'))
            with contextlib.ExitStack() as e2:
                zps = [e2.enter_context(nc.psum_tensor(_U(f'zzp{i}'), [128, 512], F32)) for i in range(8)]
                t_mm = {}
                for fs in range(6):
                    r0 = fs * 512
                    rw = min(512, DIP - r0)
                    nch = rw // 128
                    pw2 = [prep.get('w2')] if prep is not None and q == 0 and fs == 0 else []
                    tc2 = dma_ch(wba[:, 0:nch * 1024], W2BF[li][r0:r0 + rw, :],
                                 [t_zw.get('w2free')] + ([t_hev] if fs == 0 else []) + pw2)
                    last = None
                    for of in range(8):
                        for m in range(nch):
                            kc = fs * 4 + m
                            P.do('t', lambda e, oo=of, mm2=m, k=kc, zps=zps, wba=wba, h_bf=h_bf: e.matmul(zps[oo][:], wba[:, mm2 * 1024 + oo * 128:mm2 * 1024 + (oo + 1) * 128], h_bf[:, k * 512:(k + 1) * 512], start=k == 0, stop=k == NK2 - 1), [tc2, t_hev] if of == 0 and m == 0 else [])
                        last = P.tok('t')
                        t_mm[of] = last
                    t_zw['w2free'] = last
                tlast = []
                for of in range(8):
                    P.do('v', lambda e, oo=of, t0=tq0, zc=zc, z_sb=z_sb, zps=zps, b2=b2: e.scalar_tensor_tensor(zc(z_sb, oo, t0, t0 + 512), zps[oo][:], b2[:, oo:oo + 1], zc(z_sb, oo, t0, t0 + 512), ALU.add, ALU.add), [t_mm[of], tok_cst])
                    tlast.append(P.tok('v'))
                t_zw['last'] = tlast[-1]
        return tlast

def emit_l1_attn(nc, P, xn, gb, CB, ST, sqs, tok_con, WQKV1, WO1, BQKV1, BO1, NQW, NKW, XSP, OUT, fin, tap, tap_toks, attn_head, zc, dma_ch, WBuf=None):
    ident = CB['ident'][:]
    P.barrier()
    with contextlib.ExitStack() as EL:
        sqs = EL.enter_context(nc.sbuf_tensor(_U('sqs'), [128, 512], BF16))
        k_all = EL.enter_context(nc.sbuf_tensor(_U('lk'), [128, NC * T], BF16))
        nmx = EL.enter_context(nc.sbuf_tensor(_U('lnm'), [128, 16], F32))
        sums = EL.enter_context(nc.sbuf_tensor(_U('lsm'), [128, 16], F32))
        rcs = EL.enter_context(nc.sbuf_tensor(_U('lrc'), [128, 16], F32))
        bqkv = [EL.enter_context(nc.sbuf_tensor(_U(f'lbq{j}'), [128, 8], F32)) for j in range(3)]
        bo = EL.enter_context(nc.sbuf_tensor(_U('lbo'), [128, 8], F32))
        nw8 = EL.enter_context(nc.sbuf_tensor(_U('lnw'), [128, 8], F32))
        nkw_ = EL.enter_context(nc.sbuf_tensor(_U('lkw'), [128, 8], F32))
        rr_f = EL.enter_context(nc.sbuf_tensor(_U('lrr'), [1, 512], F32))
        rr_b = EL.enter_context(nc.sbuf_tensor(_U('lrb'), [1, 512], BF16))
        for j in range(3):
            P.do('y', lambda e, d=bqkv[j], BQKV1=BQKV1, j=j: e.dma_start(out=d[:], in_=BQKV1[j]), list(fin), dma=True)
        P.do('y', lambda e, bo=bo, BO1=BO1: e.dma_start(out=bo[:], in_=BO1), dma=True)
        P.do('y', lambda e, nw8=nw8, NQW=NQW: e.dma_start(out=nw8[:], in_=NQW), dma=True)
        P.do('y', lambda e, nkw_=nkw_, NKW=NKW: e.dma_start(out=nkw_[:], in_=NKW), dma=True)
        tok_b = P.tok('y')
        P.do('v', lambda e, nw8=nw8: e.tensor_scalar(nw8[:], nw8[:], 0.125, None, ALU.mult), [tok_b])
        tok_nw = P.tok('v')

        def qk_build(WCOL, bias_j, wcol_scale, dst_sb, dst_w, spans, tag, gate):
            toks = []
            with contextlib.ExitStack() as eb:
                wbq = WBuf(eb, tag + 'w', 1024)
                ssp = eb.enter_context(nc.psum_tensor(_U(tag + 'ss'), [1, 512], F32))
                f2p = eb.enter_context(nc.psum_tensor(_U(tag + 'f2'), [128, 512], F32))
                qps = [eb.enter_context(nc.psum_tensor(_U(tag + f'qp{i}'), [128, 512], F32)) for i in range(2)]
                t_wfree = gate[-1] if gate else None
                tevs = {}
                last_wmm = None
                prevs_qk = {}
                dst_off = min(spans) * 512
                for sp in spans:
                    t0 = sp * 512
                    lt0 = t0 - dst_off
                    tsqs = []
                    for oc in range(NC):
                        wbf, tc = wbq.load(WCOL(oc), [t_wfree] if t_wfree else ())
                        t_wfree = None
                        b = oc % 2
                        w0 = [tc, tevs.get(b)] + gate
                        for k in range(NC):
                            P.do('t', lambda e, kk=k, tt0=t0, bb=b, qps=qps, wbf=wbf, zc=zc, xn=xn: e.matmul(qps[bb][:], wbf[:, kk * 128:(kk + 1) * 128], zc(xn, kk, tt0, tt0 + 512), start=kk == 0, stop=kk == NC - 1), w0 if k == 0 else [])
                        tm = P.tok('t')
                        tevs['w', oc] = tm
                        last_wmm = tm
                        wbq.consumed(tm)
                        P.do('s', lambda e, bb=b, occ=oc, sqs=sqs, qps=qps, bqkv=bqkv, bias_j=bias_j: e.activation(sqs[:], qps[bb][:], AF.Square, bias=bqkv[bias_j][:, occ:occ + 1]), [tm, tok_b])
                        tsq = P.tok('s')
                        P.do('t', lambda e, occ=oc, ssp=ssp, CB=CB, sqs=sqs: e.matmul(ssp[:], CB['onescol'][:], sqs[:], start=occ == 0, stop=occ == NC - 1), [tsq, tok_con])
                        tsqs.append(P.tok('t'))
                        P.do('s', lambda e, bb=b, occ=oc, d=dst_sb, tt0=lt0, dst_w=dst_w, qps=qps, bqkv=bqkv, bias_j=bias_j: e.activation(d[:, occ * dst_w + tt0:occ * dst_w + tt0 + 512], qps[bb][:], AF.Identity, bias=bqkv[bias_j][:, occ:occ + 1]), [tm])
                        tevs[b] = P.tok('s')
                    P.do('s', lambda e, rr_f=rr_f, ssp=ssp, CB=CB: e.activation(rr_f[:, 0:512], ssp[:], AF.Sqrt, scale=CB['cpool'][0:1, 4:5], bias=CB['cpool'][0:1, 1:2]), [tsqs[-1], prevs_qk.get('tr4')])
                    tr1 = P.tok('s')
                    P.do('v', lambda e, rr_f=rr_f: e.reciprocal(rr_f[:, 0:512], rr_f[:, 0:512]), [tr1])
                    tr2 = P.tok('v')
                    P.do('s', lambda e, rr_b=rr_b, rr_f=rr_f: e.copy(rr_b[:, 0:512], rr_f[:, 0:512]), [tr2])
                    tr3 = P.tok('s')
                    P.do('t', lambda e, f2p=f2p, CB=CB, rr_b=rr_b, t0=t0: e.matmul(f2p[:], CB['onesrow'][:], rr_b[:, 0:512], start=True, stop=True), [tr3, tok_con, prevs_qk.get('apply')])
                    tr4 = P.tok('t')
                    prevs_qk['tr4'] = tr4
                    wcol = nw8 if wcol_scale else nkw_
                    for oc in range(NC):
                        P.do('v', lambda e, occ=oc, d=dst_sb, tt0=lt0, wc=wcol, dst_w=dst_w, f2p=f2p: e.scalar_tensor_tensor(d[:, occ * dst_w + tt0:occ * dst_w + tt0 + 512], d[:, occ * dst_w + tt0:occ * dst_w + tt0 + 512], wc[:, occ:occ + 1], f2p[:], ALU.mult, ALU.mult), [tr4, tok_nw] if oc == 0 else [tr4])
                        toks.append(P.tok('v'))
                        prevs_qk['apply'] = toks[-1]
            return toks
        tk = qk_build(lambda oc: WQKV1[:, D + oc * 128:D + (oc + 1) * 128], 1, False, k_all[:], T, range(4), 'lkb', fin)
        P.barrier()
        tap('l1k', k_all[:], tk)
        for th in range(2):
            P.barrier()
            h0 = th * 1024
            with contextlib.ExitStack() as EH:
                q_h = EH.enter_context(nc.sbuf_tensor(_U('lq'), [128, NC * 1024], BF16))
                ao = EH.enter_context(nc.sbuf_tensor(_U('lao'), [128, NC * 1024], BF16))
                tq = qk_build(lambda oc: WQKV1[:, oc * 128:(oc + 1) * 128], 0, True, q_h[:], 1024, [2 * th, 2 * th + 1], 'lqb', fin)
                P.barrier()
                tq = [tq[-1]]
                tok_ao = []
                with contextlib.ExitStack() as EP:
                    wstg = EP.enter_context(nc.sbuf_tensor(_U('lvw'), [128, 1024], F32))
                    wbf = EP.enter_context(nc.sbuf_tensor(_U('lvwb'), [128, 1024], BF16))
                    v_sb = EP.enter_context(nc.sbuf_tensor(_U('lv'), [128, T], BF16))
                    vtm = EP.enter_context(nc.sbuf_tensor(_U('lvt'), [128, T], BF16))
                    p_sb = EP.enter_context(nc.sbuf_tensor(_U('lp'), [128, T], BF16))
                    pt_sb = EP.enter_context(nc.sbuf_tensor(_U('lpt'), [128, T], BF16))
                    t_wfree = fin[-1] if fin else None
                    gates = {}
                    for pr in range(8):
                        tw = dma_ch(wstg[:], WQKV1[:, 2 * D + pr * 128:2 * D + (pr + 1) * 128], [t_wfree])
                        P.do('g', lambda e, wbf=wbf, wstg=wstg: e.tensor_copy(wbf[:], wstg[:]), [tw, gates.get('vfree')])
                        tc = P.tok('g')
                        t_wfree = tc
                        with contextlib.ExitStack() as eq:
                            qps = [eq.enter_context(nc.psum_tensor(_U(f'lvp{i}'), [128, 512], F32)) for i in range(2)]
                            tev = {}
                            for sp in range(4):
                                b = sp % 2
                                w0 = [tc, tev.get(b)] + fin + ([gates.get('prev_exp'), gates.get('prev_tav'), gates.get('prev_ao'), gates.get('prev_ptev')] if sp == 0 else [])
                                for k in range(NC):
                                    P.do('t', lambda e, kk=k, ss=sp, bb=b, qps=qps, wbf=wbf, zc=zc, xn=xn: e.matmul(qps[bb][:], wbf[:, kk * 128:(kk + 1) * 128], zc(xn, kk, ss * 512, (ss + 1) * 512), start=kk == 0, stop=kk == NC - 1), w0 if k == 0 else [])
                                tm = P.tok('t')
                                P.do('s', lambda e, ss=sp, bb=b, v_sb=v_sb, qps=qps, bqkv=bqkv, pr=pr: e.activation(v_sb[:, ss * 512:(ss + 1) * 512], qps[bb][:], AF.Identity, bias=bqkv[2][:, pr:pr + 1]), [tm, tok_b])
                                tev[b] = P.tok('s')
                            tvv = [tev.get(0), tev.get(1)]
                        with contextlib.ExitStack() as ev:
                            vtp = ev.enter_context(nc.psum_tensor(_U('lvtp'), [128, 2048], BF16))
                            for a in range(NT):
                                P.do('t', lambda e, aa=a, vtp=vtp, v_sb=v_sb, ident=ident: e.transpose(vtp[:, aa * 128:(aa + 1) * 128], v_sb[:, aa * 128:(aa + 1) * 128], ident), tvv + [gates.get('prev_tav'), gates.get('prev_ao'), gates.get('prev_ptev')] if a == 0 else [])
                            tvt = P.tok('t')
                            P.do('v', lambda e, vtm=vtm, vtp=vtp: e.tensor_copy(vtm[:], vtp[:]), [tvt])
                            tok_vtm = P.tok('v')
                        gates['vfree'] = tvt
                        gates['qkv'] = tq + tk + [tok_vtm]
                        gates['vtm'] = [tok_vtm]
                        qv = q_h[:, pr * 1024:(pr + 1) * 1024]
                        kv = k_all[:, pr * T:(pr + 1) * T]
                        for hh in range(2):

                            def ao_w(qt, avp, w, pr=pr, hh=hh):
                                P.do('v', lambda e, col=qt, hq=hh * 64, ao=ao, pr=pr, avp=avp: e.tensor_copy(ao[hq:hq + 64, pr * 1024 + col * 128:pr * 1024 + (col + 1) * 128], avp[hq:hq + 64, :]), w)
                                tok_ao.append(P.tok('v'))
                                return tok_ao[-1]
                            attn_head(f'l{th}_{pr}h{hh}', hh * 64, qv, kv, vtm, p_sb, pt_sb, nmx, sums, rcs, ao_w, list(range(8)), lambda qt: qt * 128, lambda qt: (0, T), gates)
                P.barrier()
                with contextlib.ExitStack() as EW:
                    x2 = EW.enter_context(nc.sbuf_tensor(_U('x2'), [128, 8 * 1024], F32))
                    wstg2 = EW.enter_context(nc.sbuf_tensor(_U('lw2'), [128, 4096], F32))
                    wbf2 = EW.enter_context(nc.sbuf_tensor(_U('lwb2'), [128, 4096], BF16))
                    tmp = [EW.enter_context(nc.sbuf_tensor(_U(f'ltm{i}'), [128, 512], BF16)) for i in range(2)]
                    pps = [EW.enter_context(nc.psum_tensor(_U(f'lpp{i}'), [128, 512], F32)) for i in range(2)]
                    ptp = [EW.enter_context(nc.psum_tensor(_U(f'lptr{i}'), [128, 512], BF16)) for i in range(2)]
                    P.do('y', lambda e, hh0=h0, x2=x2, XSP=XSP: e.dma_start(out=x2[:], in_=XSP[:, hh0 * 8:hh0 * 8 + 8192]), fin, dma=True)
                    tx2 = P.tok('y')
                    tev = {}
                    ttr = {}
                    tadd = {}
                    t_w2f = None
                    for wh in range(2):
                        tw = dma_ch(wstg2[:], WO1[wh * 512:(wh + 1) * 512, :], [t_w2f] + tok_ao[-1:])
                        P.do('g', lambda e, wbf2=wbf2, wstg2=wstg2: e.tensor_copy(wbf2[:], wstg2[:]), [tw, tadd.get('wfree')])
                        tc = P.tok('g')
                        t_w2f = tc
                        cnt = 0
                        for of in range(8):
                            for sp in range(2):
                                b = cnt % 2
                                cnt += 1
                                w0 = [tc, tev.get(b)] + tok_ao[-1:]
                                for k in range(4):
                                    kk2 = wh * 4 + k
                                    P.do('t', lambda e, kk=k, oo=of, ss=sp, bb=b, kg=kk2, pps=pps, wbf2=wbf2, ao=ao: e.matmul(pps[bb][:], wbf2[:, kk * 1024 + oo * 128:kk * 1024 + (oo + 1) * 128], ao[:, kg * 1024 + ss * 512:kg * 1024 + (ss + 1) * 512], start=kk == 0, stop=kk == 3), w0 if k == 0 else [])
                                tm = P.tok('t')
                                P.do('v', lambda e, bb=b, oo=of, ss=sp, hh0=h0, first=wh == 0, tmp=tmp, pps=pps, bo=bo, gb=gb: e.scalar_tensor_tensor(tmp[bb][:], pps[bb][:], bo[:, oo:oo + 1] if not first else 0.0, gb[:, oo * T + hh0 + ss * 512:oo * T + hh0 + (ss + 1) * 512], ALU.add, ALU.mult), [tm, tok_b, ttr.get(b)])
                                tv = P.tok('v')
                                tev[b] = tv
                                for j in range(4):
                                    P.do('t', lambda e, bb=b, jj=j, ptp=ptp, tmp=tmp, ident=ident: e.transpose(ptp[bb][:, jj * 128:(jj + 1) * 128], tmp[bb][:, jj * 128:(jj + 1) * 128], ident), [tv, tadd.get(b)] if j == 0 else [])
                                tt2 = P.tok('t')
                                ttr[b] = tt2
                                dstx = x2[:].rearrange('p (a d) -> p a d', a=8)[:, sp * 4:sp * 4 + 4, of * 128:(of + 1) * 128]
                                P.do('v', lambda e, d=dstx, bb=b, ptp=ptp: e.tensor_tensor(d, d, ptp[bb][:].rearrange('p (j w) -> p j w', j=4), ALU.add), [tt2, tx2])
                                tadd[b] = P.tok('v')
                        tadd['wfree'] = tm
                    P.do('y', lambda e, hh0=h0, OUT=OUT, x2=x2: e.dma_start(out=OUT[hh0:hh0 + 1024, :].rearrange('(a p) d -> p a d', p=128), in_=x2[:].rearrange('p (a d) -> p a d', a=8)), [tadd.get(0), tadd.get(1)], dma=True)
                    tout = P.tok('y')
                    if th == 1:
                        P.do('g', lambda e: e.nop(), [tout] + tap_toks)
                        P.do('y', lambda e: e.nop(), [tout] + tap_toks)
                    else:
                        out_tok_half = tout

def _chunkmaj(v):
    return np.ascontiguousarray(v.reshape(8, 128).T)

def _prep_inputs(I):
    A = np.ascontiguousarray
    W1 = I['L1_ffn_W1']
    W1a = np.zeros((6, D, DIP), np.float32)
    W1g = np.zeros((6, D, DIP), np.float32)
    W1a[:, :, :DI] = W1[:, :, :DI]
    W1g[:, :, :DI] = W1[:, :, DI:]
    b1 = I['L1_ffn_b1']
    b1a = np.zeros((6, 128, NK2), np.float32)
    b1g = np.zeros((6, 128, NK2), np.float32)
    for i in range(6):
        b1a[i].T.flat[:DI] = b1[i, :DI]
        b1g[i].T.flat[:DI] = b1[i, DI:]
    b1a = np.stack([np.pad(b1[i, :DI], (0, DIP - DI)).reshape(NK2, 128).T for i in range(6)])
    b1g = np.stack([np.pad(b1[i, DI:], (0, DIP - DI)).reshape(NK2, 128).T for i in range(6)])
    W2p = np.zeros((6, DIP, D), np.float32)
    W2p[:, :DI, :] = I['L1_ffn_W2']
    base = {'w_attnT': A(I['L0_attn_w_attn'].T), 'w_projT': A(I['L0_attn_w_proj'].T), 'w_fcT': A(I['L0_ffn_W_fc'].T), 'w_prT': A(I['L0_ffn_W_proj'].T), 'wqkv1T': A(np.concatenate([I['L1_attn_Wq'].T, I['L1_attn_Wk'].T, I['L1_attn_Wv'].T], 1)), 'wo1T': A(I['L1_attn_Wo'].T), 'bqkv1': np.stack([_chunkmaj(I['L1_attn_bq']), _chunkmaj(I['L1_attn_bk']), _chunkmaj(I['L1_attn_bv'])]), 'bo1': _chunkmaj(I['L1_attn_bo']), 'nqw': _chunkmaj(I['L1_attn_norm_q_w']), 'nkw': _chunkmaj(I['L1_attn_norm_k_w']), 'wqkv6': A(I['L1_ffn_Wqkv']), 'wo6': A(I['L1_ffn_Wo']), 'qs6': A(np.tile(I['L1_ffn_q_scale'], (1, 2))[:, :, None]), 'ks6': A(np.tile(I['L1_ffn_k_scale'], (1, 2))[:, :, None]), 'lng6': np.stack([_chunkmaj(I['L1_ffn_ln_g'][i]) for i in range(6)]), 'lnb6': np.stack([_chunkmaj(I['L1_ffn_ln_b'][i]) for i in range(6)]), 'w1a6': W1a, 'w1g6': W1g, 'b1a6': b1a, 'b1g6': b1g, 'w26': W2p, 'b26': np.stack([_chunkmaj(I['L1_ffn_b2'][i]) for i in range(6)])}
    for k, v in host_constants().items():
        if k not in ('identf', 'onescolf'):
            base['c_' + k] = v
    return base

def kernel(**inputs):
    base = _prep_inputs(inputs)
    x = np.ascontiguousarray(inputs['L0_attn_x']).astype(np.float32)
    nc = build('all')
    in_maps = [dict(base, x=np.ascontiguousarray(x[i])) for i in range(8)]
    res = run_bass_kernel_spmd(nc, in_maps, core_ids=list(range(8)))
    return np.stack([np.asarray(r['out']) for r in res.results])